# version 01
# baseline (speedup 1.0000x reference)
import contextlib
import numpy as np
import concourse.bass as bass
import concourse.mybir as mybir
from concourse.bass import IndirectOffsetOnAxis
from concourse.bass_utils import run_bass_kernel_spmd

F32 = mybir.dt.float32
BF16 = mybir.dt.bfloat16
I32 = mybir.dt.int32
AF = mybir.ActivationFunctionType
ALU = mybir.AluOpType
AX = mybir.AxisListType

P = 128
D = 1024
DG = 8
NTOK = 4096
TGLOB = 32768
SEQ = 2048
LC = 512
NCH = NTOK // LC
NSTATE = 16
DTRANK = 64
DFFN = 4096
NEXP = 8
CAP = 10240
EPS = 1e-5
RF = TGLOB // P

ENGS = ("sync", "scalar", "vector", "tensor", "gpsimd")


class Sem:
    def __init__(self, nc, es, name):
        self.h = es.enter_context(nc.semaphore(name))
        self.n = 0
        self.name = name


class B:

    def __init__(self, nc, es, block):
        self.nc = nc
        self.es = es
        self.block = block
        self.script = []
        self.sems = {e: Sem(nc, es, "s_" + e) for e in ENGS}
        self.sems["qscalar"] = Sem(nc, es, "q_scalar")
        self.sems["qgpsimd"] = Sem(nc, es, "q_gpsimd")
        self.phase_tiles = []

    def alloc(self, name, shape, dt=F32):
        cm = self.nc.sbuf_tensor(name, list(shape), dt)
        t = cm.__enter__()
        self.phase_tiles.append(cm)
        return t

    def flush(self, free=True):
        for engname in ENGS:
            steps = [fn for e, fn in self.script if e == engname]
            if not steps:
                continue
            deco = getattr(self.block, engname)

            def body(eng, _steps=steps):
                for fn in _steps:
                    fn(eng)
            deco(body)
        self.script = []
        if free:
            for cm in reversed(self.phase_tiles):
                cm.__exit__(None, None, None)
            self.phase_tiles = []

    def raw(self, eng, fn):
        s = self.sems[eng]
        k = 16 if eng == "sync" else 1
        s.n += k
        self.script.append((eng, lambda e, fn=fn, s=s, k=k: fn(e).then_inc(s.h, k)))
        return s.n

    def wait(self, eng, dep_eng, val=None):
        s = self.sems[dep_eng]
        v = s.n if val is None else val
        if v > 0:
            self.script.append((eng, lambda e, s=s, v=v: e.wait_ge(s.h, v)))
        if val is None and dep_eng in ("scalar", "gpsimd"):
            q = self.sems["q" + dep_eng]
            if q.n > 0:
                self.script.append((eng, lambda e, q=q, v=q.n: e.wait_ge(q.h, v)))

    def wait_val(self, eng, sem, v):
        if v > 0:
            self.script.append((eng, lambda e, sem=sem, v=v: e.wait_ge(sem.h, v)))

    def dma(self, eng, out, in_, **kw):
        s = self.sems[{"scalar": "qscalar", "gpsimd": "qgpsimd"}.get(eng, eng)]
        if s.n > 0:
            self.wait_val(eng, s, s.n)
        s.n += 16
        self.script.append(
            (eng, lambda e, out=out, in_=in_, kw=kw, s=s:
             e.dma_start(out=out, in_=in_, **kw).then_inc(s.h, 16)))
        return s.n

    def idma(self, out, out_off, in_, in_off, bc=None):
        s = self.sems["qgpsimd"]
        if s.n > 0:
            self.wait_val("gpsimd", s, s.n)
        s.n += 16

        def f(e, out=out, oo=out_off, in_=in_, io=in_off, bc=bc, s=s):
            kw = {}
            if bc is not None:
                kw = dict(bounds_check=bc, oob_is_err=False)
            return e.indirect_dma_start(out=out, out_offset=oo, in_=in_,
                                        in_offset=io, **kw).then_inc(s.h, 16)
        self.script.append(("gpsimd", f))
        return s.n

    def act(self, out, in_, func, bias=0.0, scale=1.0):
        return self.raw("scalar", lambda e, o=out, i=in_, f=func, bb=bias, ss=scale:
                        e.activation(out=o, in_=i, func=f, bias=bb, scale=ss))

    def tt(self, out, a, b_, op, eng="vector"):
        return self.raw(eng, lambda e, o=out, a=a, b=b_, op=op:
                        e.tensor_tensor(out=o, in0=a, in1=b, op=op))

    def ts(self, out, a, s1, op0, s2=None, op1=ALU.bypass, eng="vector"):
        return self.raw(eng, lambda e, o=out, a=a, s1=s1, s2=s2, op0=op0, op1=op1:
                        e.tensor_scalar(out=o, in0=a, scalar1=s1, scalar2=s2,
                                        op0=op0, op1=op1))

    def stt(self, out, in0, scalar, in1, op0, op1, eng="vector"):
        return self.raw(eng, lambda e, o=out, i0=in0, s=scalar, i1=in1,
                        op0=op0, op1=op1:
                        e.scalar_tensor_tensor(out=o, in0=i0, scalar=s, in1=i1,
                                               op0=op0, op1=op1))

    def scan(self, out, d0, d1, init, op0=ALU.mult, op1=ALU.add, eng="vector"):
        return self.raw(eng, lambda e, o=out, d0=d0, d1=d1, i=init,
                        op0=op0, op1=op1:
                        e.tensor_tensor_scan(out=o, data0=d0, data1=d1,
                                             initial=i, op0=op0, op1=op1))

    def red(self, out, in_, op=ALU.add, negate=False):
        return self.raw("vector", lambda e, o=out, i=in_, op=op, n=negate:
                        e.tensor_reduce(out=o, in_=i, axis=AX.X, op=op, negate=n))

    def cpy(self, out, in_, eng="vector"):
        return self.raw(eng, lambda e, o=out, i=in_: e.tensor_copy(out=o, in_=i))

    def memset(self, ap, v, eng="gpsimd"):
        return self.raw(eng, lambda e, a=ap, v=v: e.memset(a, v))

    def recip(self, out, in_):
        return self.raw("vector", lambda e, o=out, i=in_: e.reciprocal(out=o, in_=i))

    def mm(self, out, lhsT, rhs, start, stop, inc=False):
        s = self.sems["tensor"]
        if inc:
            s.n += 1
            self.script.append(
                ("tensor", lambda e, o=out, l=lhsT, r=rhs, a=start, bb=stop, s=s:
                 e.matmul(o, l, r, start=a, stop=bb).then_inc(s.h, 1)))
        else:
            self.script.append(
                ("tensor", lambda e, o=out, l=lhsT, r=rhs, a=start, bb=stop:
                 e.matmul(o, l, r, start=a, stop=bb)))
        return s.n

    def transpose(self, out, in_, ident):
        s = self.sems["tensor"]
        s.n += 1
        self.script.append(("tensor", lambda e, o=out, i=in_, d=ident, s=s:
                            e.transpose(o, i, d).then_inc(s.h, 1)))
        return s.n

    def iota(self, out, pattern, base=0, chan=0):
        return self.raw("gpsimd", lambda e, o=out, pp=pattern, bb=base, c=chan:
                        e.iota(o, pattern=pp, base=bb, channel_multiplier=c,
                               allow_small_or_imprecise_dtypes=True))

    def barrier(self):
        allsems = list(self.sems.values())
        for e in ENGS:
            for s in allsems:
                if s.name in ("s_" + e,):
                    continue
                if s.n > 0:
                    self.wait_val(e, s, s.n)


def rr(ap, spec, **kw):
    return ap.rearrange(spec, **kw)


ALL_PHASES = ("R", "A1", "A2a", "A2b", "A3", "M", "CMB", "C")


def build_program(phases=ALL_PHASES, dbg=()):
    nc = bass.Bass("TRN2", target_bir_lowering=False)

    def din(name, shape, dt=F32):
        return nc.dram_tensor(name, list(shape), dt, kind="ExternalInput").ap()

    x_fm_full = din("x_fm_full", [D, TGLOB])
    x_fm = din("x_fm", [D, NTOK])
    W_inT = din("W_inT", [D, 2 * D])
    conv_w0 = din("conv_w0", [P, DG])
    conv_w1 = din("conv_w1", [P, DG])
    conv_b = din("conv_b", [P, DG])
    W_xpT = din("W_xpT", [D, 96])
    W_dtT = din("W_dtT", [DTRANK, D])
    dt_b = din("dt_b", [P, DG])
    A_log0 = din("A_log0", [P, DG, NSTATE])
    Dp0 = din("Dp0", [P, DG])
    W_outT = din("W_outT", [D, D])
    mnorm_g = din("mnorm_g", [P, DG]); mnorm_b = din("mnorm_b", [P, DG])
    n1_g = din("n1_g", [P, DG]); n1_b = din("n1_b", [P, DG])
    n2_g = din("n2_g", [P, DG]); n2_b = din("n2_b", [P, DG])
    n3_g = din("n3_g", [P, DG]); n3_b = din("n3_b", [P, DG])
    W_f1T = din("W_f1T", [D, DFFN])
    W_f2T = din("W_f2T", [DFFN, D])
    W_feT = din("W_feT", [D, D])
    fe_b = din("fe_b", [P, DG])
    WrT = din("WrT", [D, NEXP])
    W1 = din("W1", [NEXP, D, DFFN])
    b1 = din("b1", [P, NEXP, DFFN // P])
    W2 = din("W2", [NEXP, DFFN, D])
    b2 = din("b2", [P, NEXP, D // P])
    fw1 = din("fw1", [D, DFFN])
    fb1 = din("fb1", [P, DFFN // P])
    fw2 = din("fw2", [DFFN, D])
    fb2 = din("fb2", [P, DG])
    A_log1 = din("A_log1", [P, DG, NSTATE])
    W_x1T = din("W_x1T", [D, 96])
    W_dt1T = din("W_dt1T", [DTRANK, D])
    b_dt1 = din("b_dt1", [P, DG])
    D_par = din("D_par", [P, DG])
    WR1T = din("WR1T", [D, D])
    WK1T = din("WK1T", [D, D])
    WV1T = din("WV1T", [D, D])
    mu_r = din("mu_r", [P, DG]); mu_k = din("mu_k", [P, DG])
    idx_own = din("idx_own", [P, 1], I32)

    out_fm = nc.dram_tensor("out_fm", [D, NTOK], F32, kind="ExternalOutput").ap()

    def scr(name, shape, dt=F32):
        return nc.dram_tensor(name, list(shape), dt, kind="Internal").ap()

    lraw_g = scr("lraw_g", [2 * NEXP + 2, TGLOB])
    own_stats = scr("own_stats", [2, NTOK])
    own_rq = scr("own_rq", [NEXP + 1, NTOK])
    xn_fm_d = scr("xn_fm_d", [D, NTOK])
    xnbf_fm_d = scr("xnbf_fm_d", [D, NTOK], BF16)
    zs_d = scr("zs_d", [D, NTOK])
    xc_d = scr("xc_d", [D, NTOK])
    delta_d = scr("delta_d", [D, NTOK])
    hm_d = scr("hm_d", [D, NTOK])
    ymix_d = scr("ymix_d", [D, NTOK])
    ysc_d = scr("ysc_d", [D, NTOK])
    tw_d = scr("tw_d", [P, 4096])
    td_d = scr("td_d", [P, 2048])
    a1_d = scr("a1_d", [P, DG * NSTATE])
    xn1_d = scr("xn1_d2", [P, 4096])
    dl1_d = scr("dl1_d", [P, 4096])
    th_d = scr("th_d", [P, 2048])
    src2_d = scr("src2_d", [D, NTOK])
    src3_d = scr("src3_d", [D, NTOK])
    s_attn_d = scr("s_attn_d", [D, NTOK])
    yfb_d = scr("yfb_d", [D, NTOK])
    gmoe_d = scr("gmoe_d", [D, NTOK])
    x1_d = scr("x1_d", [D, NTOK])
    rowb_d = scr("rowb_d", [2, LC])
    bc_d = scr("bc_d", [32, LC])
    sw_d = scr("sw_d", [NEXP])

    DBG_REG = {"own_stats": own_stats, "own_rq": own_rq, "xn": xn_fm_d,
               "xc": xc_d, "delta": delta_d, "hm": hm_d, "ymix": ymix_d,
               "ysc": ysc_d, "tw": tw_d, "td": td_d, "th": th_d,
               "a1sb": a1_d, "xn1": xn1_d, "dl1": dl1_d,
               "zs": zs_d, "src2": src2_d, "src3": src3_d, "s_attn": s_attn_d,
               "yfb": yfb_d, "gmoe": gmoe_d, "x1": x1_d, "lraw": lraw_g}
    dbg_outs = {}
    for k in dbg:
        t = DBG_REG[k]
        dbg_outs[k] = nc.dram_tensor("dbg_" + k, list(t.shape),
                                     t.dtype, kind="ExternalOutput").ap()

    with contextlib.ExitStack() as es:
        block = es.enter_context(nc.Block())
        b = B(nc, es, block)

        def psbt(name, shape, dt=F32):
            return es.enter_context(nc.sbuf_tensor(name, list(shape), dt))

        pv = {}
        for nm in ["conv_w0", "conv_w1", "conv_b", "dt_b", "Dp0", "mnorm_g",
                   "mnorm_b", "n1_g", "n1_b", "n2_g", "n2_b", "n3_g", "n3_b",
                   "fe_b", "fb2", "b_dt1", "D_par", "mu_r", "mu_k",
                   "om_mu_r", "om_mu_k"]:
            pv[nm] = psbt("pv_" + nm, [P, DG])
        A0_sb = psbt("A0_sb", [P, DG * NSTATE])
        A1_sb = psbt("A1_sb", [P, DG * NSTATE])
        fb1_sb = psbt("fb1_sb", [P, DFFN // P])
        b1e_sb = psbt("b1e_sb", [P, NEXP * (DFFN // P)])
        b2e_sb = psbt("b2e_sb", [P, NEXP * (D // P)])
        ones_sb = psbt("ones_sb", [P, 2])
        tri_sb = psbt("tri_sb", [P, P])
        WrT9 = psbt("WrT9", [P, DG * 9])
        swcol = psbt("swcol", [P, NEXP])
        idx1 = psbt("idx1", [P, 1], I32)
        idxq = psbt("idxq", [P, 1], I32)
        carry = psbt("carry", [P, 4 * DG])
        hcar = psbt("hcar", [P, DG * NSTATE])
        rows = psbt("rows", [P, 64])
        pb = [es.enter_context(nc.psum_tensor(f"pb{i}", [P, 512], F32))
              for i in range(7)]
        pbx = es.enter_context(nc.psum_tensor("pbx", [P, 512], F32))

        for nm, srcw in [("conv_b", conv_b), ("dt_b", dt_b), ("Dp0", Dp0),
                         ("mnorm_g", mnorm_g), ("mnorm_b", mnorm_b),
                         ("n1_g", n1_g), ("n1_b", n1_b), ("n2_g", n2_g),
                         ("n2_b", n2_b), ("n3_g", n3_g), ("n3_b", n3_b),
                         ("fe_b", fe_b), ("fb2", fb2), ("b_dt1", b_dt1),
                         ("D_par", D_par), ("mu_r", mu_r), ("mu_k", mu_k),
                         ("conv_w0", conv_w0), ("conv_w1", conv_w1)]:
            b.dma("sync", pv[nm][:], srcw)
        b.dma("sync", rr(A0_sb[:], "p (g n) -> p g n", g=DG), A_log0)
        b.dma("sync", rr(A1_sb[:], "p (g n) -> p g n", g=DG), A_log1)
        b.dma("sync", fb1_sb[:], fb1)
        b.dma("sync", rr(b1e_sb[:], "p (e m) -> p e m", e=NEXP), b1)
        b.dma("sync", rr(b2e_sb[:], "p (e m) -> p e m", e=NEXP), b2)
        ld0 = b.dma("sync", idx1[:], idx_own)
        b.wait_val("gpsimd", b.sems["sync"], ld0)
        b.memset(ones_sb[:], 1.0)
        it_a = b.alloc("it_a", [P, P])
        it_b = b.alloc("it_b", [P, P])
        b.iota(it_a[:], [[1, P]], 0, 0)
        b.iota(it_b[:], [[0, P]], 0, 1)
        b.memset(WrT9[:], 1.0)
        gp0 = b.sems["gpsimd"].n
        b.wait_val("sync", b.sems["gpsimd"], gp0)
        b.dma("sync", rr(WrT9[:], "p (g m) -> p g m", g=DG)[:, :, 0:NEXP],
              rr(WrT, "(g p) m -> p g m", p=P))
        b.wait("vector", "gpsimd")
        b.wait("vector", "sync")
        b.tt(tri_sb[:], it_b[:], it_a[:], ALU.is_lt)
        b.ts(pv["om_mu_r"][:], pv["mu_r"][:], -1.0, ALU.mult, 1.0, ALU.add)
        b.ts(pv["om_mu_k"][:], pv["mu_k"][:], -1.0, ALU.mult, 1.0, ALU.add)
        b.wait_val("scalar", b.sems["sync"], ld0)
        b.act(A0_sb[:], A0_sb[:], AF.Exp)
        b.act(A1_sb[:], A1_sb[:], AF.Exp)
        b.wait("vector", "scalar")
        b.ts(A0_sb[:], A0_sb[:], -1.0, ALU.mult)
        b.ts(A1_sb[:], A1_sb[:], -1.0, ALU.mult)
        b.wait("tensor", "sync")
        for g in range(DG):
            b.mm(pbx[0:NEXP, 0:1],
                 rr(WrT9[:], "p (g m) -> p g m", g=DG)[:, g, 0:NEXP],
                 ones_sb[:, 0:1], start=(g == 0), stop=(g == DG - 1),
                 inc=(g == DG - 1))
        b.wait("vector", "tensor")
        swcp = b.cpy(rows[0:NEXP, 0:1], pbx[0:NEXP, 0:1])
        b.wait_val("scalar", b.sems["vector"], swcp)
        b.dma("scalar", sw_d, rows[0:NEXP, 0:1].squeeze(1))
        b.wait("gpsimd", "scalar")
        b.dma("gpsimd", swcol[:], sw_d.unsqueeze(0).broadcast_to((P, NEXP)))
        b.memset(carry[:], 0.0)
        b.barrier()
        b.flush()

        if "R" in phases:
            xg0 = b.alloc("r_xg0", [P, 4096])
            xg1 = b.alloc("r_xg1", [P, 4096])
            xq = b.alloc("r_xq", [P, 4096])
            sm = b.alloc("r_sm", [P, 512])
            NIT = TGLOB // 512
            pe_done = [0] * NIT
            ve_done = [0] * NIT
            for i in range(NIT):
                buf = rr([xg0, xg1][i % 2][:], "p (g t) -> p g t", g=DG)
                if i >= 2:
                    b.wait_val("sync", b.sems["tensor"], pe_done[i - 2])
                ldn = b.dma("sync", buf,
                            rr(x_fm_full, "(g p) t -> p g t", p=P)[:, :, i * 512:(i + 1) * 512])
                b.wait_val("scalar", b.sems["sync"], ldn)
                if i >= 1:
                    b.wait_val("scalar", b.sems["tensor"], pe_done[i - 1])
                sq = b.act(rr(xq[:], "p (g t) -> p g t", g=DG), buf, AF.Square)
                b.wait_val("tensor", b.sems["sync"], ldn)
                b.wait_val("tensor", b.sems["scalar"], sq)
                if i >= 1:
                    b.wait_val("tensor", b.sems["vector"], ve_done[i - 1])
                for g in range(DG):
                    b.mm(pb[0][0:9, :],
                         rr(WrT9[:], "p (g m) -> p g m", g=DG)[:, g, :],
                         buf[:, g, :], start=(g == 0), stop=(g == DG - 1))
                for g in range(DG):
                    b.mm(pb[1][0:1, :], ones_sb[:, 0:1],
                         rr(xq[:], "p (g t) -> p g t", g=DG)[:, g, :],
                         start=(g == 0), stop=(g == DG - 1),
                         inc=(g == DG - 1))
                pe_done[i] = b.sems["tensor"].n
                b.wait_val("vector", b.sems["tensor"], pe_done[i])
                if i >= 1:
                    b.wait_val("vector", b.sems["qscalar"], st_done[i - 1])
                b.cpy(sm[0:9, :], pb[0][0:9, :])
                ev = b.cpy(sm[32:33, :], pb[1][0:1, :])
                ve_done[i] = ev
                b.wait_val("scalar", b.sems["vector"], ev)
                b.dma("scalar", lraw_g[0:9, i * 512:(i + 1) * 512], sm[0:9, :])
                if i == 0:
                    st_done = [0] * NIT
                st_done[i] = b.dma("scalar", lraw_g[9:10, i * 512:(i + 1) * 512],
                                   sm[32:33, :])
            b.barrier()
            b.flush()

            ga = b.alloc("r_ga", [P, NEXP * RF])
            gb = b.alloc("r_gb", [P, 12 * RF])
            gc = b.alloc("r_gc", [P, 4 * RF])
            growbig = b.alloc("r_rowbig", [P, NTOK // P])

            def gt(arena, slot):
                return arena[:, slot * RF:(slot + 1) * RF]

            mu_t, r_t, tv, m1_t, p2_t, sig_t, tmp_t, omf_t, csum_t, pos_t, tmp2_t, _ = \
                [gt(gb, i) for i in range(12)]
            le = [gt(ga, e) for e in range(NEXP)]
            wk_t, elig_t, kk_t, tmp3_t = [gt(gc, i) for i in range(4)]

            def row_g(q):
                return rr(lraw_g[q], "(p f) -> p f", p=P)

            b.dma("sync", mu_t, row_g(8))
            l2 = b.dma("sync", r_t, row_g(9))
            b.wait_val("vector", b.sems["sync"], l2)
            b.ts(mu_t, mu_t, 1.0 / D, ALU.mult)
            b.ts(tv, r_t, 1.0 / D, ALU.mult)
            b.tt(tmp_t, mu_t, mu_t, ALU.mult)
            b.tt(tv, tv, tmp_t, ALU.subtract)
            vv = b.ts(tv, tv, EPS, ALU.add)
            b.wait_val("scalar", b.sems["vector"], vv)
            b.act(tv, tv, AF.Ln)
            rrr = b.act(r_t, tv, AF.Exp, scale=-0.5)
            b.wait_val("vector", b.sems["scalar"], rrr)
            b.wait_val("scalar", b.sems["scalar"], rrr)
            st1 = b.dma("scalar", row_g(8), mu_t)
            st2 = b.dma("scalar", row_g(9), r_t)
            b.barrier()
            lraw_flat = rr(lraw_g, "a (bb cc) -> (a bb) cc", cc=32)

            def own_gather(q, dst):
                tsn = b.ts(idxq[:], idx1[:], q * (TGLOB // 32), ALU.add)
                b.wait_val("gpsimd", b.sems["vector"], tsn)
                gg = b.idma(growbig[:], None, lraw_flat,
                            IndirectOffsetOnAxis(ap=idxq[:], axis=0))
                b.wait_val("gpsimd", b.sems["qgpsimd"], gg)
                b.dma("gpsimd", rr(dst, "(p f) -> p f", p=P), growbig[:])
                b.wait_val("vector", b.sems["qgpsimd"], gg)

            for q, dst in [(8, own_stats[0]), (9, own_stats[1])]:
                own_gather(q, dst)
            for e in range(NEXP):
                ln = b.dma("sync", le[e], row_g(e))
                b.wait_val("vector", b.sems["sync"], ln)
                b.ts(tmp_t, mu_t, swcol[:, e:e + 1], ALU.mult)
                b.tt(le[e], le[e], tmp_t, ALU.subtract)
                len_ = b.tt(le[e], le[e], r_t, ALU.mult)
                b.wait_val("scalar", b.sems["vector"], len_)
                b.dma("scalar", row_g(10 + e), le[e])
            b.cpy(m1_t, le[0])
            for e in range(1, NEXP):
                b.tt(m1_t, m1_t, le[e], ALU.max)
            for e in range(NEXP):
                b.tt(tmp_t, le[e], m1_t, ALU.is_equal)
                b.stt(tmp2_t, tmp_t, -1e30, le[e], ALU.mult, ALU.add)
                if e == 0:
                    b.cpy(p2_t, tmp2_t)
                else:
                    b.tt(p2_t, p2_t, tmp2_t, ALU.max)
            sb1 = b.tt(tmp_t, p2_t, m1_t, ALU.subtract)
            b.wait_val("scalar", b.sems["vector"], sb1)
            ex = b.act(tmp_t, tmp_t, AF.Exp)
            b.wait_val("vector", b.sems["scalar"], ex)
            b.ts(tmp_t, tmp_t, 1.0, ALU.add)
            b.recip(sig_t, tmp_t)
            b.memset(omf_t, 1.0, eng="vector")
            stw_prev = [0]
            for e in range(NEXP):
                s2 = b.tt(tmp2_t, le[e], m1_t, ALU.subtract)
                b.wait_val("scalar", b.sems["vector"], s2)
                w2c = b.act(tmp2_t, tmp2_t, AF.Exp)
                b.wait_val("vector", b.sems["scalar"], w2c)
                b.wait_val("vector", b.sems["qscalar"], stw_prev[0])
                b.tt(wk_t, tmp2_t, sig_t, ALU.mult)
                b.tt(tmp3_t, le[e], p2_t, ALU.is_ge)
                b.tt(wk_t, wk_t, tmp3_t, ALU.mult)
                b.tt(elig_t, tmp3_t, omf_t, ALU.mult)
                b.scan(csum_t, elig_t, elig_t, 0.0, ALU.add, ALU.bypass)
                vcp = b.cpy(rows[:, 32:33], csum_t[:, RF - 1:RF])
                b.wait_val("tensor", b.sems["vector"], vcp)
                mmn = b.mm(pbx[:, 0:1], tri_sb[:], rows[:, 32:33],
                           start=True, stop=True, inc=True)
                b.wait_val("vector", b.sems["tensor"], mmn)
                b.cpy(rows[:, 33:34], pbx[:, 0:1])
                b.ts(pos_t, csum_t, rows[:, 33:34], ALU.add)
                b.ts(kk_t, pos_t, float(CAP), ALU.is_le)
                b.tt(kk_t, kk_t, elig_t, ALU.mult)
                b.tt(tmp3_t, elig_t, kk_t, ALU.subtract)
                b.tt(omf_t, omf_t, tmp3_t, ALU.subtract)
                wkn = b.tt(wk_t, wk_t, kk_t, ALU.mult)
                b.wait_val("scalar", b.sems["vector"], wkn)
                stw = b.dma("scalar", row_g(e), wk_t)
                stw_prev[0] = stw
            fbn = b.ts(omf_t, omf_t, -1.0, ALU.mult, 1.0, ALU.add)
            b.wait_val("scalar", b.sems["vector"], fbn)
            b.dma("scalar", row_g(8), omf_t)
            b.barrier()
            for e in range(NEXP):
                own_gather(e, own_rq[e])
            own_gather(8, own_rq[NEXP])
            b.barrier()
            b.flush()

        rb_guard = [0]
        bc_guard = [0]
        bcd_guard = [0]

        _SLOTMAP = {0: (0, 0), 1: (32, 0), 2: (64, 0), 3: (0, 1), 4: (0, 2),
                    5: (64, 1), 6: (32, 2)}

        def _slot(t_sm, i, T):
            base, cb = _SLOTMAP[i]
            return t_sm[base:base + 1, cb * T:(cb + 1) * T]

        def ln_rows(src_g, sq_g, t_sm, T, pbA, pbB):
            S = [_slot(t_sm, i, T) for i in range(7)]
            sqn = b.act(sq_g, src_g, AF.Square)
            b.wait_val("tensor", b.sems["scalar"], sqn)
            b.wait("tensor", "vector")
            for g in range(DG):
                b.mm(pbA[0:1, 0:T], ones_sb[:, 0:1], src_g[:, g, :],
                     start=(g == 0), stop=(g == DG - 1))
            for g in range(DG):
                b.mm(pbB[0:1, 0:T], ones_sb[:, 0:1], sq_g[:, g, :],
                     start=(g == 0), stop=(g == DG - 1), inc=(g == DG - 1))
            mmn = b.sems["tensor"].n
            b.wait_val("vector", b.sems["tensor"], mmn)
            b.cpy(S[0], pbA[0:1, 0:T])
            b.cpy(S[1], pbB[0:1, 0:T])
            b.ts(S[2], S[0], 1.0 / D, ALU.mult)
            b.ts(S[3], S[1], 1.0 / D, ALU.mult)
            b.tt(S[4], S[2], S[2], ALU.mult)
            b.tt(S[4], S[3], S[4], ALU.subtract)
            vn = b.ts(S[4], S[4], EPS, ALU.add)
            b.wait_val("scalar", b.sems["vector"], vn)
            b.act(S[4], S[4], AF.Ln)
            en = b.act(S[5], S[4], AF.Exp, scale=-0.5)
            b.wait_val("vector", b.sems["scalar"], en)
            b.tt(S[6], S[2], S[5], ALU.mult)
            return b.ts(S[6], S[6], -1.0, ALU.mult)

        def repl_rows(t_sm, murch, T, vec_count):
            b.wait_val("scalar", b.sems["vector"], vec_count)
            b.wait_val("scalar", b.sems["scalar"], b.sems["scalar"].n)
            b.wait_val("scalar", b.sems["qgpsimd"], rb_guard[0])
            b.dma("scalar", rowb_d[0:1, 0:T], _slot(t_sm, 5, T))
            stn = b.dma("scalar", rowb_d[1:2, 0:T], _slot(t_sm, 6, T))
            b.wait_val("gpsimd", b.sems["qscalar"], stn)
            gn = b.dma("gpsimd", murch[:, 0:2, 0:T],
                       rowb_d[:, 0:T].unsqueeze(0).broadcast_to((P, 2, T)))
            rb_guard[0] = gn
            return gn

        def ln_apply(src_g, dst_g, murch, T, gn, g_keys=None, out_bf=None):
            b.wait_val("vector", b.sems["qgpsimd"], gn)
            r_b = murch[:, 0:1, 0:T].broadcast_to((P, DG, T))
            nm_b = murch[:, 1:2, 0:T].broadcast_to((P, DG, T))
            b.tt(dst_g, src_g, r_b, ALU.mult)
            vn = b.tt(dst_g, dst_g, nm_b, ALU.add)
            if g_keys is None:
                return vn
            gk, bk = g_keys
            b.wait_val("scalar", b.sems["vector"], vn)
            for g in range(DG):
                sn = b.act(out_bf[:, g, :], dst_g[:, g, :], AF.Identity,
                           bias=pv[bk][:, g:g + 1], scale=pv[gk][:, g:g + 1])
            return sn

        if "A1" in phases:
            wIN = b.alloc("wIN", [P, DG * 16 * P], BF16)
            wXP = b.alloc("wXP", [P, DG * 96], BF16)
            wDT = b.alloc("wDT", [DTRANK, DG * P], BF16)
            wOUT = b.alloc("wOUT", [P, DG * DG * P], BF16)
            t_x = b.alloc("t_x", [P, 4096])
            t_u = b.alloc("t_u", [P, 4096])
            t_v = b.alloc("t_v", [P, 4096])
            t_w = b.alloc("t_w", [P, 4096])
            t_h = b.alloc("t_h", [P, 2048])
            t_y = b.alloc("t_y", [P, 4096])
            t_d = b.alloc("t_d", [P, 2048])
            t_lb = b.alloc("t_lb", [P, 4096], BF16)
            t_cb = b.alloc("t_cb", [P, 4096], BF16)
            dtb = b.alloc("dtb", [DTRANK, 512], BF16)
            t_sm = b.alloc("t_sm", [P, 3 * 512])
            murch = b.alloc("murch", [P, 2 * 512])
            lw1 = b.dma("gpsimd", rr(wIN[:], "p (g m q) -> p g m q", g=DG, m=16),
                        rr(W_inT, "(g p) (m q) -> p g m q", p=P, q=P))
            lw2 = b.dma("gpsimd", rr(wXP[:], "p (g m) -> p g m", g=DG),
                        rr(W_xpT, "(g p) m -> p g m", p=P))
            lw3 = b.dma("gpsimd", rr(wDT[:], "k (g q) -> k g q", g=DG),
                        rr(W_dtT, "k (g q) -> k g q", q=P))
            lw4 = b.dma("gpsimd", rr(wOUT[:], "p (g m q) -> p g m q", g=DG, m=DG),
                        rr(W_outT, "(g p) (m q) -> p g m q", p=P, q=P))
            b.wait_val("tensor", b.sems["qgpsimd"], lw4)

            for k in range(NCH):
                sl = slice(k * LC, (k + 1) * LC)
                first = (k % (SEQ // LC) == 0)
                xv = rr(t_x[:], "p (g t) -> p g t", g=DG)
                uv = rr(t_u[:], "p (g t) -> p g t", g=DG)
                vv_ = rr(t_v[:], "p (g t) -> p g t", g=DG)
                wv = rr(t_w[:], "p (g t) -> p g t", g=DG)
                yv = rr(t_y[:], "p (g t) -> p g t", g=DG)
                lbv = rr(t_lb[:], "p (g t) -> p g t", g=DG)
                cbv = rr(t_cb[:], "p (g t) -> p g t", g=DG)
                mv = rr(murch[:], "p (r t) -> p r t", r=2)
                if first:
                    b.memset(hcar[:], 0.0)
                    b.memset(carry[:, 0:DG], 0.0, eng="vector")
                ldx = b.dma("sync", xv, rr(x_fm, "(g p) t -> p g t", p=P)[:, :, sl])
                ldm = b.dma("gpsimd", mv,
                            own_stats[:, sl].unsqueeze(0).broadcast_to((P, 2, LC)))
                b.wait_val("vector", b.sems["sync"], ldx)
                b.wait_val("vector", b.sems["qgpsimd"], ldm)
                mu_b = mv[:, 0:1, :].broadcast_to((P, DG, LC))
                r_b = mv[:, 1:2, :].broadcast_to((P, DG, LC))
                b.tt(uv, xv, mu_b, ALU.subtract)
                xnn = b.tt(uv, uv, r_b, ALU.mult)
                b.wait_val("scalar", b.sems["vector"], xnn)
                xnst = b.dma("scalar", rr(xn_fm_d, "(g p) t -> p g t", p=P)[:, :, sl], uv)
                b.wait_val("gpsimd", b.sems["vector"], xnn)
                xnbst = b.dma("gpsimd", rr(xnbf_fm_d, "(g p) t -> p g t", p=P)[:, :, sl], uv)
                lr = ln_rows(uv, xv, t_sm, LC, pb[2], pb[3])
                gn = repl_rows(t_sm, mv, LC, lr)
                lnb = ln_apply(uv, xv, mv, LC, gn, ("n1_g", "n1_b"), lbv)
                b.wait_val("tensor", b.sems["scalar"], lnb)
                ev_prev = [0, 0]
                for m in range(16):
                    bank = pb[m % 2]
                    if ev_prev[m % 2]:
                        b.wait_val("tensor", b.sems["scalar"], ev_prev[m % 2])
                    for g in range(DG):
                        b.mm(bank[:, :], rr(wIN[:], "p (g m q) -> p g m q",
                                            g=DG, m=16)[:, g, m, :],
                             lbv[:, g, :], start=(g == 0), stop=(g == DG - 1),
                             inc=(g == DG - 1))
                    mmn = b.sems["tensor"].n
                    b.wait_val("scalar", b.sems["tensor"], mmn)
                    if m < DG:
                        ev_prev[m % 2] = b.act(vv_[:, m, :], bank[:, :], AF.Identity)
                    else:
                        ev_prev[m % 2] = b.act(wv[:, m - DG, :], bank[:, :], AF.Silu)
                zs_done = ev_prev
                b.wait_val("scalar", b.sems["scalar"], max(zs_done))
                zst = b.dma("scalar", rr(zs_d, "(g p) t -> p g t", p=P)[:, :, sl], wv)
                b.wait_val("scalar", b.sems["qscalar"], xnst)
                b.wait_val("scalar", b.sems["qgpsimd"], xnbst)
                cwaits = max(zs_done)
                b.wait_val("vector", b.sems["scalar"], cwaits)
                for g in range(DG):
                    p1n = b.act(t_sm[:, 0:512], vv_[:, g, :], AF.Identity,
                                bias=pv["conv_b"][:, g:g + 1],
                                scale=pv["conv_w1"][:, g:g + 1])
                    b.wait_val("vector", b.sems["scalar"], p1n)
                    b.cpy(t_d[:, 1:512], vv_[:, g, 0:511])
                    b.cpy(t_d[:, 0:1], carry[:, g:g + 1])
                    b.cpy(carry[:, g:g + 1], vv_[:, g, 511:512])
                    b.stt(t_d[:, 512:1024], t_d[:, 0:512], pv["conv_w0"][:, g:g + 1],
                          t_sm[:, 0:512], ALU.mult, ALU.add)
                    vn = b.sems["vector"].n
                    b.wait_val("scalar", b.sems["vector"], vn)
                    xcn = b.act(uv[:, g, :], t_d[:, 512:1024], AF.Silu)
                    b.wait_val("vector", b.sems["scalar"], xcn)
                    b.cpy(cbv[:, g, :], uv[:, g, :])
                xc_done_v = b.sems["vector"].n
                xc_done_s = b.sems["scalar"].n
                if "xc" in dbg:
                    b.wait_val("scalar", b.sems["vector"], xc_done_v)
                    b.dma("scalar", rr(xc_d, "(g p) t -> p g t", p=P)[:, :, sl], uv)
                b.wait_val("tensor", b.sems["vector"], xc_done_v)
                for g in range(DG):
                    b.mm(pb[4][0:DTRANK, :], rr(wXP[:], "p (g m) -> p g m",
                                                g=DG)[:, g, 0:DTRANK],
                         cbv[:, g, :], start=(g == 0), stop=(g == DG - 1))
                for g in range(DG):
                    b.mm(pb[5][0:32, :], rr(wXP[:], "p (g m) -> p g m",
                                            g=DG)[:, g, DTRANK:96],
                         cbv[:, g, :], start=(g == 0), stop=(g == DG - 1),
                         inc=(g == DG - 1))
                xpn = b.sems["tensor"].n
                b.wait_val("scalar", b.sems["tensor"], xpn)
                dtev = b.act(dtb[:, :], pb[4][0:DTRANK, :], AF.Identity)
                bcev = b.act(t_sm[0:32, 0:512], pb[5][0:32, :], AF.Identity)
                b.wait_val("scalar", b.sems["qgpsimd"], bcd_guard[0])
                b.wait_val("scalar", b.sems["scalar"], bcev)
                bcst = b.dma("scalar", bc_d[:, 0:LC], t_sm[0:32, 0:512])
                b.wait_val("scalar", b.sems["qscalar"], bcst)
                b.wait_val("tensor", b.sems["scalar"], dtev)
                for g in range(DG):
                    bank = pb[6] if g % 2 == 0 else pb[4]
                    if g >= 2:
                        b.wait_val("tensor", b.sems["vector"], dproj_done[g % 2])
                    b.mm(bank[:, :], rr(wDT[:], "k (g q) -> k g q", g=DG)[:, g, :],
                         dtb[:, :], start=True, stop=True, inc=True)
                    mmn = b.sems["tensor"].n
                    b.wait_val("scalar", b.sems["tensor"], mmn)
                    if g >= 1:
                        b.wait_val("scalar", b.sems["vector"], dproj_done[(g + 1) % 2])
                    b.act(t_sm[:, 0:512], bank[:, :], AF.Identity,
                          bias=pv["dt_b"][:, g:g + 1])
                    b.act(t_d[:, 0:512], t_sm[:, 0:512], AF.Abs)
                    b.act(t_d[:, 0:512], t_d[:, 0:512], AF.Exp, scale=-1.0)
                    spn = b.act(t_d[:, 0:512], t_d[:, 0:512], AF.Ln, bias=1.0)
                    b.wait_val("vector", b.sems["scalar"], spn)
                    b.ts(t_d[:, 512:1024], t_sm[:, 0:512], 0.0, ALU.max)
                    dn = b.tt(xv[:, g, :], t_d[:, 512:1024], t_d[:, 0:512], ALU.add)
                    if g == 0:
                        dproj_done = [0, 0]
                    dproj_done[g % 2] = dn
                delta_done = b.sems["vector"].n
                if "delta" in dbg:
                    b.wait_val("scalar", b.sems["vector"], delta_done)
                    b.dma("scalar", rr(delta_d, "(g p) t -> p g t", p=P)[:, :, sl], xv)
                dxn = b.tt(t_v[:], t_x[:], t_u[:], ALU.mult)
                b.wait_val("gpsimd", b.sems["qscalar"], bcst)
                for q4 in range(4):
                    ns = q4 * 4
                    b.wait_val("gpsimd", b.sems["vector"], bc_guard[0])
                    rbb = b.dma("gpsimd", rr(t_w[:, 0:2048], "p (j t) -> p j t", j=4),
                                bc_d[ns:ns + 4, 0:LC].unsqueeze(0).broadcast_to((P, 4, LC)))
                    rcc = b.dma("gpsimd", rr(t_w[:, 2048:4096], "p (j t) -> p j t", j=4),
                                bc_d[16 + ns:16 + ns + 4, 0:LC].unsqueeze(0).broadcast_to((P, 4, LC)))
                    bcd_guard[0] = rcc
                    b.wait_val("vector", b.sems["qgpsimd"], rcc)
                    if "tw" in dbg and k == 0 and q4 == 0:
                        b.dma("gpsimd", tw_d, t_w[:])
                        b.wait_val("vector", b.sems["qgpsimd"], b.sems["qgpsimd"].n)
                    h8 = rr(t_h[:], "p (j t) -> p j t", j=4)
                    for g in range(DG):
                        for j in range(4):
                            n = ns + j
                            sa = t_d[:, (j % 2) * 1024:(j % 2) * 1024 + 512]
                            se = t_d[:, (j % 2) * 1024 + 512:(j % 2) * 1024 + 1024]
                            tsn = b.ts(sa, xv[:, g, :], A0_sb[:, g * NSTATE + n:g * NSTATE + n + 1],
                                       ALU.mult)
                            b.wait_val("scalar", b.sems["vector"], tsn)
                            expn = b.act(se, sa, AF.Exp)
                            b.wait_val("vector", b.sems["scalar"], expn)
                            b.tt(sa, t_v[:].rearrange("p (g t) -> p g t", g=DG)[:, g, :],
                                 rr(t_w[:, 0:2048], "p (j t) -> p j t", j=4)[:, j, :],
                                 ALU.mult)
                            b.scan(h8[:, j, :], se, sa,
                                   hcar[:, g * NSTATE + n:g * NSTATE + n + 1])
                        if "th" in dbg and k == 0 and q4 == 0 and g == 0:
                            hn = b.sems["vector"].n
                            b.wait_val("scalar", b.sems["vector"], hn)
                            b.dma("scalar", td_d, t_d[:])
                            b.dma("scalar", th_d, t_h[:])
                            b.wait_val("vector", b.sems["qscalar"], b.sems["qscalar"].n)
                        b.cpy(hcar[:, g * NSTATE + ns:g * NSTATE + ns + 4].unsqueeze(2),
                              h8[:, :, 511:512])
                        b.tt(t_h[:], t_h[:], t_w[:, 2048:4096], ALU.mult)
                        if q4 == 0:
                            b.red(yv[:, g, :], rr(t_h[:], "p (n t) -> p t n", n=4))
                        else:
                            b.red(t_d[:, 0:512], rr(t_h[:], "p (n t) -> p t n", n=4))
                            b.tt(yv[:, g, :], yv[:, g, :], t_d[:, 0:512], ALU.add)
                    bc_guard[0] = b.sems["vector"].n
                if "ysc" in dbg:
                    b.wait_val("scalar", b.sems["vector"], bc_guard[0])
                    b.dma("scalar", rr(ysc_d, "(g p) t -> p g t", p=P)[:, :, sl], yv)
                b.wait_val("sync", b.sems["vector"], bc_guard[0])
                b.wait_val("sync", b.sems["qscalar"], zst)
                zsl = b.dma("sync", vv_, rr(zs_d, "(g p) t -> p g t", p=P)[:, :, sl])
                b.wait_val("vector", b.sems["sync"], zsl)
                for g in range(DG):
                    b.stt(uv[:, g, :], uv[:, g, :], pv["Dp0"][:, g:g + 1],
                          yv[:, g, :], ALU.mult, ALU.add)
                    b.tt(cbv[:, g, :], uv[:, g, :], vv_[:, g, :], ALU.mult)
                yg_done = b.sems["vector"].n
                if "ymix" in dbg:
                    b.wait_val("scalar", b.sems["vector"], yg_done)
                    b.dma("scalar", rr(ymix_d, "(g p) t -> p g t", p=P)[:, :, sl], uv)
                b.wait_val("tensor", b.sems["vector"], yg_done)
                ev_prev = [0, 0]
                for m in range(DG):
                    bank = pb[m % 2]
                    if ev_prev[m % 2]:
                        b.wait_val("tensor", b.sems["scalar"], ev_prev[m % 2])
                    for gk in range(DG):
                        b.mm(bank[:, :], rr(wOUT[:], "p (g m q) -> p g m q",
                                            g=DG, m=DG)[:, gk, m, :],
                             cbv[:, gk, :], start=(gk == 0), stop=(gk == DG - 1),
                             inc=(gk == DG - 1))
                    mmn = b.sems["tensor"].n
                    b.wait_val("scalar", b.sems["tensor"], mmn)
                    ev_prev[m % 2] = b.act(xv[:, m, :], bank[:, :], AF.Identity)
                hm_done = max(ev_prev)
                if "hm" in dbg:
                    b.wait_val("scalar", b.sems["scalar"], hm_done)
                    b.dma("scalar", rr(hm_d, "(g p) t -> p g t", p=P)[:, :, sl], xv)
                b.wait_val("vector", b.sems["scalar"], hm_done)
                lr = ln_rows(xv, vv_, t_sm, LC, pb[2], pb[3])
                gn = repl_rows(t_sm, mv, LC, lr)
                lhn = ln_apply(xv, vv_, mv, LC, gn, None, None)
                b.wait_val("scalar", b.sems["vector"], lhn)
                for g in range(DG):
                    b.act(vv_[:, g, :], vv_[:, g, :], AF.Identity,
                          bias=pv["mnorm_b"][:, g:g + 1],
                          scale=pv["mnorm_g"][:, g:g + 1])
                mnn = b.sems["scalar"].n
                b.wait_val("sync", b.sems["qscalar"], xnst)
                b.wait_val("sync", b.sems["vector"], yg_done)
                b.wait_val("sync", b.sems["qscalar"], b.sems["qscalar"].n)
                xnl = b.dma("sync", uv, rr(xn_fm_d, "(g p) t -> p g t", p=P)[:, :, sl])
                b.wait_val("vector", b.sems["sync"], xnl)
                b.wait_val("vector", b.sems["scalar"], mnn)
                s2n = b.tt(t_u[:], t_u[:], t_v[:], ALU.add)
                b.wait_val("scalar", b.sems["vector"], s2n)
                b.dma("scalar", rr(src2_d, "(g p) t -> p g t", p=P)[:, :, sl], uv)
                b.barrier()
            b.flush()


        if "A2a" in phases:
            LC2 = 256
            wf1 = b.alloc("wf1", [P, DG * 32 * P], BF16)
            wf2 = b.alloc("wf2", [P, 32 * DG * P], BF16)
            t_a = b.alloc("a2_a", [P, DG * LC2])
            t_q = b.alloc("a2_q", [P, DG * LC2])
            t_nb = b.alloc("a2_nb", [P, DG * LC2], BF16)
            t_fb = b.alloc("a2_fb", [P, 32 * LC2], BF16)
            t_e = b.alloc("a2_e", [P, LC2])
            t_sm = b.alloc("a2_sm", [P, 3 * LC2])
            murch = b.alloc("a2_mu", [P, 2 * LC2])
            lw1 = b.dma("gpsimd", rr(wf1[:], "p (g m q) -> p g m q", g=DG, m=32),
                        rr(W_f1T, "(g p) (m q) -> p g m q", p=P, q=P))
            lw2 = b.dma("gpsimd", rr(wf2[:], "p (kk m q) -> p kk m q", kk=32, m=DG),
                        rr(W_f2T, "(kk p) (m q) -> p kk m q", p=P, q=P))
            b.wait_val("tensor", b.sems["qgpsimd"], lw2)
            wf1v = rr(wf1[:], "p (g m q) -> p g m q", g=DG, m=32)
            wf2v = rr(wf2[:], "p (kk m q) -> p kk m q", kk=32, m=DG)
            for k in range(NTOK // LC2):
                sl = slice(k * LC2, (k + 1) * LC2)
                av = rr(t_a[:], "p (g t) -> p g t", g=DG)
                qv = rr(t_q[:], "p (g t) -> p g t", g=DG)
                nv = rr(t_nb[:], "p (g t) -> p g t", g=DG)
                fv = rr(t_fb[:], "p (m t) -> p m t", m=32)
                mv = rr(murch[:], "p (r t) -> p r t", r=2)
                ldx = b.dma("sync", av, rr(src2_d, "(g p) t -> p g t", p=P)[:, :, sl])
                b.wait_val("scalar", b.sems["sync"], ldx)
                b.wait_val("vector", b.sems["sync"], ldx)
                lr = ln_rows(av, qv, t_sm, LC2, pb[2], pb[3])
                gn = repl_rows(t_sm, mv, LC2, lr)
                lnb = ln_apply(av, qv, mv, LC2, gn, ("n2_g", "n2_b"), nv)
                b.wait_val("tensor", b.sems["scalar"], lnb)
                ev_prev = [0, 0]
                for m in range(32):
                    bank = pb[m % 2]
                    if ev_prev[m % 2]:
                        b.wait_val("tensor", b.sems["vector"], ev_prev[m % 2])
                    for g in range(DG):
                        b.mm(bank[:, 0:LC2], wf1v[:, g, m, :], nv[:, g, :],
                             start=(g == 0), stop=(g == DG - 1), inc=(g == DG - 1))
                    mmn = b.sems["tensor"].n
                    b.wait_val("scalar", b.sems["tensor"], mmn)
                    if max(ev_prev):
                        b.wait_val("scalar", b.sems["vector"], max(ev_prev))
                    een = b.act(t_e[:, :], bank[:, 0:LC2], AF.Identity)
                    b.wait_val("vector", b.sems["scalar"], een)
                    ev_prev[m % 2] = b.stt(fv[:, m, :], t_e[:, :], 0.01, t_e[:, :],
                                           ALU.mult, ALU.max)
                f_done = b.sems["vector"].n
                b.wait_val("tensor", b.sems["vector"], f_done)
                ev_prev = [0, 0]
                for m in range(DG):
                    bank = pb[m % 2]
                    if ev_prev[m % 2]:
                        b.wait_val("tensor", b.sems["vector"], ev_prev[m % 2])
                    for kk in range(32):
                        b.mm(bank[:, 0:LC2], wf2v[:, kk, m, :], fv[:, kk, :],
                             start=(kk == 0), stop=(kk == 31), inc=(kk == 31))
                    mmn = b.sems["tensor"].n
                    b.wait_val("vector", b.sems["tensor"], mmn)
                    ev_prev[m % 2] = b.tt(av[:, m, :], bank[:, 0:LC2], av[:, m, :],
                                          ALU.add)
                s3n = b.sems["vector"].n
                b.wait_val("scalar", b.sems["vector"], s3n)
                b.dma("scalar", rr(src3_d, "(g p) t -> p g t", p=P)[:, :, sl], av)
                b.barrier()
            b.flush()

        if "A2b" in phases:
            wfe = b.alloc("wfe", [P, DG * DG * P], BF16)
            t_a = b.alloc("b2_a", [P, DG * LC])
            t_q = b.alloc("b2_q", [P, DG * LC])
            t_nb = b.alloc("b2_nb", [P, DG * LC], BF16)
            t_e = b.alloc("b2_e", [P, LC])
            t_sm = b.alloc("b2_sm", [P, 3 * LC])
            murch = b.alloc("b2_mu", [P, 2 * LC])
            lw = b.dma("gpsimd", rr(wfe[:], "p (g m q) -> p g m q", g=DG, m=DG),
                       rr(W_feT, "(g p) (m q) -> p g m q", p=P, q=P))
            b.wait_val("tensor", b.sems["qgpsimd"], lw)
            wfev = rr(wfe[:], "p (g m q) -> p g m q", g=DG, m=DG)
            for k in range(NCH):
                sl = slice(k * LC, (k + 1) * LC)
                av = rr(t_a[:], "p (g t) -> p g t", g=DG)
                qv = rr(t_q[:], "p (g t) -> p g t", g=DG)
                nv = rr(t_nb[:], "p (g t) -> p g t", g=DG)
                mv = rr(murch[:], "p (r t) -> p r t", r=2)
                ldx = b.dma("sync", av, rr(src3_d, "(g p) t -> p g t", p=P)[:, :, sl])
                b.wait_val("scalar", b.sems["sync"], ldx)
                b.wait_val("vector", b.sems["sync"], ldx)
                lr = ln_rows(av, qv, t_sm, LC, pb[2], pb[3])
                gn = repl_rows(t_sm, mv, LC, lr)
                lnb = ln_apply(av, qv, mv, LC, gn, ("n3_g", "n3_b"), nv)
                b.wait_val("tensor", b.sems["scalar"], lnb)
                ev_prev = [0, 0]
                for m in range(DG):
                    bank = pb[m % 2]
                    if ev_prev[m % 2]:
                        b.wait_val("tensor", b.sems["vector"], ev_prev[m % 2])
                    for g in range(DG):
                        b.mm(bank[:, :], wfev[:, g, m, :], nv[:, g, :],
                             start=(g == 0), stop=(g == DG - 1), inc=(g == DG - 1))
                    mmn = b.sems["tensor"].n
                    b.wait_val("scalar", b.sems["tensor"], mmn)
                    if max(ev_prev):
                        b.wait_val("scalar", b.sems["vector"], max(ev_prev))
                    een = b.act(t_e[:, :], bank[:, :], AF.Identity,
                                bias=pv["fe_b"][:, m:m + 1])
                    b.wait_val("vector", b.sems["scalar"], een)
                    ev_prev[m % 2] = b.tt(av[:, m, :], t_e[:, :], av[:, m, :],
                                          ALU.add)
                sn = b.sems["vector"].n
                b.wait_val("scalar", b.sems["vector"], sn)
                b.dma("scalar", rr(s_attn_d, "(g p) t -> p g t", p=P)[:, :, sl], av)
                b.barrier()
            b.flush()

        if "A3" in phases:
            LC2 = 256
            wf1 = b.alloc("f3_w1", [P, DG * 32 * P], BF16)
            wf2 = b.alloc("f3_w2", [P, 32 * DG * P], BF16)
            t_a = b.alloc("f3_a", [P, DG * LC2])
            t_nb = b.alloc("f3_nb", [P, DG * LC2], BF16)
            t_fb = b.alloc("f3_fb", [P, 32 * LC2], BF16)
            lw1 = b.dma("gpsimd", rr(wf1[:], "p (g m q) -> p g m q", g=DG, m=32),
                        rr(fw1, "(g p) (m q) -> p g m q", p=P, q=P))
            lw2 = b.dma("gpsimd", rr(wf2[:], "p (kk m q) -> p kk m q", kk=32, m=DG),
                        rr(fw2, "(kk p) (m q) -> p kk m q", p=P, q=P))
            b.wait_val("tensor", b.sems["qgpsimd"], lw2)
            wf1v = rr(wf1[:], "p (g m q) -> p g m q", g=DG, m=32)
            wf2v = rr(wf2[:], "p (kk m q) -> p kk m q", kk=32, m=DG)
            for k in range(NTOK // LC2):
                sl = slice(k * LC2, (k + 1) * LC2)
                av = rr(t_a[:], "p (g t) -> p g t", g=DG)
                nv = rr(t_nb[:], "p (g t) -> p g t", g=DG)
                fv = rr(t_fb[:], "p (m t) -> p m t", m=32)
                ldx = b.dma("sync", nv,
                            rr(xnbf_fm_d, "(g p) t -> p g t", p=P)[:, :, sl])
                b.wait_val("tensor", b.sems["sync"], ldx)
                ev_prev = [0, 0]
                for m in range(32):
                    bank = pb[m % 2]
                    if ev_prev[m % 2]:
                        b.wait_val("tensor", b.sems["scalar"], ev_prev[m % 2])
                    for g in range(DG):
                        b.mm(bank[:, 0:LC2], wf1v[:, g, m, :], nv[:, g, :],
                             start=(g == 0), stop=(g == DG - 1), inc=(g == DG - 1))
                    mmn = b.sems["tensor"].n
                    b.wait_val("scalar", b.sems["tensor"], mmn)
                    ev_prev[m % 2] = b.act(fv[:, m, :], bank[:, 0:LC2], AF.Gelu,
                                           bias=fb1_sb[:, m:m + 1])
                f_done = b.sems["scalar"].n
                b.wait_val("tensor", b.sems["scalar"], f_done)
                ev_prev = [0, 0]
                for m in range(DG):
                    bank = pb[m % 2]
                    if ev_prev[m % 2]:
                        b.wait_val("tensor", b.sems["scalar"], ev_prev[m % 2])
                    for kk in range(32):
                        b.mm(bank[:, 0:LC2], wf2v[:, kk, m, :], fv[:, kk, :],
                             start=(kk == 0), stop=(kk == 31), inc=(kk == 31))
                    mmn = b.sems["tensor"].n
                    b.wait_val("scalar", b.sems["tensor"], mmn)
                    ev_prev[m % 2] = b.act(av[:, m, :], bank[:, 0:LC2], AF.Identity,
                                           bias=pv["fb2"][:, m:m + 1])
                yn = b.sems["scalar"].n
                b.wait_val("scalar", b.sems["scalar"], yn)
                b.dma("scalar", rr(yfb_d, "(g p) t -> p g t", p=P)[:, :, sl], av)
                b.barrier()
            b.flush()

        if "M" in phases:
            LC2 = 256
            wf1 = b.alloc("m_w1", [P, DG * 32 * P], BF16)
            wf2 = b.alloc("m_w2", [P, 32 * DG * P], BF16)
            t_a = b.alloc("m_a", [P, DG * LC2])
            t_b = b.alloc("m_b", [P, DG * LC2])
            t_nb = b.alloc("m_nb", [P, DG * LC2], BF16)
            t_fb = b.alloc("m_fb", [P, 32 * LC2], BF16)
            wkr = b.alloc("m_wkr", [P, LC2])
            wf1v = rr(wf1[:], "p (g m q) -> p g m q", g=DG, m=32)
            wf2v = rr(wf2[:], "p (kk m q) -> p kk m q", kk=32, m=DG)
            for e in range(NEXP):
                b.wait("gpsimd", "tensor")
                lw1 = b.dma("gpsimd", wf1v, rr(W1[e], "(g p) (m q) -> p g m q",
                                               p=P, q=P))
                lw2 = b.dma("gpsimd", wf2v, rr(W2[e], "(kk p) (m q) -> p kk m q",
                                               p=P, q=P))
                b.wait_val("tensor", b.sems["qgpsimd"], lw2)
                for k in range(NTOK // LC2):
                    sl = slice(k * LC2, (k + 1) * LC2)
                    av = rr(t_a[:], "p (g t) -> p g t", g=DG)
                    bv2 = rr(t_b[:], "p (g t) -> p g t", g=DG)
                    nv = rr(t_nb[:], "p (g t) -> p g t", g=DG)
                    fv = rr(t_fb[:], "p (m t) -> p m t", m=32)
                    ldx = b.dma("sync", nv,
                                rr(xnbf_fm_d, "(g p) t -> p g t", p=P)[:, :, sl])
                    lwk = b.dma("gpsimd", wkr[:],
                                own_rq[e, sl].unsqueeze(0).broadcast_to((P, LC2)))
                    if e > 0:
                        ldg = b.dma("sync", bv2,
                                    rr(gmoe_d, "(g p) t -> p g t", p=P)[:, :, sl])
                    b.wait_val("tensor", b.sems["sync"], ldx)
                    ev_prev = [0, 0]
                    for m in range(32):
                        bank = pb[m % 2]
                        if ev_prev[m % 2]:
                            b.wait_val("tensor", b.sems["scalar"], ev_prev[m % 2])
                        for g in range(DG):
                            b.mm(bank[:, 0:LC2], wf1v[:, g, m, :], nv[:, g, :],
                                 start=(g == 0), stop=(g == DG - 1),
                                 inc=(g == DG - 1))
                        mmn = b.sems["tensor"].n
                        b.wait_val("scalar", b.sems["tensor"], mmn)
                        ev_prev[m % 2] = b.act(fv[:, m, :], bank[:, 0:LC2], AF.Gelu,
                                               bias=b1e_sb[:, e * 32 + m:e * 32 + m + 1])
                    f_done = b.sems["scalar"].n
                    b.wait_val("tensor", b.sems["scalar"], f_done)
                    ev_prev = [0, 0]
                    for m in range(DG):
                        bank = pb[m % 2]
                        if ev_prev[m % 2]:
                            b.wait_val("tensor", b.sems["scalar"], ev_prev[m % 2])
                        for kk in range(32):
                            b.mm(bank[:, 0:LC2], wf2v[:, kk, m, :], fv[:, kk, :],
                                 start=(kk == 0), stop=(kk == 31), inc=(kk == 31))
                        mmn = b.sems["tensor"].n
                        b.wait_val("scalar", b.sems["tensor"], mmn)
                        ev_prev[m % 2] = b.act(av[:, m, :], bank[:, 0:LC2],
                                               AF.Identity,
                                               bias=b2e_sb[:, e * 8 + m:e * 8 + m + 1])
                    yn = b.sems["scalar"].n
                    b.wait_val("vector", b.sems["scalar"], yn)
                    b.wait_val("vector", b.sems["qgpsimd"], lwk)
                    wkb = wkr[:].unsqueeze(1).broadcast_to((P, DG, LC2))
                    if e == 0:
                        gm = b.tt(bv2, av, wkb, ALU.mult)
                    else:
                        b.wait_val("vector", b.sems["sync"], ldg)
                        b.tt(av, av, wkb, ALU.mult)
                        gm = b.tt(bv2, bv2, av, ALU.add)
                    b.wait_val("scalar", b.sems["vector"], gm)
                    b.dma("scalar", rr(gmoe_d, "(g p) t -> p g t", p=P)[:, :, sl],
                          bv2)
                    b.barrier()
            b.flush()

        if "CMB" in phases:
            t_a = b.alloc("c_a", [P, DG * LC])
            t_b = b.alloc("c_b", [P, DG * LC])
            t_c = b.alloc("c_c", [P, DG * LC])
            t_dd = b.alloc("c_d", [P, DG * LC])
            fbr = b.alloc("c_fbr", [P, LC])
            ofr = b.alloc("c_ofr", [P, LC])
            for k in range(NCH):
                sl = slice(k * LC, (k + 1) * LC)
                av = rr(t_a[:], "p (g t) -> p g t", g=DG)
                bv2 = rr(t_b[:], "p (g t) -> p g t", g=DG)
                cv = rr(t_c[:], "p (g t) -> p g t", g=DG)
                dv = rr(t_dd[:], "p (g t) -> p g t", g=DG)
                l1_ = b.dma("sync", av, rr(gmoe_d, "(g p) t -> p g t", p=P)[:, :, sl])
                l2_ = b.dma("sync", bv2, rr(yfb_d, "(g p) t -> p g t", p=P)[:, :, sl])
                l3_ = b.dma("sync", cv, rr(x_fm, "(g p) t -> p g t", p=P)[:, :, sl])
                l4_ = b.dma("sync", dv,
                            rr(s_attn_d, "(g p) t -> p g t", p=P)[:, :, sl])
                lf = b.dma("gpsimd", fbr[:],
                           own_rq[NEXP, sl].unsqueeze(0).broadcast_to((P, LC)))
                b.wait_val("vector", b.sems["sync"], l4_)
                b.wait_val("vector", b.sems["qgpsimd"], lf)
                b.ts(ofr[:], fbr[:], -1.0, ALU.mult, 1.0, ALU.add)
                ofb = ofr[:].unsqueeze(1).broadcast_to((P, DG, LC))
                fbb = fbr[:].unsqueeze(1).broadcast_to((P, DG, LC))
                b.tt(av, av, ofb, ALU.mult)
                b.tt(bv2, bv2, fbb, ALU.mult)
                ggn = b.tt(av, av, bv2, ALU.add)
                b.wait_val("scalar", b.sems["vector"], ggn)
                sgn = b.act(av, av, AF.Silu)
                b.wait_val("vector", b.sems["scalar"], sgn)
                b.tt(dv, dv, av, ALU.mult)
                x1n = b.tt(cv, cv, dv, ALU.add)
                b.wait_val("scalar", b.sems["vector"], x1n)
                b.dma("scalar", rr(x1_d, "(g p) t -> p g t", p=P)[:, :, sl], cv)
                b.barrier()
            b.flush()

        if "C" in phases:
            wx1 = b.alloc("wx1", [P, DG * 96], BF16)
            wdt1 = b.alloc("wdt1", [DTRANK, DG * P], BF16)
            wR = b.alloc("wR", [P, DG * DG * P], BF16)
            wK = b.alloc("wK", [P, DG * DG * P], BF16)
            wV = b.alloc("wV", [P, DG * DG * P], BF16)
            t_x = b.alloc("c1_x", [P, 4096])
            t_u = b.alloc("c1_u", [P, 4096])
            t_v = b.alloc("c1_v", [P, 4096])
            t_w = b.alloc("c1_w", [P, 4096])
            t_h = b.alloc("c1_h", [P, 2048])
            t_y = b.alloc("c1_y", [P, 4096])
            t_d = b.alloc("c1_d", [P, 2048])
            t_lb = b.alloc("c1_lb", [P, 4096], BF16)
            t_cb = b.alloc("c1_cb", [P, 4096], BF16)
            t_cb2 = b.alloc("c1_cb2", [P, 4096], BF16)
            dtb = b.alloc("c1_dtb", [DTRANK, 512], BF16)
            t_sm = b.alloc("c1_sm", [P, 3 * 512])
            murch = b.alloc("c1_mu", [P, 2 * 512])
            lw1 = b.dma("gpsimd", rr(wx1[:], "p (g m) -> p g m", g=DG),
                        rr(W_x1T, "(g p) m -> p g m", p=P))
            lw2 = b.dma("gpsimd", rr(wdt1[:], "k (g q) -> k g q", g=DG),
                        rr(W_dt1T, "k (g q) -> k g q", q=P))
            lw3 = b.dma("gpsimd", rr(wR[:], "p (g m q) -> p g m q", g=DG, m=DG),
                        rr(WR1T, "(g p) (m q) -> p g m q", p=P, q=P))
            lw4 = b.dma("gpsimd", rr(wK[:], "p (g m q) -> p g m q", g=DG, m=DG),
                        rr(WK1T, "(g p) (m q) -> p g m q", p=P, q=P))
            lw5 = b.dma("gpsimd", rr(wV[:], "p (g m q) -> p g m q", g=DG, m=DG),
                        rr(WV1T, "(g p) (m q) -> p g m q", p=P, q=P))
            b.wait_val("tensor", b.sems["qgpsimd"], lw5)
            wx1v = rr(wx1[:], "p (g m) -> p g m", g=DG)
            wdt1v = rr(wdt1[:], "k (g q) -> k g q", g=DG)
            wRv = rr(wR[:], "p (g m q) -> p g m q", g=DG, m=DG)
            wKv = rr(wK[:], "p (g m q) -> p g m q", g=DG, m=DG)
            wVv = rr(wV[:], "p (g m q) -> p g m q", g=DG, m=DG)
            for k in range(NCH):
                sl = slice(k * LC, (k + 1) * LC)
                first = (k % (SEQ // LC) == 0)
                xv = rr(t_x[:], "p (g t) -> p g t", g=DG)
                uv = rr(t_u[:], "p (g t) -> p g t", g=DG)
                vv_ = rr(t_v[:], "p (g t) -> p g t", g=DG)
                wv = rr(t_w[:], "p (g t) -> p g t", g=DG)
                yv = rr(t_y[:], "p (g t) -> p g t", g=DG)
                lbv = rr(t_lb[:], "p (g t) -> p g t", g=DG)
                cbv = rr(t_cb[:], "p (g t) -> p g t", g=DG)
                cb2v = rr(t_cb2[:], "p (g t) -> p g t", g=DG)
                mv = rr(murch[:], "p (r t) -> p r t", r=2)
                if first:
                    b.memset(hcar[:], 0.0)
                    b.memset(carry[:, DG:2 * DG], 0.0, eng="vector")
                ldx = b.dma("sync", xv, rr(x1_d, "(g p) t -> p g t", p=P)[:, :, sl])
                b.wait_val("scalar", b.sems["sync"], ldx)
                b.wait_val("vector", b.sems["sync"], ldx)
                lr = ln_rows(xv, wv, t_sm, LC, pb[2], pb[3])
                gn = repl_rows(t_sm, mv, LC, lr)
                b.wait_val("vector", b.sems["qgpsimd"], gn)
                rr_b = mv[:, 0:1, :].broadcast_to((P, DG, LC))
                nmr_b = mv[:, 1:2, :].broadcast_to((P, DG, LC))
                b.tt(uv, xv, rr_b, ALU.mult)
                xnn = b.tt(uv, uv, nmr_b, ALU.add)
                xbn = b.cpy(t_lb[:], t_u[:])
                for g in range(DG):
                    b.cpy(t_d[:, 1:512], uv[:, g, 0:511])
                    b.cpy(t_d[:, 0:1], carry[:, DG + g:DG + g + 1])
                    b.cpy(carry[:, DG + g:DG + g + 1], uv[:, g, 511:512])
                    b.ts(t_d[:, 512:1024], t_d[:, 0:512],
                         pv["om_mu_r"][:, g:g + 1], ALU.mult)
                    b.stt(cbv[:, g, :], uv[:, g, :], pv["mu_r"][:, g:g + 1],
                          t_d[:, 512:1024], ALU.mult, ALU.add)
                    b.ts(t_d[:, 512:1024], t_d[:, 0:512],
                         pv["om_mu_k"][:, g:g + 1], ALU.mult)
                    b.stt(cb2v[:, g, :], uv[:, g, :], pv["mu_k"][:, g:g + 1],
                          t_d[:, 512:1024], ALU.mult, ALU.add)
                shf_done = b.sems["vector"].n
                b.wait_val("tensor", b.sems["vector"], xbn)
                for g in range(DG):
                    b.mm(pb[4][0:DTRANK, :], wx1v[:, g, 0:DTRANK], lbv[:, g, :],
                         start=(g == 0), stop=(g == DG - 1))
                for g in range(DG):
                    b.mm(pb[5][0:32, :], wx1v[:, g, DTRANK:96], lbv[:, g, :],
                         start=(g == 0), stop=(g == DG - 1), inc=(g == DG - 1))
                xpn = b.sems["tensor"].n
                b.wait_val("scalar", b.sems["tensor"], xpn)
                dtev = b.act(dtb[:, :], pb[4][0:DTRANK, :], AF.Identity)
                bcev = b.act(t_sm[0:32, 0:512], pb[5][0:32, :], AF.Identity)
                b.wait_val("scalar", b.sems["qgpsimd"], bcd_guard[0])
                b.wait_val("scalar", b.sems["scalar"], bcev)
                bcst = b.dma("scalar", bc_d[:, 0:LC], t_sm[0:32, 0:512])
                b.wait_val("scalar", b.sems["qscalar"], bcst)
                b.wait_val("scalar", b.sems["vector"], shf_done)
                b.wait_val("tensor", b.sems["scalar"], dtev)
                for g in range(DG):
                    bank = pb[6] if g % 2 == 0 else pb[4]
                    if g >= 2:
                        b.wait_val("tensor", b.sems["vector"], dproj_done[g % 2])
                    b.mm(bank[:, :], wdt1v[:, g, :], dtb[:, :],
                         start=True, stop=True, inc=True)
                    mmn = b.sems["tensor"].n
                    b.wait_val("scalar", b.sems["tensor"], mmn)
                    if g >= 1:
                        b.wait_val("scalar", b.sems["vector"], dproj_done[(g + 1) % 2])
                    b.act(t_sm[:, 0:512], bank[:, :], AF.Identity,
                          bias=pv["b_dt1"][:, g:g + 1])
                    if k == 0 and g == 0 and "th" in dbg:
                        pdn = b.sems["scalar"].n
                        b.wait_val("scalar", b.sems["scalar"], pdn)
                        b.dma("scalar", th_d[:, 0:512], t_sm[:, 0:512])
                    b.act(t_d[:, 0:512], t_sm[:, 0:512], AF.Abs)
                    b.act(t_d[:, 0:512], t_d[:, 0:512], AF.Exp, scale=-1.0)
                    spn = b.act(t_d[:, 0:512], t_d[:, 0:512], AF.Ln, bias=1.0)
                    b.wait_val("vector", b.sems["scalar"], spn)
                    b.ts(t_d[:, 512:1024], t_sm[:, 0:512], 0.0, ALU.max)
                    dn = b.tt(xv[:, g, :], t_d[:, 512:1024], t_d[:, 0:512], ALU.add)
                    if k == 0 and g == 0 and "th" in dbg:
                        b.wait_val("scalar", b.sems["vector"], dn)
                        b.dma("scalar", th_d[:, 512:1024], t_d[:, 0:512])
                        b.dma("scalar", th_d[:, 1024:1536], xv[:, g, :])
                        b.wait_val("vector", b.sems["qscalar"], b.sems["qscalar"].n)
                    if g == 0:
                        dproj_done = [0, 0]
                    dproj_done[g % 2] = dn
                dl1n = b.tt(t_v[:], t_x[:], t_u[:], ALU.mult)
                if k == 0:
                    b.wait_val("scalar", b.sems["vector"], dl1n)
                    b.dma("scalar", a1_d, A1_sb[:])
                    b.dma("scalar", xn1_d, t_u[:])
                    dstn = b.dma("scalar", dl1_d, t_x[:])
                    b.wait_val("vector", b.sems["qscalar"], dstn)
                for q4 in range(4):
                    ns = q4 * 4
                    b.wait_val("gpsimd", b.sems["vector"], bc_guard[0])
                    rbb = b.dma("gpsimd", rr(t_w[:, 0:2048], "p (j t) -> p j t", j=4),
                                bc_d[ns:ns + 4, 0:LC].unsqueeze(0).broadcast_to((P, 4, LC)))
                    rcc = b.dma("gpsimd", rr(t_w[:, 2048:4096], "p (j t) -> p j t", j=4),
                                bc_d[16 + ns:16 + ns + 4, 0:LC].unsqueeze(0).broadcast_to((P, 4, LC)))
                    bcd_guard[0] = rcc
                    b.wait_val("vector", b.sems["qgpsimd"], rcc)
                    h8 = rr(t_h[:], "p (j t) -> p j t", j=4)
                    for g in range(DG):
                        for j in range(4):
                            n = ns + j
                            sa = t_d[:, (j % 2) * 1024:(j % 2) * 1024 + 512]
                            se = t_d[:, (j % 2) * 1024 + 512:(j % 2) * 1024 + 1024]
                            tsn = b.ts(sa, xv[:, g, :],
                                       A1_sb[:, g * NSTATE + n:g * NSTATE + n + 1],
                                       ALU.mult)
                            b.wait_val("scalar", b.sems["vector"], tsn)
                            expn = b.act(se, sa, AF.Exp)
                            b.wait_val("vector", b.sems["scalar"], expn)
                            b.tt(sa, vv_[:, g, :],
                                 rr(t_w[:, 0:2048], "p (j t) -> p j t", j=4)[:, j, :],
                                 ALU.mult)
                            b.scan(h8[:, j, :], se, sa,
                                   hcar[:, g * NSTATE + n:g * NSTATE + n + 1])
                        b.cpy(hcar[:, g * NSTATE + ns:g * NSTATE + ns + 4].unsqueeze(2),
                              h8[:, :, 511:512])
                        b.tt(t_h[:], t_h[:], t_w[:, 2048:4096], ALU.mult)
                        if q4 == 0:
                            b.red(yv[:, g, :], rr(t_h[:], "p (n t) -> p t n", n=4))
                        else:
                            b.red(t_d[:, 0:512], rr(t_h[:], "p (n t) -> p t n", n=4))
                            b.tt(yv[:, g, :], yv[:, g, :], t_d[:, 0:512], ALU.add)
                    bc_guard[0] = b.sems["vector"].n
                for g in range(DG):
                    b.stt(yv[:, g, :], uv[:, g, :], pv["D_par"][:, g:g + 1],
                          yv[:, g, :], ALU.mult, ALU.add)
                s1_done = b.sems["vector"].n
                b.wait_val("tensor", b.sems["vector"], s1_done)
                ev_prev = [0, 0]
                for m in range(DG):
                    bank = pb[m % 2]
                    if ev_prev[m % 2]:
                        b.wait_val("tensor", b.sems["scalar"], ev_prev[m % 2])
                    for gk in range(DG):
                        b.mm(bank[:, :], wRv[:, gk, m, :], cbv[:, gk, :],
                             start=(gk == 0), stop=(gk == DG - 1), inc=(gk == DG - 1))
                    mmn = b.sems["tensor"].n
                    b.wait_val("scalar", b.sems["tensor"], mmn)
                    ev_prev[m % 2] = b.act(vv_[:, m, :], bank[:, :], AF.Sigmoid)
                rt_done = b.sems["scalar"].n
                ev_prev = [0, 0]
                for m in range(DG):
                    bank = pb[m % 2]
                    if ev_prev[m % 2]:
                        b.wait_val("tensor", b.sems["scalar"], ev_prev[m % 2])
                    for gk in range(DG):
                        b.mm(bank[:, :], wKv[:, gk, m, :], cb2v[:, gk, :],
                             start=(gk == 0), stop=(gk == DG - 1), inc=(gk == DG - 1))
                    mmn = b.sems["tensor"].n
                    b.wait_val("scalar", b.sems["tensor"], mmn)
                    b.act(t_sm[:, 0:512], bank[:, :], AF.Relu)
                    ev_prev[m % 2] = b.act(lbv[:, m, :], t_sm[:, 0:512], AF.Square)
                k2_done = b.sems["scalar"].n
                b.wait_val("tensor", b.sems["scalar"], k2_done)
                b.wait_val("sync", b.sems["vector"], bc_guard[0])
                b.wait_val("sync", b.sems["qgpsimd"], b.sems["qgpsimd"].n)
                ldx1 = b.dma("sync", wv, rr(x1_d, "(g p) t -> p g t", p=P)[:, :, sl])
                ev_prev = [0, 0]
                for m in range(DG):
                    bank = pb[m % 2]
                    if ev_prev[m % 2]:
                        b.wait_val("tensor", b.sems["vector"], ev_prev[m % 2])
                    for gk in range(DG):
                        b.mm(bank[:, :], wVv[:, gk, m, :], lbv[:, gk, :],
                             start=(gk == 0), stop=(gk == DG - 1), inc=(gk == DG - 1))
                    mmn = b.sems["tensor"].n
                    b.wait_val("vector", b.sems["tensor"], mmn)
                    ev_prev[m % 2] = b.tt(xv[:, m, :], bank[:, :], vv_[:, m, :],
                                          ALU.mult)
                g1_done = b.sems["vector"].n
                b.wait_val("scalar", b.sems["vector"], g1_done)
                sgn = b.act(t_x[:], t_x[:], AF.Silu)
                b.wait_val("vector", b.sems["scalar"], sgn)
                b.wait_val("vector", b.sems["sync"], ldx1)
                b.tt(t_x[:], t_x[:], t_y[:], ALU.mult)
                onn = b.tt(t_x[:], t_x[:], t_w[:], ALU.add)
                b.wait_val("scalar", b.sems["vector"], onn)
                b.dma("scalar", rr(out_fm, "(g p) t -> p g t", p=P)[:, :, sl], xv)
                b.barrier()
            b.flush()

        for k2, ap in dbg_outs.items():
            b.dma("scalar", ap, DBG_REG[k2])
        b.barrier()
        b.flush()
    return nc


def make_in_maps(inputs):
    i = inputs
    C = np.ascontiguousarray
    x_flat = np.asarray(i["L0_attn_src"], np.float32).reshape(TGLOB, D)
    x_fm_full = C(x_flat.T)
    def vec_pg(v):
        return C(np.asarray(v, np.float32).reshape(DG, P).T)

    shared = {
        "x_fm_full": x_fm_full,
        "W_inT": C(np.asarray(i["L0_attn_in_proj_w"], np.float32).T),
        "conv_w0": vec_pg(np.asarray(i["L0_attn_conv_w"], np.float32)[:, 0]),
        "conv_w1": vec_pg(np.asarray(i["L0_attn_conv_w"], np.float32)[:, 1]),
        "conv_b": vec_pg(i["L0_attn_conv_b"]),
        "W_xpT": C(np.asarray(i["L0_attn_x_proj_w"], np.float32).T),
        "W_dtT": C(np.asarray(i["L0_attn_dt_proj_w"], np.float32).T),
        "dt_b": vec_pg(i["L0_attn_dt_proj_b"]),
        "A_log0": C(np.asarray(i["L0_attn_A_log"], np.float32).reshape(DG, P, NSTATE).transpose(1, 0, 2)),
        "Dp0": vec_pg(i["L0_attn_Dp"]),
        "W_outT": C(np.asarray(i["L0_attn_out_proj_w"], np.float32).T),
        "mnorm_g": vec_pg(i["L0_attn_mnorm_g"]),
        "mnorm_b": vec_pg(i["L0_attn_mnorm_b"]),
        "n1_g": vec_pg(i["L0_attn_n1_g"]),
        "n1_b": vec_pg(i["L0_attn_n1_b"]),
        "n2_g": vec_pg(i["L0_attn_n2_g"]),
        "n2_b": vec_pg(i["L0_attn_n2_b"]),
        "n3_g": vec_pg(i["L0_attn_n3_g"]),
        "n3_b": vec_pg(i["L0_attn_n3_b"]),
        "W_f1T": C(np.asarray(i["L0_attn_ffn_w1"], np.float32).T),
        "W_f2T": C(np.asarray(i["L0_attn_ffn_w2"], np.float32).T),
        "W_feT": C(np.asarray(i["L0_attn_fe_w"], np.float32).T),
        "fe_b": vec_pg(i["L0_attn_fe_b"]),
        "WrT": C(np.asarray(i["L0_ffn_Wr"], np.float32).T),
        "W1": C(np.asarray(i["L0_ffn_W1"], np.float32)),
        "b1": C(np.asarray(i["L0_ffn_b1"], np.float32).reshape(NEXP, 32, P).transpose(2, 0, 1)),
        "W2": C(np.asarray(i["L0_ffn_W2"], np.float32)),
        "b2": C(np.asarray(i["L0_ffn_b2"], np.float32).reshape(NEXP, 8, P).transpose(2, 0, 1)),
        "fw1": C(np.asarray(i["L0_ffn_fw1"], np.float32)),
        "fb1": C(np.asarray(i["L0_ffn_fb1"], np.float32).reshape(32, P).T),
        "fw2": C(np.asarray(i["L0_ffn_fw2"], np.float32)),
        "fb2": vec_pg(i["L0_ffn_fb2"]),
        "A_log1": C(np.asarray(i["L1_attn_A_log"], np.float32).reshape(DG, P, NSTATE).transpose(1, 0, 2)),
        "W_x1T": C(np.asarray(i["L1_attn_W_x"], np.float32).T),
        "W_dt1T": C(np.asarray(i["L1_attn_W_dt"], np.float32).T),
        "b_dt1": vec_pg(i["L1_attn_b_dt"]),
        "D_par": vec_pg(i["L1_attn_D_param"]),
        "WR1T": C(np.asarray(i["L1_ffn_W_r"], np.float32).T),
        "WK1T": C(np.asarray(i["L1_ffn_W_k"], np.float32).T),
        "WV1T": C(np.asarray(i["L1_ffn_W_v"], np.float32).T),
        "mu_r": vec_pg(i["L1_ffn_mu_r"]),
        "mu_k": vec_pg(i["L1_ffn_mu_k"]),
    }
    in_maps = []
    for c in range(8):
        m = dict(shared)
        m["x_fm"] = C(x_fm_full[:, c * NTOK:(c + 1) * NTOK])
        m["idx_own"] = (c * (NTOK // 32) + np.arange(P, dtype=np.int32)).reshape(P, 1)
        in_maps.append(m)
    return in_maps


def kernel(**inputs):
    nc = build_program(ALL_PHASES)
    in_maps = make_in_maps(inputs)
    res = run_bass_kernel_spmd(nc, in_maps, core_ids=list(range(8)))
    out = np.concatenate(
        [np.asarray(res.results[c]["out_fm"]).T for c in range(8)], axis=0)
    return np.ascontiguousarray(out.reshape(16, SEQ, D).astype(np.float32))


# version 04
# speedup vs baseline: 1.0750x; 1.0750x over previous
import contextlib
import numpy as np
import concourse.bass as bass
import concourse.mybir as mybir
from concourse.bass import IndirectOffsetOnAxis
from concourse.bass_utils import run_bass_kernel_spmd

F32 = mybir.dt.float32
BF16 = mybir.dt.bfloat16
I32 = mybir.dt.int32
AF = mybir.ActivationFunctionType
ALU = mybir.AluOpType
AX = mybir.AxisListType

P = 128
D = 1024
DG = 8
NTOK = 4096
TGLOB = 32768
SEQ = 2048
LC = 512
NCH = NTOK // LC
NSTATE = 16
DTRANK = 64
DFFN = 4096
NEXP = 8
CAP = 10240
EPS = 1e-5
RF = TGLOB // P

ENGS = ("sync", "scalar", "vector", "tensor", "gpsimd")


class Sem:
    def __init__(self, nc, es, name):
        self.h = es.enter_context(nc.semaphore(name))
        self.n = 0
        self.name = name


class B:

    def __init__(self, nc, es, block):
        self.nc = nc
        self.es = es
        self.block = block
        self.script = []
        self.sems = {e: Sem(nc, es, "s_" + e) for e in ENGS}
        self.sems["qscalar"] = Sem(nc, es, "q_scalar")
        self.sems["qgpsimd"] = Sem(nc, es, "q_gpsimd")
        self.phase_tiles = []

    def alloc(self, name, shape, dt=F32):
        cm = self.nc.sbuf_tensor(name, list(shape), dt)
        t = cm.__enter__()
        self.phase_tiles.append(cm)
        return t

    def flush(self, free=True):
        for engname in ENGS:
            steps = [fn for e, fn in self.script if e == engname]
            if not steps:
                continue
            deco = getattr(self.block, engname)

            def body(eng, _steps=steps):
                for fn in _steps:
                    fn(eng)
            deco(body)
        self.script = []
        if free:
            for cm in reversed(self.phase_tiles):
                cm.__exit__(None, None, None)
            self.phase_tiles = []

    def raw(self, eng, fn):
        s = self.sems[eng]
        k = 16 if eng == "sync" else 1
        s.n += k
        self.script.append((eng, lambda e, fn=fn, s=s, k=k: fn(e).then_inc(s.h, k)))
        return s.n

    def wait(self, eng, dep_eng, val=None):
        s = self.sems[dep_eng]
        v = s.n if val is None else val
        if v > 0:
            self.script.append((eng, lambda e, s=s, v=v: e.wait_ge(s.h, v)))
        if val is None and dep_eng in ("scalar", "gpsimd"):
            q = self.sems["q" + dep_eng]
            if q.n > 0:
                self.script.append((eng, lambda e, q=q, v=q.n: e.wait_ge(q.h, v)))

    def wait_val(self, eng, sem, v):
        if v > 0:
            self.script.append((eng, lambda e, sem=sem, v=v: e.wait_ge(sem.h, v)))

    def dma(self, eng, out, in_, **kw):
        s = self.sems[{"scalar": "qscalar", "gpsimd": "qgpsimd"}.get(eng, eng)]
        if s.n > 0:
            self.wait_val(eng, s, s.n)
        s.n += 16
        self.script.append(
            (eng, lambda e, out=out, in_=in_, kw=kw, s=s:
             e.dma_start(out=out, in_=in_, **kw).then_inc(s.h, 16)))
        return s.n

    def idma(self, out, out_off, in_, in_off, bc=None):
        s = self.sems["qgpsimd"]
        if s.n > 0:
            self.wait_val("gpsimd", s, s.n)
        s.n += 16

        def f(e, out=out, oo=out_off, in_=in_, io=in_off, bc=bc, s=s):
            kw = {}
            if bc is not None:
                kw = dict(bounds_check=bc, oob_is_err=False)
            return e.indirect_dma_start(out=out, out_offset=oo, in_=in_,
                                        in_offset=io, **kw).then_inc(s.h, 16)
        self.script.append(("gpsimd", f))
        return s.n

    def act(self, out, in_, func, bias=0.0, scale=1.0):
        return self.raw("scalar", lambda e, o=out, i=in_, f=func, bb=bias, ss=scale:
                        e.activation(out=o, in_=i, func=f, bias=bb, scale=ss))

    def tt(self, out, a, b_, op, eng="vector"):
        return self.raw(eng, lambda e, o=out, a=a, b=b_, op=op:
                        e.tensor_tensor(out=o, in0=a, in1=b, op=op))

    def ts(self, out, a, s1, op0, s2=None, op1=ALU.bypass, eng="vector"):
        return self.raw(eng, lambda e, o=out, a=a, s1=s1, s2=s2, op0=op0, op1=op1:
                        e.tensor_scalar(out=o, in0=a, scalar1=s1, scalar2=s2,
                                        op0=op0, op1=op1))

    def stt(self, out, in0, scalar, in1, op0, op1, eng="vector"):
        return self.raw(eng, lambda e, o=out, i0=in0, s=scalar, i1=in1,
                        op0=op0, op1=op1:
                        e.scalar_tensor_tensor(out=o, in0=i0, scalar=s, in1=i1,
                                               op0=op0, op1=op1))

    def scan(self, out, d0, d1, init, op0=ALU.mult, op1=ALU.add, eng="vector"):
        return self.raw(eng, lambda e, o=out, d0=d0, d1=d1, i=init,
                        op0=op0, op1=op1:
                        e.tensor_tensor_scan(out=o, data0=d0, data1=d1,
                                             initial=i, op0=op0, op1=op1))

    def red(self, out, in_, op=ALU.add, negate=False):
        return self.raw("vector", lambda e, o=out, i=in_, op=op, n=negate:
                        e.tensor_reduce(out=o, in_=i, axis=AX.X, op=op, negate=n))

    def cpy(self, out, in_, eng="vector"):
        return self.raw(eng, lambda e, o=out, i=in_: e.tensor_copy(out=o, in_=i))

    def memset(self, ap, v, eng="gpsimd"):
        return self.raw(eng, lambda e, a=ap, v=v: e.memset(a, v))

    def recip(self, out, in_):
        return self.raw("vector", lambda e, o=out, i=in_: e.reciprocal(out=o, in_=i))

    def mm(self, out, lhsT, rhs, start, stop, inc=False):
        s = self.sems["tensor"]
        if inc:
            s.n += 1
            self.script.append(
                ("tensor", lambda e, o=out, l=lhsT, r=rhs, a=start, bb=stop, s=s:
                 e.matmul(o, l, r, start=a, stop=bb).then_inc(s.h, 1)))
        else:
            self.script.append(
                ("tensor", lambda e, o=out, l=lhsT, r=rhs, a=start, bb=stop:
                 e.matmul(o, l, r, start=a, stop=bb)))
        return s.n

    def transpose(self, out, in_, ident):
        s = self.sems["tensor"]
        s.n += 1
        self.script.append(("tensor", lambda e, o=out, i=in_, d=ident, s=s:
                            e.transpose(o, i, d).then_inc(s.h, 1)))
        return s.n

    def iota(self, out, pattern, base=0, chan=0):
        return self.raw("gpsimd", lambda e, o=out, pp=pattern, bb=base, c=chan:
                        e.iota(o, pattern=pp, base=bb, channel_multiplier=c,
                               allow_small_or_imprecise_dtypes=True))

    def barrier(self):
        allsems = list(self.sems.values())
        for e in ENGS:
            for s in allsems:
                if s.name in ("s_" + e,):
                    continue
                if s.n > 0:
                    self.wait_val(e, s, s.n)


def rr(ap, spec, **kw):
    return ap.rearrange(spec, **kw)


ALL_PHASES = ("R", "A1", "A2a", "A2b", "A3", "M", "CMB", "C")


def build_program(phases=ALL_PHASES, dbg=()):
    nc = bass.Bass("TRN2", target_bir_lowering=False)

    def din(name, shape, dt=F32):
        return nc.dram_tensor(name, list(shape), dt, kind="ExternalInput").ap()

    x_fm_full = din("x_fm_full", [D, TGLOB])
    x_fm = din("x_fm", [D, NTOK])
    W_inT = din("W_inT", [D, 2 * D])
    conv_w0 = din("conv_w0", [P, DG])
    conv_w1 = din("conv_w1", [P, DG])
    conv_b = din("conv_b", [P, DG])
    W_xpT = din("W_xpT", [D, 96])
    W_dtT = din("W_dtT", [DTRANK, D])
    dt_b = din("dt_b", [P, DG])
    A_log0 = din("A_log0", [P, DG, NSTATE])
    Dp0 = din("Dp0", [P, DG])
    W_outT = din("W_outT", [D, D])
    mnorm_g = din("mnorm_g", [P, DG]); mnorm_b = din("mnorm_b", [P, DG])
    n1_g = din("n1_g", [P, DG]); n1_b = din("n1_b", [P, DG])
    n2_g = din("n2_g", [P, DG]); n2_b = din("n2_b", [P, DG])
    n3_g = din("n3_g", [P, DG]); n3_b = din("n3_b", [P, DG])
    W_f1T = din("W_f1T", [D, DFFN])
    W_f2T = din("W_f2T", [DFFN, D])
    W_feT = din("W_feT", [D, D])
    fe_b = din("fe_b", [P, DG])
    WrT = din("WrT", [D, NEXP])
    W1 = din("W1", [NEXP, D, DFFN])
    b1 = din("b1", [P, NEXP, DFFN // P])
    W2 = din("W2", [NEXP, DFFN, D])
    b2 = din("b2", [P, NEXP, D // P])
    fw1 = din("fw1", [D, DFFN])
    fb1 = din("fb1", [P, DFFN // P])
    fw2 = din("fw2", [DFFN, D])
    fb2 = din("fb2", [P, DG])
    A_log1 = din("A_log1", [P, DG, NSTATE])
    W_x1T = din("W_x1T", [D, 96])
    W_dt1T = din("W_dt1T", [DTRANK, D])
    b_dt1 = din("b_dt1", [P, DG])
    D_par = din("D_par", [P, DG])
    WR1T = din("WR1T", [D, D])
    WK1T = din("WK1T", [D, D])
    WV1T = din("WV1T", [D, D])
    mu_r = din("mu_r", [P, DG]); mu_k = din("mu_k", [P, DG])
    idx_own = din("idx_own", [P, 1], I32)

    out_fm = nc.dram_tensor("out_fm", [D, NTOK], F32, kind="ExternalOutput").ap()

    def scr(name, shape, dt=F32):
        return nc.dram_tensor(name, list(shape), dt, kind="Internal").ap()

    lraw_g = scr("lraw_g", [2 * NEXP + 2, TGLOB])
    own_stats = scr("own_stats", [2, NTOK])
    own_rq = scr("own_rq", [NEXP + 1, NTOK])
    xn_fm_d = scr("xn_fm_d", [D, NTOK])
    xnbf_fm_d = scr("xnbf_fm_d", [D, NTOK], BF16)
    zs_d = scr("zs_d", [D, NTOK])
    xc_d = scr("xc_d", [D, NTOK])
    delta_d = scr("delta_d", [D, NTOK])
    hm_d = scr("hm_d", [D, NTOK])
    ymix_d = scr("ymix_d", [D, NTOK])
    ysc_d = scr("ysc_d", [D, NTOK])
    tw_d = scr("tw_d", [P, 4096])
    td_d = scr("td_d", [P, 2048])
    a1_d = scr("a1_d", [P, DG * NSTATE])
    xn1_d = scr("xn1_d2", [P, 4096])
    dl1_d = scr("dl1_d", [P, 4096])
    th_d = scr("th_d", [P, 2048])
    src2_d = scr("src2_d", [D, NTOK])
    src3_d = scr("src3_d", [D, NTOK])
    s_attn_d = scr("s_attn_d", [D, NTOK])
    yfb_d = scr("yfb_d", [D, NTOK])
    gmoe_d = scr("gmoe_d", [D, NTOK])
    x1_d = scr("x1_d", [D, NTOK])
    rowb_d = scr("rowb_d", [2, LC])
    bc_d = scr("bc_d", [32, LC])
    sw_d = scr("sw_d", [NEXP])

    DBG_REG = {"own_stats": own_stats, "own_rq": own_rq, "xn": xn_fm_d,
               "xc": xc_d, "delta": delta_d, "hm": hm_d, "ymix": ymix_d,
               "ysc": ysc_d, "tw": tw_d, "td": td_d, "th": th_d,
               "a1sb": a1_d, "xn1": xn1_d, "dl1": dl1_d,
               "zs": zs_d, "src2": src2_d, "src3": src3_d, "s_attn": s_attn_d,
               "yfb": yfb_d, "gmoe": gmoe_d, "x1": x1_d, "lraw": lraw_g}
    dbg_outs = {}
    for k in dbg:
        t = DBG_REG[k]
        dbg_outs[k] = nc.dram_tensor("dbg_" + k, list(t.shape),
                                     t.dtype, kind="ExternalOutput").ap()

    with contextlib.ExitStack() as es:
        block = es.enter_context(nc.Block())
        b = B(nc, es, block)

        def psbt(name, shape, dt=F32):
            return es.enter_context(nc.sbuf_tensor(name, list(shape), dt))

        pv = {}
        for nm in ["conv_w0", "conv_w1", "conv_b", "dt_b", "Dp0", "mnorm_g",
                   "mnorm_b", "n1_g", "n1_b", "n2_g", "n2_b", "n3_g", "n3_b",
                   "fe_b", "fb2", "b_dt1", "D_par", "mu_r", "mu_k",
                   "om_mu_r", "om_mu_k"]:
            pv[nm] = psbt("pv_" + nm, [P, DG])
        A0_sb = psbt("A0_sb", [P, DG * NSTATE])
        A1_sb = psbt("A1_sb", [P, DG * NSTATE])
        fb1_sb = psbt("fb1_sb", [P, DFFN // P])
        b1e_sb = psbt("b1e_sb", [P, NEXP * (DFFN // P)])
        b2e_sb = psbt("b2e_sb", [P, NEXP * (D // P)])
        ones_sb = psbt("ones_sb", [P, 2])
        tri_sb = psbt("tri_sb", [P, P])
        WrT9 = psbt("WrT9", [P, DG * 9])
        swcol = psbt("swcol", [P, NEXP])
        idx1 = psbt("idx1", [P, 1], I32)
        idxq = psbt("idxq", [P, 1], I32)
        carry = psbt("carry", [P, 4 * DG])
        hcar = psbt("hcar", [P, DG * NSTATE])
        rows = psbt("rows", [P, 64])
        pb = [es.enter_context(nc.psum_tensor(f"pb{i}", [P, 512], F32))
              for i in range(7)]
        pbx = es.enter_context(nc.psum_tensor("pbx", [P, 512], F32))

        for nm, srcw in [("conv_b", conv_b), ("dt_b", dt_b), ("Dp0", Dp0),
                         ("mnorm_g", mnorm_g), ("mnorm_b", mnorm_b),
                         ("n1_g", n1_g), ("n1_b", n1_b), ("n2_g", n2_g),
                         ("n2_b", n2_b), ("n3_g", n3_g), ("n3_b", n3_b),
                         ("fe_b", fe_b), ("fb2", fb2), ("b_dt1", b_dt1),
                         ("D_par", D_par), ("mu_r", mu_r), ("mu_k", mu_k),
                         ("conv_w0", conv_w0), ("conv_w1", conv_w1)]:
            b.dma("sync", pv[nm][:], srcw)
        b.dma("sync", rr(A0_sb[:], "p (g n) -> p g n", g=DG), A_log0)
        b.dma("sync", rr(A1_sb[:], "p (g n) -> p g n", g=DG), A_log1)
        b.dma("sync", fb1_sb[:], fb1)
        b.dma("sync", rr(b1e_sb[:], "p (e m) -> p e m", e=NEXP), b1)
        b.dma("sync", rr(b2e_sb[:], "p (e m) -> p e m", e=NEXP), b2)
        ld0 = b.dma("sync", idx1[:], idx_own)
        b.wait_val("gpsimd", b.sems["sync"], ld0)
        b.memset(ones_sb[:], 1.0)
        it_a = b.alloc("it_a", [P, P])
        it_b = b.alloc("it_b", [P, P])
        b.iota(it_a[:], [[1, P]], 0, 0)
        b.iota(it_b[:], [[0, P]], 0, 1)
        b.memset(WrT9[:], 1.0)
        gp0 = b.sems["gpsimd"].n
        b.wait_val("sync", b.sems["gpsimd"], gp0)
        b.dma("sync", rr(WrT9[:], "p (g m) -> p g m", g=DG)[:, :, 0:NEXP],
              rr(WrT, "(g p) m -> p g m", p=P))
        b.wait("vector", "gpsimd")
        b.wait("vector", "sync")
        b.tt(tri_sb[:], it_b[:], it_a[:], ALU.is_lt)
        b.ts(pv["om_mu_r"][:], pv["mu_r"][:], -1.0, ALU.mult, 1.0, ALU.add)
        b.ts(pv["om_mu_k"][:], pv["mu_k"][:], -1.0, ALU.mult, 1.0, ALU.add)
        b.wait_val("scalar", b.sems["sync"], ld0)
        b.act(A0_sb[:], A0_sb[:], AF.Exp)
        b.act(A1_sb[:], A1_sb[:], AF.Exp)
        b.wait("vector", "scalar")
        b.ts(A0_sb[:], A0_sb[:], -1.0, ALU.mult)
        b.ts(A1_sb[:], A1_sb[:], -1.0, ALU.mult)
        b.wait("tensor", "sync")
        for g in range(DG):
            b.mm(pbx[0:NEXP, 0:1],
                 rr(WrT9[:], "p (g m) -> p g m", g=DG)[:, g, 0:NEXP],
                 ones_sb[:, 0:1], start=(g == 0), stop=(g == DG - 1),
                 inc=(g == DG - 1))
        b.wait("vector", "tensor")
        swcp = b.cpy(rows[0:NEXP, 0:1], pbx[0:NEXP, 0:1])
        b.wait_val("scalar", b.sems["vector"], swcp)
        b.dma("scalar", sw_d, rows[0:NEXP, 0:1].squeeze(1))
        b.wait("gpsimd", "scalar")
        b.dma("gpsimd", swcol[:], sw_d.unsqueeze(0).broadcast_to((P, NEXP)))
        b.memset(carry[:], 0.0)
        b.barrier()
        b.flush()

        if "R" in phases:
            xg0 = b.alloc("r_xg0", [P, 4096])
            xg1 = b.alloc("r_xg1", [P, 4096])
            xq = b.alloc("r_xq", [P, 4096])
            sm = b.alloc("r_sm", [P, 512])
            NIT = TGLOB // 512
            pe_done = [0] * NIT
            ve_done = [0] * NIT
            for i in range(NIT):
                buf = rr([xg0, xg1][i % 2][:], "p (g t) -> p g t", g=DG)
                if i >= 2:
                    b.wait_val("sync", b.sems["tensor"], pe_done[i - 2])
                ldn = b.dma("sync", buf,
                            rr(x_fm_full, "(g p) t -> p g t", p=P)[:, :, i * 512:(i + 1) * 512])
                b.wait_val("scalar", b.sems["sync"], ldn)
                if i >= 1:
                    b.wait_val("scalar", b.sems["tensor"], pe_done[i - 1])
                sq = b.act(rr(xq[:], "p (g t) -> p g t", g=DG), buf, AF.Square)
                b.wait_val("tensor", b.sems["sync"], ldn)
                b.wait_val("tensor", b.sems["scalar"], sq)
                if i >= 1:
                    b.wait_val("tensor", b.sems["vector"], ve_done[i - 1])
                for g in range(DG):
                    b.mm(pb[0][0:9, :],
                         rr(WrT9[:], "p (g m) -> p g m", g=DG)[:, g, :],
                         buf[:, g, :], start=(g == 0), stop=(g == DG - 1))
                for g in range(DG):
                    b.mm(pb[1][0:1, :], ones_sb[:, 0:1],
                         rr(xq[:], "p (g t) -> p g t", g=DG)[:, g, :],
                         start=(g == 0), stop=(g == DG - 1),
                         inc=(g == DG - 1))
                pe_done[i] = b.sems["tensor"].n
                b.wait_val("vector", b.sems["tensor"], pe_done[i])
                if i >= 1:
                    b.wait_val("vector", b.sems["qscalar"], st_done[i - 1])
                b.cpy(sm[0:9, :], pb[0][0:9, :])
                ev = b.cpy(sm[32:33, :], pb[1][0:1, :])
                ve_done[i] = ev
                b.wait_val("scalar", b.sems["vector"], ev)
                b.dma("scalar", lraw_g[0:9, i * 512:(i + 1) * 512], sm[0:9, :])
                if i == 0:
                    st_done = [0] * NIT
                st_done[i] = b.dma("scalar", lraw_g[9:10, i * 512:(i + 1) * 512],
                                   sm[32:33, :])
            b.barrier()
            b.flush()

            ga = b.alloc("r_ga", [P, NEXP * RF])
            gb = b.alloc("r_gb", [P, 12 * RF])
            gc = b.alloc("r_gc", [P, 4 * RF])
            growbig = b.alloc("r_rowbig", [P, NTOK // P])

            def gt(arena, slot):
                return arena[:, slot * RF:(slot + 1) * RF]

            mu_t, r_t, tv, m1_t, p2_t, sig_t, tmp_t, omf_t, csum_t, pos_t, tmp2_t, _ = \
                [gt(gb, i) for i in range(12)]
            le = [gt(ga, e) for e in range(NEXP)]
            wk_t, elig_t, kk_t, tmp3_t = [gt(gc, i) for i in range(4)]

            def row_g(q):
                return rr(lraw_g[q], "(p f) -> p f", p=P)

            b.dma("sync", mu_t, row_g(8))
            l2 = b.dma("sync", r_t, row_g(9))
            b.wait_val("vector", b.sems["sync"], l2)
            b.ts(mu_t, mu_t, 1.0 / D, ALU.mult)
            b.ts(tv, r_t, 1.0 / D, ALU.mult)
            b.tt(tmp_t, mu_t, mu_t, ALU.mult)
            b.tt(tv, tv, tmp_t, ALU.subtract)
            vv = b.ts(tv, tv, EPS, ALU.add)
            b.wait_val("scalar", b.sems["vector"], vv)
            b.act(tv, tv, AF.Ln)
            rrr = b.act(r_t, tv, AF.Exp, scale=-0.5)
            b.wait_val("vector", b.sems["scalar"], rrr)
            b.wait_val("scalar", b.sems["scalar"], rrr)
            st1 = b.dma("scalar", row_g(8), mu_t)
            st2 = b.dma("scalar", row_g(9), r_t)
            b.barrier()
            lraw_flat = rr(lraw_g, "a (bb cc) -> (a bb) cc", cc=32)

            def own_gather(q, dst):
                tsn = b.ts(idxq[:], idx1[:], q * (TGLOB // 32), ALU.add)
                b.wait_val("gpsimd", b.sems["vector"], tsn)
                gg = b.idma(growbig[:], None, lraw_flat,
                            IndirectOffsetOnAxis(ap=idxq[:], axis=0))
                b.wait_val("gpsimd", b.sems["qgpsimd"], gg)
                b.dma("gpsimd", rr(dst, "(p f) -> p f", p=P), growbig[:])
                b.wait_val("vector", b.sems["qgpsimd"], gg)

            for q, dst in [(8, own_stats[0]), (9, own_stats[1])]:
                own_gather(q, dst)
            for e in range(NEXP):
                ln = b.dma("sync", le[e], row_g(e))
                b.wait_val("vector", b.sems["sync"], ln)
                b.ts(tmp_t, mu_t, swcol[:, e:e + 1], ALU.mult)
                b.tt(le[e], le[e], tmp_t, ALU.subtract)
                len_ = b.tt(le[e], le[e], r_t, ALU.mult)
                b.wait_val("scalar", b.sems["vector"], len_)
                b.dma("scalar", row_g(10 + e), le[e])
            b.cpy(m1_t, le[0])
            for e in range(1, NEXP):
                b.tt(m1_t, m1_t, le[e], ALU.max)
            for e in range(NEXP):
                b.tt(tmp_t, le[e], m1_t, ALU.is_equal)
                b.stt(tmp2_t, tmp_t, -1e30, le[e], ALU.mult, ALU.add)
                if e == 0:
                    b.cpy(p2_t, tmp2_t)
                else:
                    b.tt(p2_t, p2_t, tmp2_t, ALU.max)
            sb1 = b.tt(tmp_t, p2_t, m1_t, ALU.subtract)
            b.wait_val("scalar", b.sems["vector"], sb1)
            ex = b.act(tmp_t, tmp_t, AF.Exp)
            b.wait_val("vector", b.sems["scalar"], ex)
            b.ts(tmp_t, tmp_t, 1.0, ALU.add)
            b.recip(sig_t, tmp_t)
            b.memset(omf_t, 1.0, eng="vector")
            stw_prev = [0]
            for e in range(NEXP):
                s2 = b.tt(tmp2_t, le[e], m1_t, ALU.subtract)
                b.wait_val("scalar", b.sems["vector"], s2)
                w2c = b.act(tmp2_t, tmp2_t, AF.Exp)
                b.wait_val("vector", b.sems["scalar"], w2c)
                b.wait_val("vector", b.sems["qscalar"], stw_prev[0])
                b.tt(wk_t, tmp2_t, sig_t, ALU.mult)
                b.tt(tmp3_t, le[e], p2_t, ALU.is_ge)
                b.tt(wk_t, wk_t, tmp3_t, ALU.mult)
                b.tt(elig_t, tmp3_t, omf_t, ALU.mult)
                b.scan(csum_t, elig_t, elig_t, 0.0, ALU.add, ALU.bypass)
                vcp = b.cpy(rows[:, 32:33], csum_t[:, RF - 1:RF])
                b.wait_val("tensor", b.sems["vector"], vcp)
                mmn = b.mm(pbx[:, 0:1], tri_sb[:], rows[:, 32:33],
                           start=True, stop=True, inc=True)
                b.wait_val("vector", b.sems["tensor"], mmn)
                b.cpy(rows[:, 33:34], pbx[:, 0:1])
                b.ts(pos_t, csum_t, rows[:, 33:34], ALU.add)
                b.ts(kk_t, pos_t, float(CAP), ALU.is_le)
                b.tt(kk_t, kk_t, elig_t, ALU.mult)
                b.tt(tmp3_t, elig_t, kk_t, ALU.subtract)
                b.tt(omf_t, omf_t, tmp3_t, ALU.subtract)
                wkn = b.tt(wk_t, wk_t, kk_t, ALU.mult)
                b.wait_val("scalar", b.sems["vector"], wkn)
                stw = b.dma("scalar", row_g(e), wk_t)
                stw_prev[0] = stw
            fbn = b.ts(omf_t, omf_t, -1.0, ALU.mult, 1.0, ALU.add)
            b.wait_val("scalar", b.sems["vector"], fbn)
            b.dma("scalar", row_g(8), omf_t)
            b.barrier()
            for e in range(NEXP):
                own_gather(e, own_rq[e])
            own_gather(8, own_rq[NEXP])
            b.barrier()
            b.flush()

        rb_guard = [0]
        bc_guard = [0]
        bcd_guard = [0]

        _SLOTMAP = {0: (0, 0), 1: (32, 0), 2: (64, 0), 3: (0, 1), 4: (0, 2),
                    5: (64, 1), 6: (32, 2)}

        def _slot(t_sm, i, T):
            base, cb = _SLOTMAP[i]
            return t_sm[base:base + 1, cb * T:(cb + 1) * T]

        def ln_rows(src_g, sq_g, t_sm, T, pbA, pbB):
            S = [_slot(t_sm, i, T) for i in range(7)]
            sqn = b.act(sq_g, src_g, AF.Square)
            b.wait_val("tensor", b.sems["scalar"], sqn)
            b.wait("tensor", "vector")
            for g in range(DG):
                b.mm(pbA[0:1, 0:T], ones_sb[:, 0:1], src_g[:, g, :],
                     start=(g == 0), stop=(g == DG - 1))
            for g in range(DG):
                b.mm(pbB[0:1, 0:T], ones_sb[:, 0:1], sq_g[:, g, :],
                     start=(g == 0), stop=(g == DG - 1), inc=(g == DG - 1))
            mmn = b.sems["tensor"].n
            b.wait_val("vector", b.sems["tensor"], mmn)
            b.cpy(S[0], pbA[0:1, 0:T])
            b.cpy(S[1], pbB[0:1, 0:T])
            b.ts(S[2], S[0], 1.0 / D, ALU.mult)
            b.ts(S[3], S[1], 1.0 / D, ALU.mult)
            b.tt(S[4], S[2], S[2], ALU.mult)
            b.tt(S[4], S[3], S[4], ALU.subtract)
            vn = b.ts(S[4], S[4], EPS, ALU.add)
            b.wait_val("scalar", b.sems["vector"], vn)
            b.act(S[4], S[4], AF.Ln)
            en = b.act(S[5], S[4], AF.Exp, scale=-0.5)
            b.wait_val("vector", b.sems["scalar"], en)
            b.tt(S[6], S[2], S[5], ALU.mult)
            return b.ts(S[6], S[6], -1.0, ALU.mult)

        def repl_rows(t_sm, murch, T, vec_count):
            b.wait_val("scalar", b.sems["vector"], vec_count)
            b.wait_val("scalar", b.sems["scalar"], b.sems["scalar"].n)
            b.wait_val("scalar", b.sems["qgpsimd"], rb_guard[0])
            b.dma("scalar", rowb_d[0:1, 0:T], _slot(t_sm, 5, T))
            stn = b.dma("scalar", rowb_d[1:2, 0:T], _slot(t_sm, 6, T))
            b.wait_val("gpsimd", b.sems["qscalar"], stn)
            gn = b.dma("gpsimd", murch[:, 0:2, 0:T],
                       rowb_d[:, 0:T].unsqueeze(0).broadcast_to((P, 2, T)))
            rb_guard[0] = gn
            return gn

        def ln_apply(src_g, dst_g, murch, T, gn, g_keys=None, out_bf=None):
            b.wait_val("vector", b.sems["qgpsimd"], gn)
            r_b = murch[:, 0:1, 0:T].broadcast_to((P, DG, T))
            nm_b = murch[:, 1:2, 0:T].broadcast_to((P, DG, T))
            b.tt(dst_g, src_g, r_b, ALU.mult)
            vn = b.tt(dst_g, dst_g, nm_b, ALU.add)
            if g_keys is None:
                return vn
            gk, bk = g_keys
            b.wait_val("scalar", b.sems["vector"], vn)
            for g in range(DG):
                sn = b.act(out_bf[:, g, :], dst_g[:, g, :], AF.Identity,
                           bias=pv[bk][:, g:g + 1], scale=pv[gk][:, g:g + 1])
            return sn

        if "A1" in phases:
            wIN = b.alloc("wIN", [P, DG * 16 * P], BF16)
            wXP = b.alloc("wXP", [P, DG * 96], BF16)
            wDT = b.alloc("wDT", [DTRANK, DG * P], BF16)
            wOUT = b.alloc("wOUT", [P, DG * DG * P], BF16)
            t_x = b.alloc("t_x", [P, 4096])
            t_u = b.alloc("t_u", [P, 4096])
            t_v = b.alloc("t_v", [P, 4096])
            t_w = b.alloc("t_w", [P, 4096])
            t_h = b.alloc("t_h", [P, 2048])
            t_y = b.alloc("t_y", [P, 4096])
            t_d = b.alloc("t_d", [P, 4096])
            prev_scans = [0]
            t_lb = b.alloc("t_lb", [P, 4096], BF16)
            t_cb = b.alloc("t_cb", [P, 4096], BF16)
            dtb = b.alloc("dtb", [DTRANK, 512], BF16)
            t_sm = b.alloc("t_sm", [P, 3 * 512])
            murch = b.alloc("murch", [P, 2 * 512])
            lw1 = b.dma("gpsimd", rr(wIN[:], "p (g m q) -> p g m q", g=DG, m=16),
                        rr(W_inT, "(g p) (m q) -> p g m q", p=P, q=P))
            lw2 = b.dma("gpsimd", rr(wXP[:], "p (g m) -> p g m", g=DG),
                        rr(W_xpT, "(g p) m -> p g m", p=P))
            lw3 = b.dma("gpsimd", rr(wDT[:], "k (g q) -> k g q", g=DG),
                        rr(W_dtT, "k (g q) -> k g q", q=P))
            lw4 = b.dma("gpsimd", rr(wOUT[:], "p (g m q) -> p g m q", g=DG, m=DG),
                        rr(W_outT, "(g p) (m q) -> p g m q", p=P, q=P))
            b.wait_val("tensor", b.sems["qgpsimd"], lw4)

            for k in range(NCH):
                sl = slice(k * LC, (k + 1) * LC)
                first = (k % (SEQ // LC) == 0)
                xv = rr(t_x[:], "p (g t) -> p g t", g=DG)
                uv = rr(t_u[:], "p (g t) -> p g t", g=DG)
                vv_ = rr(t_v[:], "p (g t) -> p g t", g=DG)
                wv = rr(t_w[:], "p (g t) -> p g t", g=DG)
                yv = rr(t_y[:], "p (g t) -> p g t", g=DG)
                lbv = rr(t_lb[:], "p (g t) -> p g t", g=DG)
                cbv = rr(t_cb[:], "p (g t) -> p g t", g=DG)
                mv = rr(murch[:], "p (r t) -> p r t", r=2)
                if first:
                    b.memset(hcar[:], 0.0)
                    b.memset(carry[:, 0:DG], 0.0, eng="vector")
                ldx = b.dma("sync", xv, rr(x_fm, "(g p) t -> p g t", p=P)[:, :, sl])
                ldm = b.dma("gpsimd", mv,
                            own_stats[:, sl].unsqueeze(0).broadcast_to((P, 2, LC)))
                b.wait_val("vector", b.sems["sync"], ldx)
                b.wait_val("vector", b.sems["qgpsimd"], ldm)
                mu_b = mv[:, 0:1, :].broadcast_to((P, DG, LC))
                r_b = mv[:, 1:2, :].broadcast_to((P, DG, LC))
                b.tt(uv, xv, mu_b, ALU.subtract)
                xnn = b.tt(uv, uv, r_b, ALU.mult)
                b.wait_val("scalar", b.sems["vector"], xnn)
                xnst = b.dma("scalar", rr(xn_fm_d, "(g p) t -> p g t", p=P)[:, :, sl], uv)
                b.wait_val("gpsimd", b.sems["vector"], xnn)
                xnbst = b.dma("gpsimd", rr(xnbf_fm_d, "(g p) t -> p g t", p=P)[:, :, sl], uv)
                lr = ln_rows(uv, xv, t_sm, LC, pb[2], pb[3])
                gn = repl_rows(t_sm, mv, LC, lr)
                lnb = ln_apply(uv, xv, mv, LC, gn, ("n1_g", "n1_b"), lbv)
                b.wait_val("tensor", b.sems["scalar"], lnb)
                ev_prev = [0, 0]
                for m in range(16):
                    bank = pb[m % 2]
                    if ev_prev[m % 2]:
                        b.wait_val("tensor", b.sems["scalar"], ev_prev[m % 2])
                    for g in range(DG):
                        b.mm(bank[:, :], rr(wIN[:], "p (g m q) -> p g m q",
                                            g=DG, m=16)[:, g, m, :],
                             lbv[:, g, :], start=(g == 0), stop=(g == DG - 1),
                             inc=(g == DG - 1))
                    mmn = b.sems["tensor"].n
                    b.wait_val("scalar", b.sems["tensor"], mmn)
                    if m < DG:
                        ev_prev[m % 2] = b.act(vv_[:, m, :], bank[:, :], AF.Identity)
                    else:
                        ev_prev[m % 2] = b.act(wv[:, m - DG, :], bank[:, :], AF.Silu)
                zs_done = ev_prev
                b.wait_val("scalar", b.sems["scalar"], max(zs_done))
                zst = b.dma("scalar", rr(zs_d, "(g p) t -> p g t", p=P)[:, :, sl], wv)
                b.wait_val("scalar", b.sems["qscalar"], xnst)
                b.wait_val("scalar", b.sems["qgpsimd"], xnbst)
                cwaits = max(zs_done)
                b.wait_val("vector", b.sems["scalar"], cwaits)
                for g in range(DG):
                    p1n = b.act(t_sm[:, 0:512], vv_[:, g, :], AF.Identity,
                                bias=pv["conv_b"][:, g:g + 1],
                                scale=pv["conv_w1"][:, g:g + 1])
                    b.wait_val("vector", b.sems["scalar"], p1n)
                    b.cpy(t_d[:, 1:512], vv_[:, g, 0:511])
                    b.cpy(t_d[:, 0:1], carry[:, g:g + 1])
                    b.cpy(carry[:, g:g + 1], vv_[:, g, 511:512])
                    b.stt(t_d[:, 512:1024], t_d[:, 0:512], pv["conv_w0"][:, g:g + 1],
                          t_sm[:, 0:512], ALU.mult, ALU.add)
                    vn = b.sems["vector"].n
                    b.wait_val("scalar", b.sems["vector"], vn)
                    xcn = b.act(uv[:, g, :], t_d[:, 512:1024], AF.Silu)
                    b.wait_val("vector", b.sems["scalar"], xcn)
                    b.cpy(cbv[:, g, :], uv[:, g, :])
                xc_done_v = b.sems["vector"].n
                xc_done_s = b.sems["scalar"].n
                if "xc" in dbg:
                    b.wait_val("scalar", b.sems["vector"], xc_done_v)
                    b.dma("scalar", rr(xc_d, "(g p) t -> p g t", p=P)[:, :, sl], uv)
                b.wait_val("tensor", b.sems["vector"], xc_done_v)
                for g in range(DG):
                    b.mm(pb[4][0:DTRANK, :], rr(wXP[:], "p (g m) -> p g m",
                                                g=DG)[:, g, 0:DTRANK],
                         cbv[:, g, :], start=(g == 0), stop=(g == DG - 1))
                for g in range(DG):
                    b.mm(pb[5][0:32, :], rr(wXP[:], "p (g m) -> p g m",
                                            g=DG)[:, g, DTRANK:96],
                         cbv[:, g, :], start=(g == 0), stop=(g == DG - 1),
                         inc=(g == DG - 1))
                xpn = b.sems["tensor"].n
                b.wait_val("scalar", b.sems["tensor"], xpn)
                dtev = b.act(dtb[:, :], pb[4][0:DTRANK, :], AF.Identity)
                bcev = b.act(t_sm[0:32, 0:512], pb[5][0:32, :], AF.Identity)
                b.wait_val("scalar", b.sems["qgpsimd"], bcd_guard[0])
                b.wait_val("scalar", b.sems["scalar"], bcev)
                bcst = b.dma("scalar", bc_d[:, 0:LC], t_sm[0:32, 0:512])
                b.wait_val("scalar", b.sems["qscalar"], bcst)
                b.wait_val("tensor", b.sems["scalar"], dtev)
                for g in range(DG):
                    bank = pb[6] if g % 2 == 0 else pb[4]
                    if g >= 2:
                        b.wait_val("tensor", b.sems["vector"], dproj_done[g % 2])
                    b.mm(bank[:, :], rr(wDT[:], "k (g q) -> k g q", g=DG)[:, g, :],
                         dtb[:, :], start=True, stop=True, inc=True)
                    mmn = b.sems["tensor"].n
                    b.wait_val("scalar", b.sems["tensor"], mmn)
                    if g >= 1:
                        b.wait_val("scalar", b.sems["vector"], dproj_done[(g + 1) % 2])
                    b.act(t_sm[:, 0:512], bank[:, :], AF.Identity,
                          bias=pv["dt_b"][:, g:g + 1])
                    b.act(t_d[:, 0:512], t_sm[:, 0:512], AF.Abs)
                    b.act(t_d[:, 0:512], t_d[:, 0:512], AF.Exp, scale=-1.0)
                    spn = b.act(t_d[:, 0:512], t_d[:, 0:512], AF.Ln, bias=1.0)
                    b.wait_val("vector", b.sems["scalar"], spn)
                    b.ts(t_d[:, 512:1024], t_sm[:, 0:512], 0.0, ALU.max)
                    dn = b.tt(xv[:, g, :], t_d[:, 512:1024], t_d[:, 0:512], ALU.add)
                    if g == 0:
                        dproj_done = [0, 0]
                    dproj_done[g % 2] = dn
                delta_done = b.sems["vector"].n
                if "delta" in dbg:
                    b.wait_val("scalar", b.sems["vector"], delta_done)
                    b.dma("scalar", rr(delta_d, "(g p) t -> p g t", p=P)[:, :, sl], xv)
                dxn = b.tt(t_v[:], t_x[:], t_u[:], ALU.mult)
                prev_scans[0] = dxn
                b.wait_val("gpsimd", b.sems["qscalar"], bcst)
                for q4 in range(4):
                    ns = q4 * 4
                    b.wait_val("gpsimd", b.sems["vector"], bc_guard[0])
                    rbb = b.dma("gpsimd", rr(t_w[:, 0:2048], "p (j t) -> p j t", j=4),
                                bc_d[ns:ns + 4, 0:LC].unsqueeze(0).broadcast_to((P, 4, LC)))
                    rcc = b.dma("gpsimd", rr(t_w[:, 2048:4096], "p (j t) -> p j t", j=4),
                                bc_d[16 + ns:16 + ns + 4, 0:LC].unsqueeze(0).broadcast_to((P, 4, LC)))
                    bcd_guard[0] = rcc
                    b.wait_val("vector", b.sems["qgpsimd"], rcc)
                    if "tw" in dbg and k == 0 and q4 == 0:
                        b.dma("gpsimd", tw_d, t_w[:])
                        b.wait_val("vector", b.sems["qgpsimd"], b.sems["qgpsimd"].n)
                    h8 = rr(t_h[:], "p (j t) -> p j t", j=4)
                    seb = rr(t_d[:, 0:2048], "p (j t) -> p j t", j=4)
                    for g in range(DG):
                        b.wait_val("scalar", b.sems["vector"], prev_scans[0])
                        expn = 0
                        for j in range(4):
                            n = ns + j
                            expn = b.act(seb[:, j, :], xv[:, g, :], AF.Exp,
                                         scale=A0_sb[:, g * NSTATE + n:g * NSTATE + n + 1])
                        for j in range(4):
                            n = ns + j
                            sa = t_d[:, 2048 + (j % 2) * 512:2560 + (j % 2) * 512]
                            b.tt(sa, t_v[:].rearrange("p (g t) -> p g t", g=DG)[:, g, :],
                                 rr(t_w[:, 0:2048], "p (j t) -> p j t", j=4)[:, j, :],
                                 ALU.mult)
                            if j == 0:
                                b.wait_val("vector", b.sems["scalar"], expn)
                            b.scan(h8[:, j, :], seb[:, j, :], sa,
                                   hcar[:, g * NSTATE + n:g * NSTATE + n + 1])
                        if "th" in dbg and k == 0 and q4 == 0 and g == 0:
                            hn = b.sems["vector"].n
                            b.wait_val("scalar", b.sems["vector"], hn)
                            b.dma("scalar", td_d, t_d[:])
                            b.dma("scalar", th_d, t_h[:])
                            b.wait_val("vector", b.sems["qscalar"], b.sems["qscalar"].n)
                        b.cpy(hcar[:, g * NSTATE + ns:g * NSTATE + ns + 4].unsqueeze(2),
                              h8[:, :, 511:512])
                        b.tt(t_h[:], t_h[:], t_w[:, 2048:4096], ALU.mult)
                        if q4 == 0:
                            b.red(yv[:, g, :], rr(t_h[:], "p (n t) -> p t n", n=4))
                        else:
                            b.red(t_d[:, 3072:3584], rr(t_h[:], "p (n t) -> p t n", n=4))
                            b.tt(yv[:, g, :], yv[:, g, :], t_d[:, 3072:3584], ALU.add)
                        prev_scans[0] = b.sems["vector"].n
                    bc_guard[0] = b.sems["vector"].n
                if "ysc" in dbg:
                    b.wait_val("scalar", b.sems["vector"], bc_guard[0])
                    b.dma("scalar", rr(ysc_d, "(g p) t -> p g t", p=P)[:, :, sl], yv)
                b.wait_val("sync", b.sems["vector"], bc_guard[0])
                b.wait_val("sync", b.sems["qscalar"], zst)
                zsl = b.dma("sync", vv_, rr(zs_d, "(g p) t -> p g t", p=P)[:, :, sl])
                b.wait_val("vector", b.sems["sync"], zsl)
                for g in range(DG):
                    b.stt(uv[:, g, :], uv[:, g, :], pv["Dp0"][:, g:g + 1],
                          yv[:, g, :], ALU.mult, ALU.add)
                    b.tt(cbv[:, g, :], uv[:, g, :], vv_[:, g, :], ALU.mult)
                yg_done = b.sems["vector"].n
                if "ymix" in dbg:
                    b.wait_val("scalar", b.sems["vector"], yg_done)
                    b.dma("scalar", rr(ymix_d, "(g p) t -> p g t", p=P)[:, :, sl], uv)
                b.wait_val("tensor", b.sems["vector"], yg_done)
                ev_prev = [0, 0]
                for m in range(DG):
                    bank = pb[m % 2]
                    if ev_prev[m % 2]:
                        b.wait_val("tensor", b.sems["scalar"], ev_prev[m % 2])
                    for gk in range(DG):
                        b.mm(bank[:, :], rr(wOUT[:], "p (g m q) -> p g m q",
                                            g=DG, m=DG)[:, gk, m, :],
                             cbv[:, gk, :], start=(gk == 0), stop=(gk == DG - 1),
                             inc=(gk == DG - 1))
                    mmn = b.sems["tensor"].n
                    b.wait_val("scalar", b.sems["tensor"], mmn)
                    ev_prev[m % 2] = b.act(xv[:, m, :], bank[:, :], AF.Identity)
                hm_done = max(ev_prev)
                if "hm" in dbg:
                    b.wait_val("scalar", b.sems["scalar"], hm_done)
                    b.dma("scalar", rr(hm_d, "(g p) t -> p g t", p=P)[:, :, sl], xv)
                b.wait_val("vector", b.sems["scalar"], hm_done)
                lr = ln_rows(xv, vv_, t_sm, LC, pb[2], pb[3])
                gn = repl_rows(t_sm, mv, LC, lr)
                lhn = ln_apply(xv, vv_, mv, LC, gn, None, None)
                b.wait_val("scalar", b.sems["vector"], lhn)
                for g in range(DG):
                    b.act(vv_[:, g, :], vv_[:, g, :], AF.Identity,
                          bias=pv["mnorm_b"][:, g:g + 1],
                          scale=pv["mnorm_g"][:, g:g + 1])
                mnn = b.sems["scalar"].n
                b.wait_val("sync", b.sems["qscalar"], xnst)
                b.wait_val("sync", b.sems["vector"], yg_done)
                b.wait_val("sync", b.sems["qscalar"], b.sems["qscalar"].n)
                xnl = b.dma("sync", uv, rr(xn_fm_d, "(g p) t -> p g t", p=P)[:, :, sl])
                b.wait_val("vector", b.sems["sync"], xnl)
                b.wait_val("vector", b.sems["scalar"], mnn)
                s2n = b.tt(t_u[:], t_u[:], t_v[:], ALU.add)
                b.wait_val("scalar", b.sems["vector"], s2n)
                b.dma("scalar", rr(src2_d, "(g p) t -> p g t", p=P)[:, :, sl], uv)
                b.barrier()
            b.flush()


        if "A2a" in phases:
            LC2 = 256
            wf1 = b.alloc("wf1", [P, DG * 32 * P], BF16)
            wf2 = b.alloc("wf2", [P, 32 * DG * P], BF16)
            t_a = b.alloc("a2_a", [P, DG * LC2])
            t_q = b.alloc("a2_q", [P, DG * LC2])
            t_nb = b.alloc("a2_nb", [P, DG * LC2], BF16)
            t_fb = b.alloc("a2_fb", [P, 32 * LC2], BF16)
            t_e = b.alloc("a2_e", [P, LC2])
            t_sm = b.alloc("a2_sm", [P, 3 * LC2])
            murch = b.alloc("a2_mu", [P, 2 * LC2])
            lw1 = b.dma("gpsimd", rr(wf1[:], "p (g m q) -> p g m q", g=DG, m=32),
                        rr(W_f1T, "(g p) (m q) -> p g m q", p=P, q=P))
            lw2 = b.dma("gpsimd", rr(wf2[:], "p (kk m q) -> p kk m q", kk=32, m=DG),
                        rr(W_f2T, "(kk p) (m q) -> p kk m q", p=P, q=P))
            b.wait_val("tensor", b.sems["qgpsimd"], lw2)
            wf1v = rr(wf1[:], "p (g m q) -> p g m q", g=DG, m=32)
            wf2v = rr(wf2[:], "p (kk m q) -> p kk m q", kk=32, m=DG)
            for k in range(NTOK // LC2):
                sl = slice(k * LC2, (k + 1) * LC2)
                av = rr(t_a[:], "p (g t) -> p g t", g=DG)
                qv = rr(t_q[:], "p (g t) -> p g t", g=DG)
                nv = rr(t_nb[:], "p (g t) -> p g t", g=DG)
                fv = rr(t_fb[:], "p (m t) -> p m t", m=32)
                mv = rr(murch[:], "p (r t) -> p r t", r=2)
                ldx = b.dma("sync", av, rr(src2_d, "(g p) t -> p g t", p=P)[:, :, sl])
                b.wait_val("scalar", b.sems["sync"], ldx)
                b.wait_val("vector", b.sems["sync"], ldx)
                lr = ln_rows(av, qv, t_sm, LC2, pb[2], pb[3])
                gn = repl_rows(t_sm, mv, LC2, lr)
                lnb = ln_apply(av, qv, mv, LC2, gn, ("n2_g", "n2_b"), nv)
                b.wait_val("tensor", b.sems["scalar"], lnb)
                ev_prev = [0, 0]
                for m in range(32):
                    bank = pb[m % 2]
                    if ev_prev[m % 2]:
                        b.wait_val("tensor", b.sems["vector"], ev_prev[m % 2])
                    for g in range(DG):
                        b.mm(bank[:, 0:LC2], wf1v[:, g, m, :], nv[:, g, :],
                             start=(g == 0), stop=(g == DG - 1), inc=(g == DG - 1))
                    mmn = b.sems["tensor"].n
                    b.wait_val("scalar", b.sems["tensor"], mmn)
                    if max(ev_prev):
                        b.wait_val("scalar", b.sems["vector"], max(ev_prev))
                    een = b.act(t_e[:, :], bank[:, 0:LC2], AF.Identity)
                    b.wait_val("vector", b.sems["scalar"], een)
                    ev_prev[m % 2] = b.stt(fv[:, m, :], t_e[:, :], 0.01, t_e[:, :],
                                           ALU.mult, ALU.max)
                f_done = b.sems["vector"].n
                b.wait_val("tensor", b.sems["vector"], f_done)
                ev_prev = [0, 0]
                for m in range(DG):
                    bank = pb[m % 2]
                    if ev_prev[m % 2]:
                        b.wait_val("tensor", b.sems["vector"], ev_prev[m % 2])
                    for kk in range(32):
                        b.mm(bank[:, 0:LC2], wf2v[:, kk, m, :], fv[:, kk, :],
                             start=(kk == 0), stop=(kk == 31), inc=(kk == 31))
                    mmn = b.sems["tensor"].n
                    b.wait_val("vector", b.sems["tensor"], mmn)
                    ev_prev[m % 2] = b.tt(av[:, m, :], bank[:, 0:LC2], av[:, m, :],
                                          ALU.add)
                s3n = b.sems["vector"].n
                b.wait_val("scalar", b.sems["vector"], s3n)
                b.dma("scalar", rr(src3_d, "(g p) t -> p g t", p=P)[:, :, sl], av)
                b.barrier()
            b.flush()

        if "A2b" in phases:
            wfe = b.alloc("wfe", [P, DG * DG * P], BF16)
            t_a = b.alloc("b2_a", [P, DG * LC])
            t_q = b.alloc("b2_q", [P, DG * LC])
            t_nb = b.alloc("b2_nb", [P, DG * LC], BF16)
            t_e = b.alloc("b2_e", [P, LC])
            t_sm = b.alloc("b2_sm", [P, 3 * LC])
            murch = b.alloc("b2_mu", [P, 2 * LC])
            lw = b.dma("gpsimd", rr(wfe[:], "p (g m q) -> p g m q", g=DG, m=DG),
                       rr(W_feT, "(g p) (m q) -> p g m q", p=P, q=P))
            b.wait_val("tensor", b.sems["qgpsimd"], lw)
            wfev = rr(wfe[:], "p (g m q) -> p g m q", g=DG, m=DG)
            for k in range(NCH):
                sl = slice(k * LC, (k + 1) * LC)
                av = rr(t_a[:], "p (g t) -> p g t", g=DG)
                qv = rr(t_q[:], "p (g t) -> p g t", g=DG)
                nv = rr(t_nb[:], "p (g t) -> p g t", g=DG)
                mv = rr(murch[:], "p (r t) -> p r t", r=2)
                ldx = b.dma("sync", av, rr(src3_d, "(g p) t -> p g t", p=P)[:, :, sl])
                b.wait_val("scalar", b.sems["sync"], ldx)
                b.wait_val("vector", b.sems["sync"], ldx)
                lr = ln_rows(av, qv, t_sm, LC, pb[2], pb[3])
                gn = repl_rows(t_sm, mv, LC, lr)
                lnb = ln_apply(av, qv, mv, LC, gn, ("n3_g", "n3_b"), nv)
                b.wait_val("tensor", b.sems["scalar"], lnb)
                ev_prev = [0, 0]
                for m in range(DG):
                    bank = pb[m % 2]
                    if ev_prev[m % 2]:
                        b.wait_val("tensor", b.sems["vector"], ev_prev[m % 2])
                    for g in range(DG):
                        b.mm(bank[:, :], wfev[:, g, m, :], nv[:, g, :],
                             start=(g == 0), stop=(g == DG - 1), inc=(g == DG - 1))
                    mmn = b.sems["tensor"].n
                    b.wait_val("scalar", b.sems["tensor"], mmn)
                    if max(ev_prev):
                        b.wait_val("scalar", b.sems["vector"], max(ev_prev))
                    een = b.act(t_e[:, :], bank[:, :], AF.Identity,
                                bias=pv["fe_b"][:, m:m + 1])
                    b.wait_val("vector", b.sems["scalar"], een)
                    ev_prev[m % 2] = b.tt(av[:, m, :], t_e[:, :], av[:, m, :],
                                          ALU.add)
                sn = b.sems["vector"].n
                b.wait_val("scalar", b.sems["vector"], sn)
                b.dma("scalar", rr(s_attn_d, "(g p) t -> p g t", p=P)[:, :, sl], av)
                b.barrier()
            b.flush()

        if "A3" in phases:
            LC2 = 256
            wf1 = b.alloc("f3_w1", [P, DG * 32 * P], BF16)
            wf2 = b.alloc("f3_w2", [P, 32 * DG * P], BF16)
            t_a = b.alloc("f3_a", [P, DG * LC2])
            t_nb = b.alloc("f3_nb", [P, DG * LC2], BF16)
            t_fb = b.alloc("f3_fb", [P, 32 * LC2], BF16)
            lw1 = b.dma("gpsimd", rr(wf1[:], "p (g m q) -> p g m q", g=DG, m=32),
                        rr(fw1, "(g p) (m q) -> p g m q", p=P, q=P))
            lw2 = b.dma("gpsimd", rr(wf2[:], "p (kk m q) -> p kk m q", kk=32, m=DG),
                        rr(fw2, "(kk p) (m q) -> p kk m q", p=P, q=P))
            b.wait_val("tensor", b.sems["qgpsimd"], lw2)
            wf1v = rr(wf1[:], "p (g m q) -> p g m q", g=DG, m=32)
            wf2v = rr(wf2[:], "p (kk m q) -> p kk m q", kk=32, m=DG)
            for k in range(NTOK // LC2):
                sl = slice(k * LC2, (k + 1) * LC2)
                av = rr(t_a[:], "p (g t) -> p g t", g=DG)
                nv = rr(t_nb[:], "p (g t) -> p g t", g=DG)
                fv = rr(t_fb[:], "p (m t) -> p m t", m=32)
                ldx = b.dma("sync", nv,
                            rr(xnbf_fm_d, "(g p) t -> p g t", p=P)[:, :, sl])
                b.wait_val("tensor", b.sems["sync"], ldx)
                ev_prev = [0, 0]
                for m in range(32):
                    bank = pb[m % 2]
                    if ev_prev[m % 2]:
                        b.wait_val("tensor", b.sems["scalar"], ev_prev[m % 2])
                    for g in range(DG):
                        b.mm(bank[:, 0:LC2], wf1v[:, g, m, :], nv[:, g, :],
                             start=(g == 0), stop=(g == DG - 1), inc=(g == DG - 1))
                    mmn = b.sems["tensor"].n
                    b.wait_val("scalar", b.sems["tensor"], mmn)
                    ev_prev[m % 2] = b.act(fv[:, m, :], bank[:, 0:LC2], AF.Gelu,
                                           bias=fb1_sb[:, m:m + 1])
                f_done = b.sems["scalar"].n
                b.wait_val("tensor", b.sems["scalar"], f_done)
                ev_prev = [0, 0]
                for m in range(DG):
                    bank = pb[m % 2]
                    if ev_prev[m % 2]:
                        b.wait_val("tensor", b.sems["scalar"], ev_prev[m % 2])
                    for kk in range(32):
                        b.mm(bank[:, 0:LC2], wf2v[:, kk, m, :], fv[:, kk, :],
                             start=(kk == 0), stop=(kk == 31), inc=(kk == 31))
                    mmn = b.sems["tensor"].n
                    b.wait_val("scalar", b.sems["tensor"], mmn)
                    ev_prev[m % 2] = b.act(av[:, m, :], bank[:, 0:LC2], AF.Identity,
                                           bias=pv["fb2"][:, m:m + 1])
                yn = b.sems["scalar"].n
                b.wait_val("scalar", b.sems["scalar"], yn)
                b.dma("scalar", rr(yfb_d, "(g p) t -> p g t", p=P)[:, :, sl], av)
                b.barrier()
            b.flush()

        if "M" in phases:
            LC2 = 256
            wf1 = b.alloc("m_w1", [P, DG * 32 * P], BF16)
            wf2 = b.alloc("m_w2", [P, 32 * DG * P], BF16)
            t_a = b.alloc("m_a", [P, DG * LC2])
            t_b = b.alloc("m_b", [P, DG * LC2])
            t_nb = b.alloc("m_nb", [P, DG * LC2], BF16)
            t_fb = b.alloc("m_fb", [P, 32 * LC2], BF16)
            wkr = b.alloc("m_wkr", [P, LC2])
            wf1v = rr(wf1[:], "p (g m q) -> p g m q", g=DG, m=32)
            wf2v = rr(wf2[:], "p (kk m q) -> p kk m q", kk=32, m=DG)
            for e in range(NEXP):
                b.wait("gpsimd", "tensor")
                lw1 = b.dma("gpsimd", wf1v, rr(W1[e], "(g p) (m q) -> p g m q",
                                               p=P, q=P))
                lw2 = b.dma("gpsimd", wf2v, rr(W2[e], "(kk p) (m q) -> p kk m q",
                                               p=P, q=P))
                b.wait_val("tensor", b.sems["qgpsimd"], lw2)
                for k in range(NTOK // LC2):
                    sl = slice(k * LC2, (k + 1) * LC2)
                    av = rr(t_a[:], "p (g t) -> p g t", g=DG)
                    bv2 = rr(t_b[:], "p (g t) -> p g t", g=DG)
                    nv = rr(t_nb[:], "p (g t) -> p g t", g=DG)
                    fv = rr(t_fb[:], "p (m t) -> p m t", m=32)
                    ldx = b.dma("sync", nv,
                                rr(xnbf_fm_d, "(g p) t -> p g t", p=P)[:, :, sl])
                    lwk = b.dma("gpsimd", wkr[:],
                                own_rq[e, sl].unsqueeze(0).broadcast_to((P, LC2)))
                    if e > 0:
                        ldg = b.dma("sync", bv2,
                                    rr(gmoe_d, "(g p) t -> p g t", p=P)[:, :, sl])
                    b.wait_val("tensor", b.sems["sync"], ldx)
                    ev_prev = [0, 0]
                    for m in range(32):
                        bank = pb[m % 2]
                        if ev_prev[m % 2]:
                            b.wait_val("tensor", b.sems["scalar"], ev_prev[m % 2])
                        for g in range(DG):
                            b.mm(bank[:, 0:LC2], wf1v[:, g, m, :], nv[:, g, :],
                                 start=(g == 0), stop=(g == DG - 1),
                                 inc=(g == DG - 1))
                        mmn = b.sems["tensor"].n
                        b.wait_val("scalar", b.sems["tensor"], mmn)
                        ev_prev[m % 2] = b.act(fv[:, m, :], bank[:, 0:LC2], AF.Gelu,
                                               bias=b1e_sb[:, e * 32 + m:e * 32 + m + 1])
                    f_done = b.sems["scalar"].n
                    b.wait_val("tensor", b.sems["scalar"], f_done)
                    ev_prev = [0, 0]
                    for m in range(DG):
                        bank = pb[m % 2]
                        if ev_prev[m % 2]:
                            b.wait_val("tensor", b.sems["scalar"], ev_prev[m % 2])
                        for kk in range(32):
                            b.mm(bank[:, 0:LC2], wf2v[:, kk, m, :], fv[:, kk, :],
                                 start=(kk == 0), stop=(kk == 31), inc=(kk == 31))
                        mmn = b.sems["tensor"].n
                        b.wait_val("scalar", b.sems["tensor"], mmn)
                        ev_prev[m % 2] = b.act(av[:, m, :], bank[:, 0:LC2],
                                               AF.Identity,
                                               bias=b2e_sb[:, e * 8 + m:e * 8 + m + 1])
                    yn = b.sems["scalar"].n
                    b.wait_val("vector", b.sems["scalar"], yn)
                    b.wait_val("vector", b.sems["qgpsimd"], lwk)
                    wkb = wkr[:].unsqueeze(1).broadcast_to((P, DG, LC2))
                    if e == 0:
                        gm = b.tt(bv2, av, wkb, ALU.mult)
                    else:
                        b.wait_val("vector", b.sems["sync"], ldg)
                        b.tt(av, av, wkb, ALU.mult)
                        gm = b.tt(bv2, bv2, av, ALU.add)
                    b.wait_val("scalar", b.sems["vector"], gm)
                    b.dma("scalar", rr(gmoe_d, "(g p) t -> p g t", p=P)[:, :, sl],
                          bv2)
                    b.barrier()
            b.flush()

        if "CMB" in phases:
            t_a = b.alloc("c_a", [P, DG * LC])
            t_b = b.alloc("c_b", [P, DG * LC])
            t_c = b.alloc("c_c", [P, DG * LC])
            t_dd = b.alloc("c_d", [P, DG * LC])
            fbr = b.alloc("c_fbr", [P, LC])
            ofr = b.alloc("c_ofr", [P, LC])
            for k in range(NCH):
                sl = slice(k * LC, (k + 1) * LC)
                av = rr(t_a[:], "p (g t) -> p g t", g=DG)
                bv2 = rr(t_b[:], "p (g t) -> p g t", g=DG)
                cv = rr(t_c[:], "p (g t) -> p g t", g=DG)
                dv = rr(t_dd[:], "p (g t) -> p g t", g=DG)
                l1_ = b.dma("sync", av, rr(gmoe_d, "(g p) t -> p g t", p=P)[:, :, sl])
                l2_ = b.dma("sync", bv2, rr(yfb_d, "(g p) t -> p g t", p=P)[:, :, sl])
                l3_ = b.dma("sync", cv, rr(x_fm, "(g p) t -> p g t", p=P)[:, :, sl])
                l4_ = b.dma("sync", dv,
                            rr(s_attn_d, "(g p) t -> p g t", p=P)[:, :, sl])
                lf = b.dma("gpsimd", fbr[:],
                           own_rq[NEXP, sl].unsqueeze(0).broadcast_to((P, LC)))
                b.wait_val("vector", b.sems["sync"], l4_)
                b.wait_val("vector", b.sems["qgpsimd"], lf)
                b.ts(ofr[:], fbr[:], -1.0, ALU.mult, 1.0, ALU.add)
                ofb = ofr[:].unsqueeze(1).broadcast_to((P, DG, LC))
                fbb = fbr[:].unsqueeze(1).broadcast_to((P, DG, LC))
                b.tt(av, av, ofb, ALU.mult)
                b.tt(bv2, bv2, fbb, ALU.mult)
                ggn = b.tt(av, av, bv2, ALU.add)
                b.wait_val("scalar", b.sems["vector"], ggn)
                sgn = b.act(av, av, AF.Silu)
                b.wait_val("vector", b.sems["scalar"], sgn)
                b.tt(dv, dv, av, ALU.mult)
                x1n = b.tt(cv, cv, dv, ALU.add)
                b.wait_val("scalar", b.sems["vector"], x1n)
                b.dma("scalar", rr(x1_d, "(g p) t -> p g t", p=P)[:, :, sl], cv)
                b.barrier()
            b.flush()

        if "C" in phases:
            wx1 = b.alloc("wx1", [P, DG * 96], BF16)
            wdt1 = b.alloc("wdt1", [DTRANK, DG * P], BF16)
            wR = b.alloc("wR", [P, DG * DG * P], BF16)
            wK = b.alloc("wK", [P, DG * DG * P], BF16)
            wV = b.alloc("wV", [P, DG * DG * P], BF16)
            t_x = b.alloc("c1_x", [P, 4096])
            t_u = b.alloc("c1_u", [P, 4096])
            t_v = b.alloc("c1_v", [P, 4096])
            t_w = b.alloc("c1_w", [P, 4096])
            t_h = b.alloc("c1_h", [P, 2048])
            t_y = b.alloc("c1_y", [P, 4096])
            t_d = b.alloc("c1_d", [P, 4096])
            prev_scans = [0]
            t_lb = b.alloc("c1_lb", [P, 4096], BF16)
            t_cb = b.alloc("c1_cb", [P, 4096], BF16)
            t_cb2 = b.alloc("c1_cb2", [P, 4096], BF16)
            dtb = b.alloc("c1_dtb", [DTRANK, 512], BF16)
            t_sm = b.alloc("c1_sm", [P, 3 * 512])
            murch = b.alloc("c1_mu", [P, 2 * 512])
            lw1 = b.dma("gpsimd", rr(wx1[:], "p (g m) -> p g m", g=DG),
                        rr(W_x1T, "(g p) m -> p g m", p=P))
            lw2 = b.dma("gpsimd", rr(wdt1[:], "k (g q) -> k g q", g=DG),
                        rr(W_dt1T, "k (g q) -> k g q", q=P))
            lw3 = b.dma("gpsimd", rr(wR[:], "p (g m q) -> p g m q", g=DG, m=DG),
                        rr(WR1T, "(g p) (m q) -> p g m q", p=P, q=P))
            lw4 = b.dma("gpsimd", rr(wK[:], "p (g m q) -> p g m q", g=DG, m=DG),
                        rr(WK1T, "(g p) (m q) -> p g m q", p=P, q=P))
            lw5 = b.dma("gpsimd", rr(wV[:], "p (g m q) -> p g m q", g=DG, m=DG),
                        rr(WV1T, "(g p) (m q) -> p g m q", p=P, q=P))
            b.wait_val("tensor", b.sems["qgpsimd"], lw5)
            wx1v = rr(wx1[:], "p (g m) -> p g m", g=DG)
            wdt1v = rr(wdt1[:], "k (g q) -> k g q", g=DG)
            wRv = rr(wR[:], "p (g m q) -> p g m q", g=DG, m=DG)
            wKv = rr(wK[:], "p (g m q) -> p g m q", g=DG, m=DG)
            wVv = rr(wV[:], "p (g m q) -> p g m q", g=DG, m=DG)
            for k in range(NCH):
                sl = slice(k * LC, (k + 1) * LC)
                first = (k % (SEQ // LC) == 0)
                xv = rr(t_x[:], "p (g t) -> p g t", g=DG)
                uv = rr(t_u[:], "p (g t) -> p g t", g=DG)
                vv_ = rr(t_v[:], "p (g t) -> p g t", g=DG)
                wv = rr(t_w[:], "p (g t) -> p g t", g=DG)
                yv = rr(t_y[:], "p (g t) -> p g t", g=DG)
                lbv = rr(t_lb[:], "p (g t) -> p g t", g=DG)
                cbv = rr(t_cb[:], "p (g t) -> p g t", g=DG)
                cb2v = rr(t_cb2[:], "p (g t) -> p g t", g=DG)
                mv = rr(murch[:], "p (r t) -> p r t", r=2)
                if first:
                    b.memset(hcar[:], 0.0)
                    b.memset(carry[:, DG:2 * DG], 0.0, eng="vector")
                ldx = b.dma("sync", xv, rr(x1_d, "(g p) t -> p g t", p=P)[:, :, sl])
                b.wait_val("scalar", b.sems["sync"], ldx)
                b.wait_val("vector", b.sems["sync"], ldx)
                lr = ln_rows(xv, wv, t_sm, LC, pb[2], pb[3])
                gn = repl_rows(t_sm, mv, LC, lr)
                b.wait_val("vector", b.sems["qgpsimd"], gn)
                rr_b = mv[:, 0:1, :].broadcast_to((P, DG, LC))
                nmr_b = mv[:, 1:2, :].broadcast_to((P, DG, LC))
                b.tt(uv, xv, rr_b, ALU.mult)
                xnn = b.tt(uv, uv, nmr_b, ALU.add)
                xbn = b.cpy(t_lb[:], t_u[:])
                for g in range(DG):
                    b.cpy(t_d[:, 1:512], uv[:, g, 0:511])
                    b.cpy(t_d[:, 0:1], carry[:, DG + g:DG + g + 1])
                    b.cpy(carry[:, DG + g:DG + g + 1], uv[:, g, 511:512])
                    b.ts(t_d[:, 512:1024], t_d[:, 0:512],
                         pv["om_mu_r"][:, g:g + 1], ALU.mult)
                    b.stt(cbv[:, g, :], uv[:, g, :], pv["mu_r"][:, g:g + 1],
                          t_d[:, 512:1024], ALU.mult, ALU.add)
                    b.ts(t_d[:, 512:1024], t_d[:, 0:512],
                         pv["om_mu_k"][:, g:g + 1], ALU.mult)
                    b.stt(cb2v[:, g, :], uv[:, g, :], pv["mu_k"][:, g:g + 1],
                          t_d[:, 512:1024], ALU.mult, ALU.add)
                shf_done = b.sems["vector"].n
                b.wait_val("tensor", b.sems["vector"], xbn)
                for g in range(DG):
                    b.mm(pb[4][0:DTRANK, :], wx1v[:, g, 0:DTRANK], lbv[:, g, :],
                         start=(g == 0), stop=(g == DG - 1))
                for g in range(DG):
                    b.mm(pb[5][0:32, :], wx1v[:, g, DTRANK:96], lbv[:, g, :],
                         start=(g == 0), stop=(g == DG - 1), inc=(g == DG - 1))
                xpn = b.sems["tensor"].n
                b.wait_val("scalar", b.sems["tensor"], xpn)
                dtev = b.act(dtb[:, :], pb[4][0:DTRANK, :], AF.Identity)
                bcev = b.act(t_sm[0:32, 0:512], pb[5][0:32, :], AF.Identity)
                b.wait_val("scalar", b.sems["qgpsimd"], bcd_guard[0])
                b.wait_val("scalar", b.sems["scalar"], bcev)
                bcst = b.dma("scalar", bc_d[:, 0:LC], t_sm[0:32, 0:512])
                b.wait_val("scalar", b.sems["qscalar"], bcst)
                b.wait_val("scalar", b.sems["vector"], shf_done)
                b.wait_val("tensor", b.sems["scalar"], dtev)
                for g in range(DG):
                    bank = pb[6] if g % 2 == 0 else pb[4]
                    if g >= 2:
                        b.wait_val("tensor", b.sems["vector"], dproj_done[g % 2])
                    b.mm(bank[:, :], wdt1v[:, g, :], dtb[:, :],
                         start=True, stop=True, inc=True)
                    mmn = b.sems["tensor"].n
                    b.wait_val("scalar", b.sems["tensor"], mmn)
                    if g >= 1:
                        b.wait_val("scalar", b.sems["vector"], dproj_done[(g + 1) % 2])
                    b.act(t_sm[:, 0:512], bank[:, :], AF.Identity,
                          bias=pv["b_dt1"][:, g:g + 1])
                    if k == 0 and g == 0 and "th" in dbg:
                        pdn = b.sems["scalar"].n
                        b.wait_val("scalar", b.sems["scalar"], pdn)
                        b.dma("scalar", th_d[:, 0:512], t_sm[:, 0:512])
                    b.act(t_d[:, 0:512], t_sm[:, 0:512], AF.Abs)
                    b.act(t_d[:, 0:512], t_d[:, 0:512], AF.Exp, scale=-1.0)
                    spn = b.act(t_d[:, 0:512], t_d[:, 0:512], AF.Ln, bias=1.0)
                    b.wait_val("vector", b.sems["scalar"], spn)
                    b.ts(t_d[:, 512:1024], t_sm[:, 0:512], 0.0, ALU.max)
                    dn = b.tt(xv[:, g, :], t_d[:, 512:1024], t_d[:, 0:512], ALU.add)
                    if k == 0 and g == 0 and "th" in dbg:
                        b.wait_val("scalar", b.sems["vector"], dn)
                        b.dma("scalar", th_d[:, 512:1024], t_d[:, 0:512])
                        b.dma("scalar", th_d[:, 1024:1536], xv[:, g, :])
                        b.wait_val("vector", b.sems["qscalar"], b.sems["qscalar"].n)
                    if g == 0:
                        dproj_done = [0, 0]
                    dproj_done[g % 2] = dn
                dl1n = b.tt(t_v[:], t_x[:], t_u[:], ALU.mult)
                prev_scans[0] = dl1n
                if k == 0:
                    b.wait_val("scalar", b.sems["vector"], dl1n)
                    b.dma("scalar", a1_d, A1_sb[:])
                    b.dma("scalar", xn1_d, t_u[:])
                    dstn = b.dma("scalar", dl1_d, t_x[:])
                    b.wait_val("vector", b.sems["qscalar"], dstn)
                for q4 in range(4):
                    ns = q4 * 4
                    b.wait_val("gpsimd", b.sems["vector"], bc_guard[0])
                    rbb = b.dma("gpsimd", rr(t_w[:, 0:2048], "p (j t) -> p j t", j=4),
                                bc_d[ns:ns + 4, 0:LC].unsqueeze(0).broadcast_to((P, 4, LC)))
                    rcc = b.dma("gpsimd", rr(t_w[:, 2048:4096], "p (j t) -> p j t", j=4),
                                bc_d[16 + ns:16 + ns + 4, 0:LC].unsqueeze(0).broadcast_to((P, 4, LC)))
                    bcd_guard[0] = rcc
                    b.wait_val("vector", b.sems["qgpsimd"], rcc)
                    h8 = rr(t_h[:], "p (j t) -> p j t", j=4)
                    seb = rr(t_d[:, 0:2048], "p (j t) -> p j t", j=4)
                    for g in range(DG):
                        b.wait_val("scalar", b.sems["vector"], prev_scans[0])
                        expn = 0
                        for j in range(4):
                            n = ns + j
                            expn = b.act(seb[:, j, :], xv[:, g, :], AF.Exp,
                                         scale=A1_sb[:, g * NSTATE + n:g * NSTATE + n + 1])
                        for j in range(4):
                            n = ns + j
                            sa = t_d[:, 2048 + (j % 2) * 512:2560 + (j % 2) * 512]
                            b.tt(sa, vv_[:, g, :],
                                 rr(t_w[:, 0:2048], "p (j t) -> p j t", j=4)[:, j, :],
                                 ALU.mult)
                            if j == 0:
                                b.wait_val("vector", b.sems["scalar"], expn)
                            b.scan(h8[:, j, :], seb[:, j, :], sa,
                                   hcar[:, g * NSTATE + n:g * NSTATE + n + 1])
                        b.cpy(hcar[:, g * NSTATE + ns:g * NSTATE + ns + 4].unsqueeze(2),
                              h8[:, :, 511:512])
                        b.tt(t_h[:], t_h[:], t_w[:, 2048:4096], ALU.mult)
                        if q4 == 0:
                            b.red(yv[:, g, :], rr(t_h[:], "p (n t) -> p t n", n=4))
                        else:
                            b.red(t_d[:, 3072:3584], rr(t_h[:], "p (n t) -> p t n", n=4))
                            b.tt(yv[:, g, :], yv[:, g, :], t_d[:, 3072:3584], ALU.add)
                        prev_scans[0] = b.sems["vector"].n
                    bc_guard[0] = b.sems["vector"].n
                for g in range(DG):
                    b.stt(yv[:, g, :], uv[:, g, :], pv["D_par"][:, g:g + 1],
                          yv[:, g, :], ALU.mult, ALU.add)
                s1_done = b.sems["vector"].n
                b.wait_val("tensor", b.sems["vector"], s1_done)
                ev_prev = [0, 0]
                for m in range(DG):
                    bank = pb[m % 2]
                    if ev_prev[m % 2]:
                        b.wait_val("tensor", b.sems["scalar"], ev_prev[m % 2])
                    for gk in range(DG):
                        b.mm(bank[:, :], wRv[:, gk, m, :], cbv[:, gk, :],
                             start=(gk == 0), stop=(gk == DG - 1), inc=(gk == DG - 1))
                    mmn = b.sems["tensor"].n
                    b.wait_val("scalar", b.sems["tensor"], mmn)
                    ev_prev[m % 2] = b.act(vv_[:, m, :], bank[:, :], AF.Sigmoid)
                rt_done = b.sems["scalar"].n
                ev_prev = [0, 0]
                for m in range(DG):
                    bank = pb[m % 2]
                    if ev_prev[m % 2]:
                        b.wait_val("tensor", b.sems["scalar"], ev_prev[m % 2])
                    for gk in range(DG):
                        b.mm(bank[:, :], wKv[:, gk, m, :], cb2v[:, gk, :],
                             start=(gk == 0), stop=(gk == DG - 1), inc=(gk == DG - 1))
                    mmn = b.sems["tensor"].n
                    b.wait_val("scalar", b.sems["tensor"], mmn)
                    b.act(t_sm[:, 0:512], bank[:, :], AF.Relu)
                    ev_prev[m % 2] = b.act(lbv[:, m, :], t_sm[:, 0:512], AF.Square)
                k2_done = b.sems["scalar"].n
                b.wait_val("tensor", b.sems["scalar"], k2_done)
                b.wait_val("sync", b.sems["vector"], bc_guard[0])
                b.wait_val("sync", b.sems["qgpsimd"], b.sems["qgpsimd"].n)
                ldx1 = b.dma("sync", wv, rr(x1_d, "(g p) t -> p g t", p=P)[:, :, sl])
                ev_prev = [0, 0]
                for m in range(DG):
                    bank = pb[m % 2]
                    if ev_prev[m % 2]:
                        b.wait_val("tensor", b.sems["vector"], ev_prev[m % 2])
                    for gk in range(DG):
                        b.mm(bank[:, :], wVv[:, gk, m, :], lbv[:, gk, :],
                             start=(gk == 0), stop=(gk == DG - 1), inc=(gk == DG - 1))
                    mmn = b.sems["tensor"].n
                    b.wait_val("vector", b.sems["tensor"], mmn)
                    ev_prev[m % 2] = b.tt(xv[:, m, :], bank[:, :], vv_[:, m, :],
                                          ALU.mult)
                g1_done = b.sems["vector"].n
                b.wait_val("scalar", b.sems["vector"], g1_done)
                sgn = b.act(t_x[:], t_x[:], AF.Silu)
                b.wait_val("vector", b.sems["scalar"], sgn)
                b.wait_val("vector", b.sems["sync"], ldx1)
                b.tt(t_x[:], t_x[:], t_y[:], ALU.mult)
                onn = b.tt(t_x[:], t_x[:], t_w[:], ALU.add)
                b.wait_val("scalar", b.sems["vector"], onn)
                b.dma("scalar", rr(out_fm, "(g p) t -> p g t", p=P)[:, :, sl], xv)
                b.barrier()
            b.flush()

        for k2, ap in dbg_outs.items():
            b.dma("scalar", ap, DBG_REG[k2])
        b.barrier()
        b.flush()
    return nc


def make_in_maps(inputs):
    i = inputs
    C = np.ascontiguousarray
    x_flat = np.asarray(i["L0_attn_src"], np.float32).reshape(TGLOB, D)
    x_fm_full = C(x_flat.T)
    def vec_pg(v):
        return C(np.asarray(v, np.float32).reshape(DG, P).T)

    shared = {
        "x_fm_full": x_fm_full,
        "W_inT": C(np.asarray(i["L0_attn_in_proj_w"], np.float32).T),
        "conv_w0": vec_pg(np.asarray(i["L0_attn_conv_w"], np.float32)[:, 0]),
        "conv_w1": vec_pg(np.asarray(i["L0_attn_conv_w"], np.float32)[:, 1]),
        "conv_b": vec_pg(i["L0_attn_conv_b"]),
        "W_xpT": C(np.asarray(i["L0_attn_x_proj_w"], np.float32).T),
        "W_dtT": C(np.asarray(i["L0_attn_dt_proj_w"], np.float32).T),
        "dt_b": vec_pg(i["L0_attn_dt_proj_b"]),
        "A_log0": C(np.asarray(i["L0_attn_A_log"], np.float32).reshape(DG, P, NSTATE).transpose(1, 0, 2)),
        "Dp0": vec_pg(i["L0_attn_Dp"]),
        "W_outT": C(np.asarray(i["L0_attn_out_proj_w"], np.float32).T),
        "mnorm_g": vec_pg(i["L0_attn_mnorm_g"]),
        "mnorm_b": vec_pg(i["L0_attn_mnorm_b"]),
        "n1_g": vec_pg(i["L0_attn_n1_g"]),
        "n1_b": vec_pg(i["L0_attn_n1_b"]),
        "n2_g": vec_pg(i["L0_attn_n2_g"]),
        "n2_b": vec_pg(i["L0_attn_n2_b"]),
        "n3_g": vec_pg(i["L0_attn_n3_g"]),
        "n3_b": vec_pg(i["L0_attn_n3_b"]),
        "W_f1T": C(np.asarray(i["L0_attn_ffn_w1"], np.float32).T),
        "W_f2T": C(np.asarray(i["L0_attn_ffn_w2"], np.float32).T),
        "W_feT": C(np.asarray(i["L0_attn_fe_w"], np.float32).T),
        "fe_b": vec_pg(i["L0_attn_fe_b"]),
        "WrT": C(np.asarray(i["L0_ffn_Wr"], np.float32).T),
        "W1": C(np.asarray(i["L0_ffn_W1"], np.float32)),
        "b1": C(np.asarray(i["L0_ffn_b1"], np.float32).reshape(NEXP, 32, P).transpose(2, 0, 1)),
        "W2": C(np.asarray(i["L0_ffn_W2"], np.float32)),
        "b2": C(np.asarray(i["L0_ffn_b2"], np.float32).reshape(NEXP, 8, P).transpose(2, 0, 1)),
        "fw1": C(np.asarray(i["L0_ffn_fw1"], np.float32)),
        "fb1": C(np.asarray(i["L0_ffn_fb1"], np.float32).reshape(32, P).T),
        "fw2": C(np.asarray(i["L0_ffn_fw2"], np.float32)),
        "fb2": vec_pg(i["L0_ffn_fb2"]),
        "A_log1": C(np.asarray(i["L1_attn_A_log"], np.float32).reshape(DG, P, NSTATE).transpose(1, 0, 2)),
        "W_x1T": C(np.asarray(i["L1_attn_W_x"], np.float32).T),
        "W_dt1T": C(np.asarray(i["L1_attn_W_dt"], np.float32).T),
        "b_dt1": vec_pg(i["L1_attn_b_dt"]),
        "D_par": vec_pg(i["L1_attn_D_param"]),
        "WR1T": C(np.asarray(i["L1_ffn_W_r"], np.float32).T),
        "WK1T": C(np.asarray(i["L1_ffn_W_k"], np.float32).T),
        "WV1T": C(np.asarray(i["L1_ffn_W_v"], np.float32).T),
        "mu_r": vec_pg(i["L1_ffn_mu_r"]),
        "mu_k": vec_pg(i["L1_ffn_mu_k"]),
    }
    in_maps = []
    for c in range(8):
        m = dict(shared)
        m["x_fm"] = C(x_fm_full[:, c * NTOK:(c + 1) * NTOK])
        m["idx_own"] = (c * (NTOK // 32) + np.arange(P, dtype=np.int32)).reshape(P, 1)
        in_maps.append(m)
    return in_maps


def kernel(**inputs):
    nc = build_program(ALL_PHASES)
    in_maps = make_in_maps(inputs)
    res = run_bass_kernel_spmd(nc, in_maps, core_ids=list(range(8)))
    out = np.concatenate(
        [np.asarray(res.results[c]["out_fm"]).T for c in range(8)], axis=0)
    return np.ascontiguousarray(out.reshape(16, SEQ, D).astype(np.float32))


# version 06
# speedup vs baseline: 1.1880x; 1.1051x over previous
import contextlib
import numpy as np
import concourse.bass as bass
import concourse.mybir as mybir
from concourse.bass import IndirectOffsetOnAxis
from concourse.bass_utils import run_bass_kernel_spmd

F32 = mybir.dt.float32
BF16 = mybir.dt.bfloat16
I32 = mybir.dt.int32
AF = mybir.ActivationFunctionType
ALU = mybir.AluOpType
AX = mybir.AxisListType

P = 128
D = 1024
DG = 8
NTOK = 4096
TGLOB = 32768
SEQ = 2048
LC = 512
NCH = NTOK // LC
NSTATE = 16
DTRANK = 64
DFFN = 4096
NEXP = 8
CAP = 10240
EPS = 1e-5
RF = TGLOB // P

ENGS = ("sync", "scalar", "vector", "tensor", "gpsimd")


class Sem:
    def __init__(self, nc, es, name):
        self.h = es.enter_context(nc.semaphore(name))
        self.n = 0
        self.name = name


class B:

    def __init__(self, nc, es, block):
        self.nc = nc
        self.es = es
        self.block = block
        self.script = []
        self.sems = {e: Sem(nc, es, "s_" + e) for e in ENGS}
        self.sems["qscalar"] = Sem(nc, es, "q_scalar")
        self.sems["qgpsimd"] = Sem(nc, es, "q_gpsimd")
        self.phase_tiles = []

    def alloc(self, name, shape, dt=F32):
        cm = self.nc.sbuf_tensor(name, list(shape), dt)
        t = cm.__enter__()
        self.phase_tiles.append(cm)
        return t

    def flush(self, free=True):
        for engname in ENGS:
            steps = [fn for e, fn in self.script if e == engname]
            if not steps:
                continue
            deco = getattr(self.block, engname)

            def body(eng, _steps=steps):
                for fn in _steps:
                    fn(eng)
            deco(body)
        self.script = []
        if free:
            for cm in reversed(self.phase_tiles):
                cm.__exit__(None, None, None)
            self.phase_tiles = []

    def raw(self, eng, fn):
        s = self.sems[eng]
        k = 16 if eng == "sync" else 1
        s.n += k
        self.script.append((eng, lambda e, fn=fn, s=s, k=k: fn(e).then_inc(s.h, k)))
        return s.n

    def wait(self, eng, dep_eng, val=None):
        s = self.sems[dep_eng]
        v = s.n if val is None else val
        if v > 0:
            self.script.append((eng, lambda e, s=s, v=v: e.wait_ge(s.h, v)))
        if val is None and dep_eng in ("scalar", "gpsimd"):
            q = self.sems["q" + dep_eng]
            if q.n > 0:
                self.script.append((eng, lambda e, q=q, v=q.n: e.wait_ge(q.h, v)))

    def wait_val(self, eng, sem, v):
        if v > 0:
            self.script.append((eng, lambda e, sem=sem, v=v: e.wait_ge(sem.h, v)))

    def dma(self, eng, out, in_, **kw):
        s = self.sems[{"scalar": "qscalar", "gpsimd": "qgpsimd"}.get(eng, eng)]
        if s.n > 0:
            self.wait_val(eng, s, s.n)
        s.n += 16
        self.script.append(
            (eng, lambda e, out=out, in_=in_, kw=kw, s=s:
             e.dma_start(out=out, in_=in_, **kw).then_inc(s.h, 16)))
        return s.n

    def idma(self, out, out_off, in_, in_off, bc=None):
        s = self.sems["qgpsimd"]
        if s.n > 0:
            self.wait_val("gpsimd", s, s.n)
        s.n += 16

        def f(e, out=out, oo=out_off, in_=in_, io=in_off, bc=bc, s=s):
            kw = {}
            if bc is not None:
                kw = dict(bounds_check=bc, oob_is_err=False)
            return e.indirect_dma_start(out=out, out_offset=oo, in_=in_,
                                        in_offset=io, **kw).then_inc(s.h, 16)
        self.script.append(("gpsimd", f))
        return s.n

    def act(self, out, in_, func, bias=0.0, scale=1.0):
        return self.raw("scalar", lambda e, o=out, i=in_, f=func, bb=bias, ss=scale:
                        e.activation(out=o, in_=i, func=f, bias=bb, scale=ss))

    def tt(self, out, a, b_, op, eng="vector"):
        return self.raw(eng, lambda e, o=out, a=a, b=b_, op=op:
                        e.tensor_tensor(out=o, in0=a, in1=b, op=op))

    def ts(self, out, a, s1, op0, s2=None, op1=ALU.bypass, eng="vector"):
        return self.raw(eng, lambda e, o=out, a=a, s1=s1, s2=s2, op0=op0, op1=op1:
                        e.tensor_scalar(out=o, in0=a, scalar1=s1, scalar2=s2,
                                        op0=op0, op1=op1))

    def stt(self, out, in0, scalar, in1, op0, op1, eng="vector"):
        return self.raw(eng, lambda e, o=out, i0=in0, s=scalar, i1=in1,
                        op0=op0, op1=op1:
                        e.scalar_tensor_tensor(out=o, in0=i0, scalar=s, in1=i1,
                                               op0=op0, op1=op1))

    def scan(self, out, d0, d1, init, op0=ALU.mult, op1=ALU.add, eng="vector"):
        return self.raw(eng, lambda e, o=out, d0=d0, d1=d1, i=init,
                        op0=op0, op1=op1:
                        e.tensor_tensor_scan(out=o, data0=d0, data1=d1,
                                             initial=i, op0=op0, op1=op1))

    def red(self, out, in_, op=ALU.add, negate=False):
        return self.raw("vector", lambda e, o=out, i=in_, op=op, n=negate:
                        e.tensor_reduce(out=o, in_=i, axis=AX.X, op=op, negate=n))

    def cpy(self, out, in_, eng="vector"):
        return self.raw(eng, lambda e, o=out, i=in_: e.tensor_copy(out=o, in_=i))

    def memset(self, ap, v, eng="gpsimd"):
        return self.raw(eng, lambda e, a=ap, v=v: e.memset(a, v))

    def recip(self, out, in_):
        return self.raw("vector", lambda e, o=out, i=in_: e.reciprocal(out=o, in_=i))

    def mm(self, out, lhsT, rhs, start, stop, inc=False):
        s = self.sems["tensor"]
        if inc:
            s.n += 1
            self.script.append(
                ("tensor", lambda e, o=out, l=lhsT, r=rhs, a=start, bb=stop, s=s:
                 e.matmul(o, l, r, start=a, stop=bb).then_inc(s.h, 1)))
        else:
            self.script.append(
                ("tensor", lambda e, o=out, l=lhsT, r=rhs, a=start, bb=stop:
                 e.matmul(o, l, r, start=a, stop=bb)))
        return s.n

    def transpose(self, out, in_, ident):
        s = self.sems["tensor"]
        s.n += 1
        self.script.append(("tensor", lambda e, o=out, i=in_, d=ident, s=s:
                            e.transpose(o, i, d).then_inc(s.h, 1)))
        return s.n

    def iota(self, out, pattern, base=0, chan=0):
        return self.raw("gpsimd", lambda e, o=out, pp=pattern, bb=base, c=chan:
                        e.iota(o, pattern=pp, base=bb, channel_multiplier=c,
                               allow_small_or_imprecise_dtypes=True))

    def barrier(self):
        allsems = list(self.sems.values())
        for e in ENGS:
            for s in allsems:
                if s.name in ("s_" + e,):
                    continue
                if s.n > 0:
                    self.wait_val(e, s, s.n)


def rr(ap, spec, **kw):
    return ap.rearrange(spec, **kw)


ALL_PHASES = ("R", "A1", "A2a", "A2b", "A3", "M", "CMB", "C")


def build_program(phases=ALL_PHASES, dbg=()):
    nc = bass.Bass("TRN2", target_bir_lowering=False)

    def din(name, shape, dt=F32):
        return nc.dram_tensor(name, list(shape), dt, kind="ExternalInput").ap()

    x_fm_full = din("x_fm_full", [D, TGLOB])
    x_fm = din("x_fm", [D, NTOK])
    W_inT = din("W_inT", [D, 2 * D])
    conv_w0 = din("conv_w0", [P, DG])
    conv_w1 = din("conv_w1", [P, DG])
    conv_b = din("conv_b", [P, DG])
    W_xpT = din("W_xpT", [D, 96])
    W_dtT = din("W_dtT", [DTRANK, D])
    dt_b = din("dt_b", [P, DG])
    A_log0 = din("A_log0", [P, DG, NSTATE])
    Dp0 = din("Dp0", [P, DG])
    W_outT = din("W_outT", [D, D])
    mnorm_g = din("mnorm_g", [P, DG]); mnorm_b = din("mnorm_b", [P, DG])
    n1_g = din("n1_g", [P, DG]); n1_b = din("n1_b", [P, DG])
    n2_g = din("n2_g", [P, DG]); n2_b = din("n2_b", [P, DG])
    n3_g = din("n3_g", [P, DG]); n3_b = din("n3_b", [P, DG])
    W_f1T = din("W_f1T", [D, DFFN])
    W_f2T = din("W_f2T", [DFFN, D])
    W_feT = din("W_feT", [D, D])
    fe_b = din("fe_b", [P, DG])
    WrT = din("WrT", [D, NEXP])
    W1 = din("W1", [NEXP, D, DFFN])
    b1 = din("b1", [P, NEXP, DFFN // P])
    W2 = din("W2", [NEXP, DFFN, D])
    b2 = din("b2", [P, NEXP, D // P])
    fw1 = din("fw1", [D, DFFN])
    fb1 = din("fb1", [P, DFFN // P])
    fw2 = din("fw2", [DFFN, D])
    fb2 = din("fb2", [P, DG])
    A_log1 = din("A_log1", [P, DG, NSTATE])
    W_x1T = din("W_x1T", [D, 96])
    W_dt1T = din("W_dt1T", [DTRANK, D])
    b_dt1 = din("b_dt1", [P, DG])
    D_par = din("D_par", [P, DG])
    WR1T = din("WR1T", [D, D])
    WK1T = din("WK1T", [D, D])
    WV1T = din("WV1T", [D, D])
    mu_r = din("mu_r", [P, DG]); mu_k = din("mu_k", [P, DG])
    idx_own = din("idx_own", [P, 1], I32)

    out_fm = nc.dram_tensor("out_fm", [D, NTOK], F32, kind="ExternalOutput").ap()

    def scr(name, shape, dt=F32):
        return nc.dram_tensor(name, list(shape), dt, kind="Internal").ap()

    lraw_g = scr("lraw_g", [2 * NEXP + 2, TGLOB])
    own_stats = scr("own_stats", [2, NTOK])
    own_rq = scr("own_rq", [NEXP + 1, NTOK])
    xn_fm_d = scr("xn_fm_d", [D, NTOK])
    xnbf_fm_d = scr("xnbf_fm_d", [D, NTOK], BF16)
    zs_d = scr("zs_d", [D, NTOK])
    xc_d = scr("xc_d", [D, NTOK])
    delta_d = scr("delta_d", [D, NTOK])
    hm_d = scr("hm_d", [D, NTOK])
    ymix_d = scr("ymix_d", [D, NTOK])
    ysc_d = scr("ysc_d", [D, NTOK])
    tw_d = scr("tw_d", [P, 4096])
    td_d = scr("td_d", [P, 2048])
    a1_d = scr("a1_d", [P, DG * NSTATE])
    xn1_d = scr("xn1_d2", [P, 4096])
    dl1_d = scr("dl1_d", [P, 4096])
    th_d = scr("th_d", [P, 2048])
    src2_d = scr("src2_d", [D, NTOK])
    src3_d = scr("src3_d", [D, NTOK])
    s_attn_d = scr("s_attn_d", [D, NTOK])
    yfb_d = scr("yfb_d", [D, NTOK])
    gmoe_d = scr("gmoe_d", [D, NTOK])
    x1_d = scr("x1_d", [D, NTOK])
    rowb_d = scr("rowb_d", [2, LC])
    bc_d = scr("bc_d", [32, LC])
    sw_d = scr("sw_d", [NEXP])

    DBG_REG = {"own_stats": own_stats, "own_rq": own_rq, "xn": xn_fm_d,
               "xc": xc_d, "delta": delta_d, "hm": hm_d, "ymix": ymix_d,
               "ysc": ysc_d, "tw": tw_d, "td": td_d, "th": th_d,
               "a1sb": a1_d, "xn1": xn1_d, "dl1": dl1_d,
               "zs": zs_d, "src2": src2_d, "src3": src3_d, "s_attn": s_attn_d,
               "yfb": yfb_d, "gmoe": gmoe_d, "x1": x1_d, "lraw": lraw_g}
    dbg_outs = {}
    for k in dbg:
        t = DBG_REG[k]
        dbg_outs[k] = nc.dram_tensor("dbg_" + k, list(t.shape),
                                     t.dtype, kind="ExternalOutput").ap()

    with contextlib.ExitStack() as es:
        block = es.enter_context(nc.Block())
        b = B(nc, es, block)

        def psbt(name, shape, dt=F32):
            return es.enter_context(nc.sbuf_tensor(name, list(shape), dt))

        pv = {}
        for nm in ["conv_w0", "conv_w1", "conv_b", "dt_b", "Dp0", "mnorm_g",
                   "mnorm_b", "n1_g", "n1_b", "n2_g", "n2_b", "n3_g", "n3_b",
                   "fe_b", "fb2", "b_dt1", "D_par", "mu_r", "mu_k",
                   "om_mu_r", "om_mu_k"]:
            pv[nm] = psbt("pv_" + nm, [P, DG])
        A0_sb = psbt("A0_sb", [P, DG * NSTATE])
        A1_sb = psbt("A1_sb", [P, DG * NSTATE])
        fb1_sb = psbt("fb1_sb", [P, DFFN // P])
        b1e_sb = psbt("b1e_sb", [P, NEXP * (DFFN // P)])
        b2e_sb = psbt("b2e_sb", [P, NEXP * (D // P)])
        ones_sb = psbt("ones_sb", [P, 2])
        tri_sb = psbt("tri_sb", [P, P])
        WrT9 = psbt("WrT9", [P, DG * 9])
        swcol = psbt("swcol", [P, NEXP])
        idx1 = psbt("idx1", [P, 1], I32)
        idxq = psbt("idxq", [P, 1], I32)
        carry = psbt("carry", [P, 4 * DG])
        hcar = psbt("hcar", [P, DG * NSTATE])
        rows = psbt("rows", [P, 64])
        pb = [es.enter_context(nc.psum_tensor(f"pb{i}", [P, 512], F32))
              for i in range(7)]
        pbx = es.enter_context(nc.psum_tensor("pbx", [P, 512], F32))

        for nm, srcw in [("conv_b", conv_b), ("dt_b", dt_b), ("Dp0", Dp0),
                         ("mnorm_g", mnorm_g), ("mnorm_b", mnorm_b),
                         ("n1_g", n1_g), ("n1_b", n1_b), ("n2_g", n2_g),
                         ("n2_b", n2_b), ("n3_g", n3_g), ("n3_b", n3_b),
                         ("fe_b", fe_b), ("fb2", fb2), ("b_dt1", b_dt1),
                         ("D_par", D_par), ("mu_r", mu_r), ("mu_k", mu_k),
                         ("conv_w0", conv_w0), ("conv_w1", conv_w1)]:
            b.dma("sync", pv[nm][:], srcw)
        b.dma("sync", rr(A0_sb[:], "p (g n) -> p g n", g=DG), A_log0)
        b.dma("sync", rr(A1_sb[:], "p (g n) -> p g n", g=DG), A_log1)
        b.dma("sync", fb1_sb[:], fb1)
        b.dma("sync", rr(b1e_sb[:], "p (e m) -> p e m", e=NEXP), b1)
        b.dma("sync", rr(b2e_sb[:], "p (e m) -> p e m", e=NEXP), b2)
        ld0 = b.dma("sync", idx1[:], idx_own)
        b.wait_val("gpsimd", b.sems["sync"], ld0)
        b.memset(ones_sb[:], 1.0)
        it_a = b.alloc("it_a", [P, P])
        it_b = b.alloc("it_b", [P, P])
        b.iota(it_a[:], [[1, P]], 0, 0)
        b.iota(it_b[:], [[0, P]], 0, 1)
        b.memset(WrT9[:], 1.0)
        gp0 = b.sems["gpsimd"].n
        b.wait_val("sync", b.sems["gpsimd"], gp0)
        b.dma("sync", rr(WrT9[:], "p (g m) -> p g m", g=DG)[:, :, 0:NEXP],
              rr(WrT, "(g p) m -> p g m", p=P))
        b.wait("vector", "gpsimd")
        b.wait("vector", "sync")
        b.tt(tri_sb[:], it_b[:], it_a[:], ALU.is_lt)
        b.ts(pv["om_mu_r"][:], pv["mu_r"][:], -1.0, ALU.mult, 1.0, ALU.add)
        b.ts(pv["om_mu_k"][:], pv["mu_k"][:], -1.0, ALU.mult, 1.0, ALU.add)
        b.wait_val("scalar", b.sems["sync"], ld0)
        b.act(A0_sb[:], A0_sb[:], AF.Exp)
        b.act(A1_sb[:], A1_sb[:], AF.Exp)
        b.wait("vector", "scalar")
        b.ts(A0_sb[:], A0_sb[:], -1.0, ALU.mult)
        b.ts(A1_sb[:], A1_sb[:], -1.0, ALU.mult)
        b.wait("tensor", "sync")
        for g in range(DG):
            b.mm(pbx[0:NEXP, 0:1],
                 rr(WrT9[:], "p (g m) -> p g m", g=DG)[:, g, 0:NEXP],
                 ones_sb[:, 0:1], start=(g == 0), stop=(g == DG - 1),
                 inc=(g == DG - 1))
        b.wait("vector", "tensor")
        swcp = b.cpy(rows[0:NEXP, 0:1], pbx[0:NEXP, 0:1])
        b.wait_val("scalar", b.sems["vector"], swcp)
        b.dma("scalar", sw_d, rows[0:NEXP, 0:1].squeeze(1))
        b.wait("gpsimd", "scalar")
        b.dma("gpsimd", swcol[:], sw_d.unsqueeze(0).broadcast_to((P, NEXP)))
        b.memset(carry[:], 0.0)
        b.barrier()
        b.flush()

        if "R" in phases:
            xg0 = b.alloc("r_xg0", [P, 4096])
            xg1 = b.alloc("r_xg1", [P, 4096])
            xq = b.alloc("r_xq", [P, 4096])
            sm = b.alloc("r_sm", [P, 512])
            NIT = TGLOB // 512
            pe_done = [0] * NIT
            ve_done = [0] * NIT
            for i in range(NIT):
                buf = rr([xg0, xg1][i % 2][:], "p (g t) -> p g t", g=DG)
                if i >= 2:
                    b.wait_val("sync", b.sems["tensor"], pe_done[i - 2])
                ldn = b.dma("sync", buf,
                            rr(x_fm_full, "(g p) t -> p g t", p=P)[:, :, i * 512:(i + 1) * 512])
                b.wait_val("scalar", b.sems["sync"], ldn)
                if i >= 1:
                    b.wait_val("scalar", b.sems["tensor"], pe_done[i - 1])
                sq = b.act(rr(xq[:], "p (g t) -> p g t", g=DG), buf, AF.Square)
                b.wait_val("tensor", b.sems["sync"], ldn)
                b.wait_val("tensor", b.sems["scalar"], sq)
                if i >= 1:
                    b.wait_val("tensor", b.sems["vector"], ve_done[i - 1])
                for g in range(DG):
                    b.mm(pb[0][0:9, :],
                         rr(WrT9[:], "p (g m) -> p g m", g=DG)[:, g, :],
                         buf[:, g, :], start=(g == 0), stop=(g == DG - 1))
                for g in range(DG):
                    b.mm(pb[1][0:1, :], ones_sb[:, 0:1],
                         rr(xq[:], "p (g t) -> p g t", g=DG)[:, g, :],
                         start=(g == 0), stop=(g == DG - 1),
                         inc=(g == DG - 1))
                pe_done[i] = b.sems["tensor"].n
                b.wait_val("vector", b.sems["tensor"], pe_done[i])
                if i >= 1:
                    b.wait_val("vector", b.sems["qscalar"], st_done[i - 1])
                b.cpy(sm[0:9, :], pb[0][0:9, :])
                ev = b.cpy(sm[32:33, :], pb[1][0:1, :])
                ve_done[i] = ev
                b.wait_val("scalar", b.sems["vector"], ev)
                b.dma("scalar", lraw_g[0:9, i * 512:(i + 1) * 512], sm[0:9, :])
                if i == 0:
                    st_done = [0] * NIT
                st_done[i] = b.dma("scalar", lraw_g[9:10, i * 512:(i + 1) * 512],
                                   sm[32:33, :])
            b.barrier()
            b.flush()

            ga = b.alloc("r_ga", [P, NEXP * RF])
            gb = b.alloc("r_gb", [P, 12 * RF])
            gc = b.alloc("r_gc", [P, 4 * RF])
            growbig = b.alloc("r_rowbig", [P, NTOK // P])

            def gt(arena, slot):
                return arena[:, slot * RF:(slot + 1) * RF]

            mu_t, r_t, tv, m1_t, p2_t, sig_t, tmp_t, omf_t, csum_t, pos_t, tmp2_t, _ = \
                [gt(gb, i) for i in range(12)]
            le = [gt(ga, e) for e in range(NEXP)]
            wk_t, elig_t, kk_t, tmp3_t = [gt(gc, i) for i in range(4)]

            def row_g(q):
                return rr(lraw_g[q], "(p f) -> p f", p=P)

            b.dma("sync", mu_t, row_g(8))
            l2 = b.dma("sync", r_t, row_g(9))
            b.wait_val("vector", b.sems["sync"], l2)
            b.ts(mu_t, mu_t, 1.0 / D, ALU.mult)
            b.ts(tv, r_t, 1.0 / D, ALU.mult)
            b.tt(tmp_t, mu_t, mu_t, ALU.mult)
            b.tt(tv, tv, tmp_t, ALU.subtract)
            vv = b.ts(tv, tv, EPS, ALU.add)
            b.wait_val("scalar", b.sems["vector"], vv)
            b.act(tv, tv, AF.Ln)
            rrr = b.act(r_t, tv, AF.Exp, scale=-0.5)
            b.wait_val("vector", b.sems["scalar"], rrr)
            b.wait_val("scalar", b.sems["scalar"], rrr)
            st1 = b.dma("scalar", row_g(8), mu_t)
            st2 = b.dma("scalar", row_g(9), r_t)
            b.barrier()
            lraw_flat = rr(lraw_g, "a (bb cc) -> (a bb) cc", cc=32)

            def own_gather(q, dst):
                tsn = b.ts(idxq[:], idx1[:], q * (TGLOB // 32), ALU.add)
                b.wait_val("gpsimd", b.sems["vector"], tsn)
                gg = b.idma(growbig[:], None, lraw_flat,
                            IndirectOffsetOnAxis(ap=idxq[:], axis=0))
                b.wait_val("gpsimd", b.sems["qgpsimd"], gg)
                b.dma("gpsimd", rr(dst, "(p f) -> p f", p=P), growbig[:])
                b.wait_val("vector", b.sems["qgpsimd"], gg)

            for q, dst in [(8, own_stats[0]), (9, own_stats[1])]:
                own_gather(q, dst)
            for e in range(NEXP):
                ln = b.dma("sync", le[e], row_g(e))
                b.wait_val("vector", b.sems["sync"], ln)
                b.ts(tmp_t, mu_t, swcol[:, e:e + 1], ALU.mult)
                b.tt(le[e], le[e], tmp_t, ALU.subtract)
                len_ = b.tt(le[e], le[e], r_t, ALU.mult)
                b.wait_val("scalar", b.sems["vector"], len_)
                b.dma("scalar", row_g(10 + e), le[e])
            b.cpy(m1_t, le[0])
            for e in range(1, NEXP):
                b.tt(m1_t, m1_t, le[e], ALU.max)
            for e in range(NEXP):
                b.tt(tmp_t, le[e], m1_t, ALU.is_equal)
                b.stt(tmp2_t, tmp_t, -1e30, le[e], ALU.mult, ALU.add)
                if e == 0:
                    b.cpy(p2_t, tmp2_t)
                else:
                    b.tt(p2_t, p2_t, tmp2_t, ALU.max)
            sb1 = b.tt(tmp_t, p2_t, m1_t, ALU.subtract)
            b.wait_val("scalar", b.sems["vector"], sb1)
            ex = b.act(tmp_t, tmp_t, AF.Exp)
            b.wait_val("vector", b.sems["scalar"], ex)
            b.ts(tmp_t, tmp_t, 1.0, ALU.add)
            b.recip(sig_t, tmp_t)
            b.memset(omf_t, 1.0, eng="vector")
            stw_prev = [0]
            for e in range(NEXP):
                s2 = b.tt(tmp2_t, le[e], m1_t, ALU.subtract)
                b.wait_val("scalar", b.sems["vector"], s2)
                w2c = b.act(tmp2_t, tmp2_t, AF.Exp)
                b.wait_val("vector", b.sems["scalar"], w2c)
                b.wait_val("vector", b.sems["qscalar"], stw_prev[0])
                b.tt(wk_t, tmp2_t, sig_t, ALU.mult)
                b.tt(tmp3_t, le[e], p2_t, ALU.is_ge)
                b.tt(wk_t, wk_t, tmp3_t, ALU.mult)
                b.tt(elig_t, tmp3_t, omf_t, ALU.mult)
                b.scan(csum_t, elig_t, elig_t, 0.0, ALU.add, ALU.bypass)
                vcp = b.cpy(rows[:, 32:33], csum_t[:, RF - 1:RF])
                b.wait_val("tensor", b.sems["vector"], vcp)
                mmn = b.mm(pbx[:, 0:1], tri_sb[:], rows[:, 32:33],
                           start=True, stop=True, inc=True)
                b.wait_val("vector", b.sems["tensor"], mmn)
                b.cpy(rows[:, 33:34], pbx[:, 0:1])
                b.ts(pos_t, csum_t, rows[:, 33:34], ALU.add)
                b.ts(kk_t, pos_t, float(CAP), ALU.is_le)
                b.tt(kk_t, kk_t, elig_t, ALU.mult)
                b.tt(tmp3_t, elig_t, kk_t, ALU.subtract)
                b.tt(omf_t, omf_t, tmp3_t, ALU.subtract)
                wkn = b.tt(wk_t, wk_t, kk_t, ALU.mult)
                b.wait_val("scalar", b.sems["vector"], wkn)
                stw = b.dma("scalar", row_g(e), wk_t)
                stw_prev[0] = stw
            fbn = b.ts(omf_t, omf_t, -1.0, ALU.mult, 1.0, ALU.add)
            b.wait_val("scalar", b.sems["vector"], fbn)
            b.dma("scalar", row_g(8), omf_t)
            b.barrier()
            for e in range(NEXP):
                own_gather(e, own_rq[e])
            own_gather(8, own_rq[NEXP])
            b.barrier()
            b.flush()

        rb_guard = [0]
        bc_guard = [0]
        bcd_guard = [0]

        _SLOTMAP = {0: (0, 0), 1: (32, 0), 2: (64, 0), 3: (0, 1), 4: (0, 2),
                    5: (64, 1), 6: (32, 2)}

        def _slot(t_sm, i, T):
            base, cb = _SLOTMAP[i]
            return t_sm[base:base + 1, cb * T:(cb + 1) * T]

        def ln_rows(src_g, sq_g, t_sm, T, pbA, pbB):
            S = [_slot(t_sm, i, T) for i in range(7)]
            sqn = b.act(sq_g, src_g, AF.Square)
            b.wait_val("tensor", b.sems["scalar"], sqn)
            b.wait("tensor", "vector")
            for g in range(DG):
                b.mm(pbA[0:1, 0:T], ones_sb[:, 0:1], src_g[:, g, :],
                     start=(g == 0), stop=(g == DG - 1))
            for g in range(DG):
                b.mm(pbB[0:1, 0:T], ones_sb[:, 0:1], sq_g[:, g, :],
                     start=(g == 0), stop=(g == DG - 1), inc=(g == DG - 1))
            mmn = b.sems["tensor"].n
            b.wait_val("vector", b.sems["tensor"], mmn)
            b.cpy(S[0], pbA[0:1, 0:T])
            b.cpy(S[1], pbB[0:1, 0:T])
            b.ts(S[2], S[0], 1.0 / D, ALU.mult)
            b.ts(S[3], S[1], 1.0 / D, ALU.mult)
            b.tt(S[4], S[2], S[2], ALU.mult)
            b.tt(S[4], S[3], S[4], ALU.subtract)
            vn = b.ts(S[4], S[4], EPS, ALU.add)
            b.wait_val("scalar", b.sems["vector"], vn)
            b.act(S[4], S[4], AF.Ln)
            en = b.act(S[5], S[4], AF.Exp, scale=-0.5)
            b.wait_val("vector", b.sems["scalar"], en)
            b.tt(S[6], S[2], S[5], ALU.mult)
            return b.ts(S[6], S[6], -1.0, ALU.mult)

        def repl_rows(t_sm, murch, T, vec_count):
            b.wait_val("scalar", b.sems["vector"], vec_count)
            b.wait_val("scalar", b.sems["scalar"], b.sems["scalar"].n)
            b.wait_val("scalar", b.sems["qgpsimd"], rb_guard[0])
            b.dma("scalar", rowb_d[0:1, 0:T], _slot(t_sm, 5, T))
            stn = b.dma("scalar", rowb_d[1:2, 0:T], _slot(t_sm, 6, T))
            b.wait_val("gpsimd", b.sems["qscalar"], stn)
            gn = b.dma("gpsimd", murch[:, 0:2, 0:T],
                       rowb_d[:, 0:T].unsqueeze(0).broadcast_to((P, 2, T)))
            rb_guard[0] = gn
            return gn

        def ln_apply(src_g, dst_g, murch, T, gn, g_keys=None, out_bf=None):
            b.wait_val("vector", b.sems["qgpsimd"], gn)
            r_b = murch[:, 0:1, 0:T].broadcast_to((P, DG, T))
            nm_b = murch[:, 1:2, 0:T].broadcast_to((P, DG, T))
            b.tt(dst_g, src_g, r_b, ALU.mult)
            vn = b.tt(dst_g, dst_g, nm_b, ALU.add)
            if g_keys is None:
                return vn
            gk, bk = g_keys
            b.wait_val("scalar", b.sems["vector"], vn)
            for g in range(DG):
                sn = b.act(out_bf[:, g, :], dst_g[:, g, :], AF.Identity,
                           bias=pv[bk][:, g:g + 1], scale=pv[gk][:, g:g + 1])
            return sn

        if "A1" in phases:
            wIN = b.alloc("wIN", [P, DG * 16 * P], BF16)
            wXP = b.alloc("wXP", [P, DG * 96], BF16)
            wDT = b.alloc("wDT", [DTRANK, DG * P], BF16)
            wOUT = b.alloc("wOUT", [P, DG * DG * P], BF16)
            t_x = b.alloc("t_x", [P, 4096])
            t_u = b.alloc("t_u", [P, 4096])
            t_v = b.alloc("t_v", [P, 4096])
            t_w = b.alloc("t_w", [P, 4096])
            t_h = b.alloc("t_h", [P, 2048])
            t_y = b.alloc("t_y", [P, 4096])
            t_d = b.alloc("t_d", [P, 4096])
            prev_scans = [0]
            t_lb = b.alloc("t_lb", [P, 4096], BF16)
            t_cb = b.alloc("t_cb", [P, 4096], BF16)
            dtb = b.alloc("dtb", [DTRANK, 512], BF16)
            t_sm = b.alloc("t_sm", [P, 3 * 512])
            murch = b.alloc("murch", [P, 2 * 512])
            lw1 = b.dma("gpsimd", rr(wIN[:], "p (g m q) -> p g m q", g=DG, m=16),
                        rr(W_inT, "(g p) (m q) -> p g m q", p=P, q=P))
            lw2 = b.dma("gpsimd", rr(wXP[:], "p (g m) -> p g m", g=DG),
                        rr(W_xpT, "(g p) m -> p g m", p=P))
            lw3 = b.dma("gpsimd", rr(wDT[:], "k (g q) -> k g q", g=DG),
                        rr(W_dtT, "k (g q) -> k g q", q=P))
            lw4 = b.dma("gpsimd", rr(wOUT[:], "p (g m q) -> p g m q", g=DG, m=DG),
                        rr(W_outT, "(g p) (m q) -> p g m q", p=P, q=P))
            b.wait_val("tensor", b.sems["qgpsimd"], lw4)

            for k in range(NCH):
                sl = slice(k * LC, (k + 1) * LC)
                first = (k % (SEQ // LC) == 0)
                xv = rr(t_x[:], "p (g t) -> p g t", g=DG)
                uv = rr(t_u[:], "p (g t) -> p g t", g=DG)
                vv_ = rr(t_v[:], "p (g t) -> p g t", g=DG)
                wv = rr(t_w[:], "p (g t) -> p g t", g=DG)
                yv = rr(t_y[:], "p (g t) -> p g t", g=DG)
                lbv = rr(t_lb[:], "p (g t) -> p g t", g=DG)
                cbv = rr(t_cb[:], "p (g t) -> p g t", g=DG)
                mv = rr(murch[:], "p (r t) -> p r t", r=2)
                if first:
                    b.memset(hcar[:], 0.0)
                    b.memset(carry[:, 0:DG], 0.0, eng="vector")
                ldx = b.dma("sync", xv, rr(x_fm, "(g p) t -> p g t", p=P)[:, :, sl])
                ldm = b.dma("gpsimd", mv,
                            own_stats[:, sl].unsqueeze(0).broadcast_to((P, 2, LC)))
                b.wait_val("vector", b.sems["sync"], ldx)
                b.wait_val("vector", b.sems["qgpsimd"], ldm)
                mu_b = mv[:, 0:1, :].broadcast_to((P, DG, LC))
                r_b = mv[:, 1:2, :].broadcast_to((P, DG, LC))
                b.tt(uv, xv, mu_b, ALU.subtract)
                xnn = b.tt(uv, uv, r_b, ALU.mult)
                b.wait_val("scalar", b.sems["vector"], xnn)
                xnst = b.dma("scalar", rr(xn_fm_d, "(g p) t -> p g t", p=P)[:, :, sl], uv)
                b.wait_val("gpsimd", b.sems["vector"], xnn)
                xnbst = b.dma("gpsimd", rr(xnbf_fm_d, "(g p) t -> p g t", p=P)[:, :, sl], uv)
                lr = ln_rows(uv, xv, t_sm, LC, pb[2], pb[3])
                gn = repl_rows(t_sm, mv, LC, lr)
                lnb = ln_apply(uv, xv, mv, LC, gn, ("n1_g", "n1_b"), lbv)
                b.wait_val("tensor", b.sems["scalar"], lnb)
                ev_prev = [0, 0]
                for m in range(16):
                    bank = pb[m % 2]
                    if ev_prev[m % 2]:
                        b.wait_val("tensor", b.sems["scalar"], ev_prev[m % 2])
                    for g in range(DG):
                        b.mm(bank[:, :], rr(wIN[:], "p (g m q) -> p g m q",
                                            g=DG, m=16)[:, g, m, :],
                             lbv[:, g, :], start=(g == 0), stop=(g == DG - 1),
                             inc=(g == DG - 1))
                    mmn = b.sems["tensor"].n
                    b.wait_val("scalar", b.sems["tensor"], mmn)
                    if m < DG:
                        ev_prev[m % 2] = b.act(vv_[:, m, :], bank[:, :], AF.Identity)
                    else:
                        ev_prev[m % 2] = b.act(wv[:, m - DG, :], bank[:, :], AF.Silu)
                zs_done = ev_prev
                b.wait_val("scalar", b.sems["scalar"], max(zs_done))
                zst = b.dma("scalar", rr(zs_d, "(g p) t -> p g t", p=P)[:, :, sl], wv)
                b.wait_val("scalar", b.sems["qscalar"], xnst)
                b.wait_val("scalar", b.sems["qgpsimd"], xnbst)
                cwaits = max(zs_done)
                b.wait_val("vector", b.sems["scalar"], cwaits)
                for g in range(DG):
                    p1n = b.act(t_sm[:, 0:512], vv_[:, g, :], AF.Identity,
                                bias=pv["conv_b"][:, g:g + 1],
                                scale=pv["conv_w1"][:, g:g + 1])
                    b.wait_val("vector", b.sems["scalar"], p1n)
                    b.cpy(t_d[:, 1:512], vv_[:, g, 0:511])
                    b.cpy(t_d[:, 0:1], carry[:, g:g + 1])
                    b.cpy(carry[:, g:g + 1], vv_[:, g, 511:512])
                    b.stt(t_d[:, 512:1024], t_d[:, 0:512], pv["conv_w0"][:, g:g + 1],
                          t_sm[:, 0:512], ALU.mult, ALU.add)
                    vn = b.sems["vector"].n
                    b.wait_val("scalar", b.sems["vector"], vn)
                    xcn = b.act(uv[:, g, :], t_d[:, 512:1024], AF.Silu)
                    b.wait_val("vector", b.sems["scalar"], xcn)
                    b.cpy(cbv[:, g, :], uv[:, g, :])
                xc_done_v = b.sems["vector"].n
                xc_done_s = b.sems["scalar"].n
                if "xc" in dbg:
                    b.wait_val("scalar", b.sems["vector"], xc_done_v)
                    b.dma("scalar", rr(xc_d, "(g p) t -> p g t", p=P)[:, :, sl], uv)
                b.wait_val("tensor", b.sems["vector"], xc_done_v)
                for g in range(DG):
                    b.mm(pb[4][0:DTRANK, :], rr(wXP[:], "p (g m) -> p g m",
                                                g=DG)[:, g, 0:DTRANK],
                         cbv[:, g, :], start=(g == 0), stop=(g == DG - 1))
                for g in range(DG):
                    b.mm(pb[5][0:32, :], rr(wXP[:], "p (g m) -> p g m",
                                            g=DG)[:, g, DTRANK:96],
                         cbv[:, g, :], start=(g == 0), stop=(g == DG - 1),
                         inc=(g == DG - 1))
                xpn = b.sems["tensor"].n
                b.wait_val("scalar", b.sems["tensor"], xpn)
                dtev = b.act(dtb[:, :], pb[4][0:DTRANK, :], AF.Identity)
                bcev = b.act(t_sm[0:32, 0:512], pb[5][0:32, :], AF.Identity)
                b.wait_val("scalar", b.sems["qgpsimd"], bcd_guard[0])
                b.wait_val("scalar", b.sems["scalar"], bcev)
                bcst = b.dma("scalar", bc_d[:, 0:LC], t_sm[0:32, 0:512])
                b.wait_val("scalar", b.sems["qscalar"], bcst)
                b.wait_val("tensor", b.sems["scalar"], dtev)
                for g in range(DG):
                    bank = pb[6] if g % 2 == 0 else pb[4]
                    if g >= 2:
                        b.wait_val("tensor", b.sems["vector"], dproj_done[g % 2])
                    b.mm(bank[:, :], rr(wDT[:], "k (g q) -> k g q", g=DG)[:, g, :],
                         dtb[:, :], start=True, stop=True, inc=True)
                    mmn = b.sems["tensor"].n
                    b.wait_val("scalar", b.sems["tensor"], mmn)
                    if g >= 1:
                        b.wait_val("scalar", b.sems["vector"], dproj_done[(g + 1) % 2])
                    b.act(t_sm[:, 0:512], bank[:, :], AF.Identity,
                          bias=pv["dt_b"][:, g:g + 1])
                    b.act(t_d[:, 0:512], t_sm[:, 0:512], AF.Abs)
                    b.act(t_d[:, 0:512], t_d[:, 0:512], AF.Exp, scale=-1.0)
                    spn = b.act(t_d[:, 0:512], t_d[:, 0:512], AF.Ln, bias=1.0)
                    b.wait_val("vector", b.sems["scalar"], spn)
                    b.ts(t_d[:, 512:1024], t_sm[:, 0:512], 0.0, ALU.max)
                    dn = b.tt(xv[:, g, :], t_d[:, 512:1024], t_d[:, 0:512], ALU.add)
                    if g == 0:
                        dproj_done = [0, 0]
                    dproj_done[g % 2] = dn
                delta_done = b.sems["vector"].n
                if "delta" in dbg:
                    b.wait_val("scalar", b.sems["vector"], delta_done)
                    b.dma("scalar", rr(delta_d, "(g p) t -> p g t", p=P)[:, :, sl], xv)
                dxn = b.tt(t_v[:], t_x[:], t_u[:], ALU.mult)
                prev_scans[0] = dxn
                b.wait_val("gpsimd", b.sems["qscalar"], bcst)
                for q4 in range(4):
                    ns = q4 * 4
                    b.wait_val("gpsimd", b.sems["vector"], bc_guard[0])
                    rbb = b.dma("gpsimd", rr(t_w[:, 0:2048], "p (j t) -> p j t", j=4),
                                bc_d[ns:ns + 4, 0:LC].unsqueeze(0).broadcast_to((P, 4, LC)))
                    rcc = b.dma("gpsimd", rr(t_w[:, 2048:4096], "p (j t) -> p j t", j=4),
                                bc_d[16 + ns:16 + ns + 4, 0:LC].unsqueeze(0).broadcast_to((P, 4, LC)))
                    bcd_guard[0] = rcc
                    b.wait_val("vector", b.sems["qgpsimd"], rcc)
                    if "tw" in dbg and k == 0 and q4 == 0:
                        b.dma("gpsimd", tw_d, t_w[:])
                        b.wait_val("vector", b.sems["qgpsimd"], b.sems["qgpsimd"].n)
                    h8 = rr(t_h[:], "p (j t) -> p j t", j=4)
                    seb = rr(t_d[:, 0:2048], "p (j t) -> p j t", j=4)
                    for g in range(DG):
                        b.wait_val("scalar", b.sems["vector"], prev_scans[0])
                        expn = 0
                        for j in range(4):
                            n = ns + j
                            expn = b.act(seb[:, j, :], xv[:, g, :], AF.Exp,
                                         scale=A0_sb[:, g * NSTATE + n:g * NSTATE + n + 1])
                        for j in range(4):
                            n = ns + j
                            sa = t_d[:, 2048 + (j % 2) * 512:2560 + (j % 2) * 512]
                            b.tt(sa, t_v[:].rearrange("p (g t) -> p g t", g=DG)[:, g, :],
                                 rr(t_w[:, 0:2048], "p (j t) -> p j t", j=4)[:, j, :],
                                 ALU.mult)
                            if j == 0:
                                b.wait_val("vector", b.sems["scalar"], expn)
                            b.scan(h8[:, j, :], seb[:, j, :], sa,
                                   hcar[:, g * NSTATE + n:g * NSTATE + n + 1])
                        if "th" in dbg and k == 0 and q4 == 0 and g == 0:
                            hn = b.sems["vector"].n
                            b.wait_val("scalar", b.sems["vector"], hn)
                            b.dma("scalar", td_d, t_d[:])
                            b.dma("scalar", th_d, t_h[:])
                            b.wait_val("vector", b.sems["qscalar"], b.sems["qscalar"].n)
                        b.cpy(hcar[:, g * NSTATE + ns:g * NSTATE + ns + 4].unsqueeze(2),
                              h8[:, :, 511:512])
                        b.tt(t_h[:], t_h[:], t_w[:, 2048:4096], ALU.mult)
                        if q4 == 0:
                            b.red(yv[:, g, :], rr(t_h[:], "p (n t) -> p t n", n=4))
                        else:
                            b.red(t_d[:, 3072:3584], rr(t_h[:], "p (n t) -> p t n", n=4))
                            b.tt(yv[:, g, :], yv[:, g, :], t_d[:, 3072:3584], ALU.add)
                        prev_scans[0] = b.sems["vector"].n
                    bc_guard[0] = b.sems["vector"].n
                if "ysc" in dbg:
                    b.wait_val("scalar", b.sems["vector"], bc_guard[0])
                    b.dma("scalar", rr(ysc_d, "(g p) t -> p g t", p=P)[:, :, sl], yv)
                b.wait_val("sync", b.sems["vector"], bc_guard[0])
                b.wait_val("sync", b.sems["qscalar"], zst)
                zsl = b.dma("sync", vv_, rr(zs_d, "(g p) t -> p g t", p=P)[:, :, sl])
                b.wait_val("vector", b.sems["sync"], zsl)
                for g in range(DG):
                    b.stt(uv[:, g, :], uv[:, g, :], pv["Dp0"][:, g:g + 1],
                          yv[:, g, :], ALU.mult, ALU.add)
                    b.tt(cbv[:, g, :], uv[:, g, :], vv_[:, g, :], ALU.mult)
                yg_done = b.sems["vector"].n
                if "ymix" in dbg:
                    b.wait_val("scalar", b.sems["vector"], yg_done)
                    b.dma("scalar", rr(ymix_d, "(g p) t -> p g t", p=P)[:, :, sl], uv)
                b.wait_val("tensor", b.sems["vector"], yg_done)
                ev_prev = [0, 0]
                for m in range(DG):
                    bank = pb[m % 2]
                    if ev_prev[m % 2]:
                        b.wait_val("tensor", b.sems["scalar"], ev_prev[m % 2])
                    for gk in range(DG):
                        b.mm(bank[:, :], rr(wOUT[:], "p (g m q) -> p g m q",
                                            g=DG, m=DG)[:, gk, m, :],
                             cbv[:, gk, :], start=(gk == 0), stop=(gk == DG - 1),
                             inc=(gk == DG - 1))
                    mmn = b.sems["tensor"].n
                    b.wait_val("scalar", b.sems["tensor"], mmn)
                    ev_prev[m % 2] = b.act(xv[:, m, :], bank[:, :], AF.Identity)
                hm_done = max(ev_prev)
                if "hm" in dbg:
                    b.wait_val("scalar", b.sems["scalar"], hm_done)
                    b.dma("scalar", rr(hm_d, "(g p) t -> p g t", p=P)[:, :, sl], xv)
                b.wait_val("vector", b.sems["scalar"], hm_done)
                lr = ln_rows(xv, vv_, t_sm, LC, pb[2], pb[3])
                gn = repl_rows(t_sm, mv, LC, lr)
                lhn = ln_apply(xv, vv_, mv, LC, gn, None, None)
                b.wait_val("scalar", b.sems["vector"], lhn)
                for g in range(DG):
                    b.act(vv_[:, g, :], vv_[:, g, :], AF.Identity,
                          bias=pv["mnorm_b"][:, g:g + 1],
                          scale=pv["mnorm_g"][:, g:g + 1])
                mnn = b.sems["scalar"].n
                b.wait_val("sync", b.sems["qscalar"], xnst)
                b.wait_val("sync", b.sems["vector"], yg_done)
                b.wait_val("sync", b.sems["qscalar"], b.sems["qscalar"].n)
                xnl = b.dma("sync", uv, rr(xn_fm_d, "(g p) t -> p g t", p=P)[:, :, sl])
                b.wait_val("vector", b.sems["sync"], xnl)
                b.wait_val("vector", b.sems["scalar"], mnn)
                s2n = b.tt(t_u[:], t_u[:], t_v[:], ALU.add)
                b.wait_val("scalar", b.sems["vector"], s2n)
                b.dma("scalar", rr(src2_d, "(g p) t -> p g t", p=P)[:, :, sl], uv)
                b.barrier()
            b.flush()


        if "A2a" in phases:
            LC2 = 256
            wf1 = b.alloc("wf1", [P, DG * 32 * P], BF16)
            wf2 = b.alloc("wf2", [P, 32 * DG * P], BF16)
            t_a = b.alloc("a2_a", [P, DG * LC2])
            t_q = b.alloc("a2_q", [P, DG * LC2])
            t_nb = b.alloc("a2_nb", [P, DG * LC2], BF16)
            t_fb = b.alloc("a2_fb", [P, 32 * LC2], BF16)
            t_e = b.alloc("a2_e", [P, LC2])
            t_sm = b.alloc("a2_sm", [P, 3 * LC2])
            murch = b.alloc("a2_mu", [P, 2 * LC2])
            lw1 = b.dma("gpsimd", rr(wf1[:], "p (g m q) -> p g m q", g=DG, m=32),
                        rr(W_f1T, "(g p) (m q) -> p g m q", p=P, q=P))
            lw2 = b.dma("gpsimd", rr(wf2[:], "p (kk m q) -> p kk m q", kk=32, m=DG),
                        rr(W_f2T, "(kk p) (m q) -> p kk m q", p=P, q=P))
            b.wait_val("tensor", b.sems["qgpsimd"], lw2)
            wf1v = rr(wf1[:], "p (g m q) -> p g m q", g=DG, m=32)
            wf2v = rr(wf2[:], "p (kk m q) -> p kk m q", kk=32, m=DG)
            for k in range(NTOK // LC2):
                sl = slice(k * LC2, (k + 1) * LC2)
                av = rr(t_a[:], "p (g t) -> p g t", g=DG)
                qv = rr(t_q[:], "p (g t) -> p g t", g=DG)
                nv = rr(t_nb[:], "p (g t) -> p g t", g=DG)
                fv = rr(t_fb[:], "p (m t) -> p m t", m=32)
                mv = rr(murch[:], "p (r t) -> p r t", r=2)
                ldx = b.dma("sync", av, rr(src2_d, "(g p) t -> p g t", p=P)[:, :, sl])
                b.wait_val("scalar", b.sems["sync"], ldx)
                b.wait_val("vector", b.sems["sync"], ldx)
                lr = ln_rows(av, qv, t_sm, LC2, pb[2], pb[3])
                gn = repl_rows(t_sm, mv, LC2, lr)
                lnb = ln_apply(av, qv, mv, LC2, gn, ("n2_g", "n2_b"), nv)
                b.wait_val("tensor", b.sems["scalar"], lnb)
                ev_prev = [0, 0]
                for m in range(32):
                    bank = pb[m % 2]
                    if ev_prev[m % 2]:
                        b.wait_val("tensor", b.sems["vector"], ev_prev[m % 2])
                    for g in range(DG):
                        b.mm(bank[:, 0:LC2], wf1v[:, g, m, :], nv[:, g, :],
                             start=(g == 0), stop=(g == DG - 1), inc=(g == DG - 1))
                    mmn = b.sems["tensor"].n
                    b.wait_val("scalar", b.sems["tensor"], mmn)
                    if max(ev_prev):
                        b.wait_val("scalar", b.sems["vector"], max(ev_prev))
                    een = b.act(t_e[:, :], bank[:, 0:LC2], AF.Identity)
                    b.wait_val("vector", b.sems["scalar"], een)
                    ev_prev[m % 2] = b.stt(fv[:, m, :], t_e[:, :], 0.01, t_e[:, :],
                                           ALU.mult, ALU.max)
                f_done = b.sems["vector"].n
                b.wait_val("tensor", b.sems["vector"], f_done)
                ev_prev = [0, 0]
                for m in range(DG):
                    bank = pb[m % 2]
                    if ev_prev[m % 2]:
                        b.wait_val("tensor", b.sems["vector"], ev_prev[m % 2])
                    for kk in range(32):
                        b.mm(bank[:, 0:LC2], wf2v[:, kk, m, :], fv[:, kk, :],
                             start=(kk == 0), stop=(kk == 31), inc=(kk == 31))
                    mmn = b.sems["tensor"].n
                    b.wait_val("vector", b.sems["tensor"], mmn)
                    ev_prev[m % 2] = b.tt(av[:, m, :], bank[:, 0:LC2], av[:, m, :],
                                          ALU.add)
                s3n = b.sems["vector"].n
                b.wait_val("scalar", b.sems["vector"], s3n)
                b.dma("scalar", rr(src3_d, "(g p) t -> p g t", p=P)[:, :, sl], av)
                b.barrier()
            b.flush()

        if "A2b" in phases:
            wfe = b.alloc("wfe", [P, DG * DG * P], BF16)
            t_a = b.alloc("b2_a", [P, DG * LC])
            t_q = b.alloc("b2_q", [P, DG * LC])
            t_nb = b.alloc("b2_nb", [P, DG * LC], BF16)
            t_e = b.alloc("b2_e", [P, LC])
            t_sm = b.alloc("b2_sm", [P, 3 * LC])
            murch = b.alloc("b2_mu", [P, 2 * LC])
            lw = b.dma("gpsimd", rr(wfe[:], "p (g m q) -> p g m q", g=DG, m=DG),
                       rr(W_feT, "(g p) (m q) -> p g m q", p=P, q=P))
            b.wait_val("tensor", b.sems["qgpsimd"], lw)
            wfev = rr(wfe[:], "p (g m q) -> p g m q", g=DG, m=DG)
            for k in range(NCH):
                sl = slice(k * LC, (k + 1) * LC)
                av = rr(t_a[:], "p (g t) -> p g t", g=DG)
                qv = rr(t_q[:], "p (g t) -> p g t", g=DG)
                nv = rr(t_nb[:], "p (g t) -> p g t", g=DG)
                mv = rr(murch[:], "p (r t) -> p r t", r=2)
                ldx = b.dma("sync", av, rr(src3_d, "(g p) t -> p g t", p=P)[:, :, sl])
                b.wait_val("scalar", b.sems["sync"], ldx)
                b.wait_val("vector", b.sems["sync"], ldx)
                lr = ln_rows(av, qv, t_sm, LC, pb[2], pb[3])
                gn = repl_rows(t_sm, mv, LC, lr)
                lnb = ln_apply(av, qv, mv, LC, gn, ("n3_g", "n3_b"), nv)
                b.wait_val("tensor", b.sems["scalar"], lnb)
                ev_prev = [0, 0]
                for m in range(DG):
                    bank = pb[m % 2]
                    if ev_prev[m % 2]:
                        b.wait_val("tensor", b.sems["vector"], ev_prev[m % 2])
                    for g in range(DG):
                        b.mm(bank[:, :], wfev[:, g, m, :], nv[:, g, :],
                             start=(g == 0), stop=(g == DG - 1), inc=(g == DG - 1))
                    mmn = b.sems["tensor"].n
                    b.wait_val("scalar", b.sems["tensor"], mmn)
                    if max(ev_prev):
                        b.wait_val("scalar", b.sems["vector"], max(ev_prev))
                    een = b.act(t_e[:, :], bank[:, :], AF.Identity,
                                bias=pv["fe_b"][:, m:m + 1])
                    b.wait_val("vector", b.sems["scalar"], een)
                    ev_prev[m % 2] = b.tt(av[:, m, :], t_e[:, :], av[:, m, :],
                                          ALU.add)
                sn = b.sems["vector"].n
                b.wait_val("scalar", b.sems["vector"], sn)
                b.dma("scalar", rr(s_attn_d, "(g p) t -> p g t", p=P)[:, :, sl], av)
                b.barrier()
            b.flush()

        if "A3" in phases:
            LC2 = 256
            wf1 = b.alloc("f3_w1", [P, DG * 32 * P], BF16)
            wf2 = b.alloc("f3_w2", [P, 32 * DG * P], BF16)
            t_a = b.alloc("f3_a", [P, DG * LC2])
            t_nb = b.alloc("f3_nb", [P, DG * LC2], BF16)
            t_fb = b.alloc("f3_fb", [P, 32 * LC2], BF16)
            lw1 = b.dma("gpsimd", rr(wf1[:], "p (g m q) -> p g m q", g=DG, m=32),
                        rr(fw1, "(g p) (m q) -> p g m q", p=P, q=P))
            lw2 = b.dma("gpsimd", rr(wf2[:], "p (kk m q) -> p kk m q", kk=32, m=DG),
                        rr(fw2, "(kk p) (m q) -> p kk m q", p=P, q=P))
            b.wait_val("tensor", b.sems["qgpsimd"], lw2)
            wf1v = rr(wf1[:], "p (g m q) -> p g m q", g=DG, m=32)
            wf2v = rr(wf2[:], "p (kk m q) -> p kk m q", kk=32, m=DG)
            for k in range(NTOK // LC2):
                sl = slice(k * LC2, (k + 1) * LC2)
                av = rr(t_a[:], "p (g t) -> p g t", g=DG)
                nv = rr(t_nb[:], "p (g t) -> p g t", g=DG)
                fv = rr(t_fb[:], "p (m t) -> p m t", m=32)
                ldx = b.dma("sync", nv,
                            rr(xnbf_fm_d, "(g p) t -> p g t", p=P)[:, :, sl])
                b.wait_val("tensor", b.sems["sync"], ldx)
                ev_prev = [0, 0]
                for m in range(32):
                    bank = pb[m % 2]
                    if ev_prev[m % 2]:
                        b.wait_val("tensor", b.sems["scalar"], ev_prev[m % 2])
                    for g in range(DG):
                        b.mm(bank[:, 0:LC2], wf1v[:, g, m, :], nv[:, g, :],
                             start=(g == 0), stop=(g == DG - 1), inc=(g == DG - 1))
                    mmn = b.sems["tensor"].n
                    b.wait_val("scalar", b.sems["tensor"], mmn)
                    ev_prev[m % 2] = b.act(fv[:, m, :], bank[:, 0:LC2], AF.Gelu,
                                           bias=fb1_sb[:, m:m + 1])
                f_done = b.sems["scalar"].n
                b.wait_val("tensor", b.sems["scalar"], f_done)
                ev_prev = [0, 0]
                for m in range(DG):
                    bank = pb[m % 2]
                    if ev_prev[m % 2]:
                        b.wait_val("tensor", b.sems["scalar"], ev_prev[m % 2])
                    for kk in range(32):
                        b.mm(bank[:, 0:LC2], wf2v[:, kk, m, :], fv[:, kk, :],
                             start=(kk == 0), stop=(kk == 31), inc=(kk == 31))
                    mmn = b.sems["tensor"].n
                    b.wait_val("scalar", b.sems["tensor"], mmn)
                    ev_prev[m % 2] = b.act(av[:, m, :], bank[:, 0:LC2], AF.Identity,
                                           bias=pv["fb2"][:, m:m + 1])
                yn = b.sems["scalar"].n
                b.wait_val("scalar", b.sems["scalar"], yn)
                b.dma("scalar", rr(yfb_d, "(g p) t -> p g t", p=P)[:, :, sl], av)
                b.barrier()
            b.flush()

        if "M" in phases:
            LC2 = 512
            wf1 = b.alloc("m_w1", [P, DG * 32 * P], BF16)
            wf2 = b.alloc("m_w2", [P, 32 * DG * P], BF16)
            t_a = b.alloc("m_a", [P, DG * LC2])
            t_nb = b.alloc("m_nb", [P, DG * LC2], BF16)
            t_fb = b.alloc("m_fb", [P, 32 * LC2], BF16)
            wkr = b.alloc("m_wkr", [P, LC2])
            wf1v = rr(wf1[:], "p (g m q) -> p g m q", g=DG, m=32)
            wf2v = rr(wf2[:], "p (kk m q) -> p kk m q", kk=32, m=DG)
            for e in range(NEXP):
                b.wait("gpsimd", "tensor")
                lw1 = b.dma("gpsimd", wf1v, rr(W1[e], "(g p) (m q) -> p g m q",
                                               p=P, q=P))
                lw2 = b.dma("gpsimd", wf2v, rr(W2[e], "(kk p) (m q) -> p kk m q",
                                               p=P, q=P))
                b.wait_val("tensor", b.sems["qgpsimd"], lw2)
                ffn1_done = [0]
                gm_prev = [0]
                st_prev = [0]
                for k in range(NTOK // LC2):
                    sl = slice(k * LC2, (k + 1) * LC2)
                    av = rr(t_a[:], "p (g t) -> p g t", g=DG)
                    nv = rr(t_nb[:], "p (g t) -> p g t", g=DG)
                    fv = rr(t_fb[:], "p (m t) -> p m t", m=32)
                    b.wait_val("sync", b.sems["tensor"], ffn1_done[0])
                    ldx = b.dma("sync", nv,
                                rr(xnbf_fm_d, "(g p) t -> p g t", p=P)[:, :, sl])
                    b.wait_val("gpsimd", b.sems["vector"], gm_prev[0])
                    lwk = b.dma("gpsimd", wkr[:],
                                own_rq[e, sl].unsqueeze(0).broadcast_to((P, LC2)))
                    b.wait_val("tensor", b.sems["sync"], ldx)
                    ev_prev = [0, 0]
                    for m in range(32):
                        bank = pb[m % 2]
                        if ev_prev[m % 2]:
                            b.wait_val("tensor", b.sems["scalar"], ev_prev[m % 2])
                        for g in range(DG):
                            b.mm(bank[:, :], wf1v[:, g, m, :], nv[:, g, :],
                                 start=(g == 0), stop=(g == DG - 1),
                                 inc=(g == DG - 1))
                        mmn = b.sems["tensor"].n
                        b.wait_val("scalar", b.sems["tensor"], mmn)
                        ev_prev[m % 2] = b.act(fv[:, m, :], bank[:, :], AF.Gelu,
                                               bias=b1e_sb[:, e * 32 + m:e * 32 + m + 1])
                    f_done = b.sems["scalar"].n
                    ffn1_done[0] = b.sems["tensor"].n
                    b.wait_val("tensor", b.sems["scalar"], f_done)
                    ev_prev = [0, 0]
                    for m in range(DG):
                        bank = pb[m % 2]
                        if ev_prev[m % 2]:
                            b.wait_val("tensor", b.sems["scalar"], ev_prev[m % 2])
                        for kk in range(32):
                            b.mm(bank[:, :], wf2v[:, kk, m, :], fv[:, kk, :],
                                 start=(kk == 0), stop=(kk == 31), inc=(kk == 31))
                        mmn = b.sems["tensor"].n
                        b.wait_val("scalar", b.sems["tensor"], mmn)
                        if m == 0:
                            b.wait_val("scalar", b.sems["qgpsimd"], st_prev[0])
                        ev_prev[m % 2] = b.act(av[:, m, :], bank[:, :],
                                               AF.Identity,
                                               bias=b2e_sb[:, e * 8 + m:e * 8 + m + 1])
                    yn = b.sems["scalar"].n
                    b.wait_val("vector", b.sems["scalar"], yn)
                    b.wait_val("vector", b.sems["qgpsimd"], lwk)
                    wkb = wkr[:].unsqueeze(1).broadcast_to((P, DG, LC2))
                    gm = b.tt(av, av, wkb, ALU.mult)
                    gm_prev[0] = gm
                    b.wait_val("gpsimd", b.sems["vector"], gm)
                    if e == 0:
                        st_prev[0] = b.dma(
                            "gpsimd", rr(gmoe_d, "(g p) t -> p g t", p=P)[:, :, sl],
                            av)
                    else:
                        st_prev[0] = b.dma(
                            "gpsimd", rr(gmoe_d, "(g p) t -> p g t", p=P)[:, :, sl],
                            av, accum_op=ALU.add)
                b.barrier()
            b.flush()

        if "CMB" in phases:
            t_a = b.alloc("c_a", [P, DG * LC])
            t_b = b.alloc("c_b", [P, DG * LC])
            t_c = b.alloc("c_c", [P, DG * LC])
            t_dd = b.alloc("c_d", [P, DG * LC])
            fbr = b.alloc("c_fbr", [P, LC])
            ofr = b.alloc("c_ofr", [P, LC])
            for k in range(NCH):
                sl = slice(k * LC, (k + 1) * LC)
                av = rr(t_a[:], "p (g t) -> p g t", g=DG)
                bv2 = rr(t_b[:], "p (g t) -> p g t", g=DG)
                cv = rr(t_c[:], "p (g t) -> p g t", g=DG)
                dv = rr(t_dd[:], "p (g t) -> p g t", g=DG)
                l1_ = b.dma("sync", av, rr(gmoe_d, "(g p) t -> p g t", p=P)[:, :, sl])
                l2_ = b.dma("sync", bv2, rr(yfb_d, "(g p) t -> p g t", p=P)[:, :, sl])
                l3_ = b.dma("sync", cv, rr(x_fm, "(g p) t -> p g t", p=P)[:, :, sl])
                l4_ = b.dma("sync", dv,
                            rr(s_attn_d, "(g p) t -> p g t", p=P)[:, :, sl])
                lf = b.dma("gpsimd", fbr[:],
                           own_rq[NEXP, sl].unsqueeze(0).broadcast_to((P, LC)))
                b.wait_val("vector", b.sems["sync"], l4_)
                b.wait_val("vector", b.sems["qgpsimd"], lf)
                b.ts(ofr[:], fbr[:], -1.0, ALU.mult, 1.0, ALU.add)
                ofb = ofr[:].unsqueeze(1).broadcast_to((P, DG, LC))
                fbb = fbr[:].unsqueeze(1).broadcast_to((P, DG, LC))
                b.tt(av, av, ofb, ALU.mult)
                b.tt(bv2, bv2, fbb, ALU.mult)
                ggn = b.tt(av, av, bv2, ALU.add)
                b.wait_val("scalar", b.sems["vector"], ggn)
                sgn = b.act(av, av, AF.Silu)
                b.wait_val("vector", b.sems["scalar"], sgn)
                b.tt(dv, dv, av, ALU.mult)
                x1n = b.tt(cv, cv, dv, ALU.add)
                b.wait_val("scalar", b.sems["vector"], x1n)
                b.dma("scalar", rr(x1_d, "(g p) t -> p g t", p=P)[:, :, sl], cv)
                b.barrier()
            b.flush()

        if "C" in phases:
            wx1 = b.alloc("wx1", [P, DG * 96], BF16)
            wdt1 = b.alloc("wdt1", [DTRANK, DG * P], BF16)
            wR = b.alloc("wR", [P, DG * DG * P], BF16)
            wK = b.alloc("wK", [P, DG * DG * P], BF16)
            wV = b.alloc("wV", [P, DG * DG * P], BF16)
            t_x = b.alloc("c1_x", [P, 4096])
            t_u = b.alloc("c1_u", [P, 4096])
            t_v = b.alloc("c1_v", [P, 4096])
            t_w = b.alloc("c1_w", [P, 4096])
            t_h = b.alloc("c1_h", [P, 2048])
            t_y = b.alloc("c1_y", [P, 4096])
            t_d = b.alloc("c1_d", [P, 4096])
            prev_scans = [0]
            t_lb = b.alloc("c1_lb", [P, 4096], BF16)
            t_cb = b.alloc("c1_cb", [P, 4096], BF16)
            t_cb2 = b.alloc("c1_cb2", [P, 4096], BF16)
            dtb = b.alloc("c1_dtb", [DTRANK, 512], BF16)
            t_sm = b.alloc("c1_sm", [P, 3 * 512])
            murch = b.alloc("c1_mu", [P, 2 * 512])
            lw1 = b.dma("gpsimd", rr(wx1[:], "p (g m) -> p g m", g=DG),
                        rr(W_x1T, "(g p) m -> p g m", p=P))
            lw2 = b.dma("gpsimd", rr(wdt1[:], "k (g q) -> k g q", g=DG),
                        rr(W_dt1T, "k (g q) -> k g q", q=P))
            lw3 = b.dma("gpsimd", rr(wR[:], "p (g m q) -> p g m q", g=DG, m=DG),
                        rr(WR1T, "(g p) (m q) -> p g m q", p=P, q=P))
            lw4 = b.dma("gpsimd", rr(wK[:], "p (g m q) -> p g m q", g=DG, m=DG),
                        rr(WK1T, "(g p) (m q) -> p g m q", p=P, q=P))
            lw5 = b.dma("gpsimd", rr(wV[:], "p (g m q) -> p g m q", g=DG, m=DG),
                        rr(WV1T, "(g p) (m q) -> p g m q", p=P, q=P))
            b.wait_val("tensor", b.sems["qgpsimd"], lw5)
            wx1v = rr(wx1[:], "p (g m) -> p g m", g=DG)
            wdt1v = rr(wdt1[:], "k (g q) -> k g q", g=DG)
            wRv = rr(wR[:], "p (g m q) -> p g m q", g=DG, m=DG)
            wKv = rr(wK[:], "p (g m q) -> p g m q", g=DG, m=DG)
            wVv = rr(wV[:], "p (g m q) -> p g m q", g=DG, m=DG)
            for k in range(NCH):
                sl = slice(k * LC, (k + 1) * LC)
                first = (k % (SEQ // LC) == 0)
                xv = rr(t_x[:], "p (g t) -> p g t", g=DG)
                uv = rr(t_u[:], "p (g t) -> p g t", g=DG)
                vv_ = rr(t_v[:], "p (g t) -> p g t", g=DG)
                wv = rr(t_w[:], "p (g t) -> p g t", g=DG)
                yv = rr(t_y[:], "p (g t) -> p g t", g=DG)
                lbv = rr(t_lb[:], "p (g t) -> p g t", g=DG)
                cbv = rr(t_cb[:], "p (g t) -> p g t", g=DG)
                cb2v = rr(t_cb2[:], "p (g t) -> p g t", g=DG)
                mv = rr(murch[:], "p (r t) -> p r t", r=2)
                if first:
                    b.memset(hcar[:], 0.0)
                    b.memset(carry[:, DG:2 * DG], 0.0, eng="vector")
                ldx = b.dma("sync", xv, rr(x1_d, "(g p) t -> p g t", p=P)[:, :, sl])
                b.wait_val("scalar", b.sems["sync"], ldx)
                b.wait_val("vector", b.sems["sync"], ldx)
                lr = ln_rows(xv, wv, t_sm, LC, pb[2], pb[3])
                gn = repl_rows(t_sm, mv, LC, lr)
                b.wait_val("vector", b.sems["qgpsimd"], gn)
                rr_b = mv[:, 0:1, :].broadcast_to((P, DG, LC))
                nmr_b = mv[:, 1:2, :].broadcast_to((P, DG, LC))
                b.tt(uv, xv, rr_b, ALU.mult)
                xnn = b.tt(uv, uv, nmr_b, ALU.add)
                xbn = b.cpy(t_lb[:], t_u[:])
                for g in range(DG):
                    b.cpy(t_d[:, 1:512], uv[:, g, 0:511])
                    b.cpy(t_d[:, 0:1], carry[:, DG + g:DG + g + 1])
                    b.cpy(carry[:, DG + g:DG + g + 1], uv[:, g, 511:512])
                    b.ts(t_d[:, 512:1024], t_d[:, 0:512],
                         pv["om_mu_r"][:, g:g + 1], ALU.mult)
                    b.stt(cbv[:, g, :], uv[:, g, :], pv["mu_r"][:, g:g + 1],
                          t_d[:, 512:1024], ALU.mult, ALU.add)
                    b.ts(t_d[:, 512:1024], t_d[:, 0:512],
                         pv["om_mu_k"][:, g:g + 1], ALU.mult)
                    b.stt(cb2v[:, g, :], uv[:, g, :], pv["mu_k"][:, g:g + 1],
                          t_d[:, 512:1024], ALU.mult, ALU.add)
                shf_done = b.sems["vector"].n
                b.wait_val("tensor", b.sems["vector"], xbn)
                for g in range(DG):
                    b.mm(pb[4][0:DTRANK, :], wx1v[:, g, 0:DTRANK], lbv[:, g, :],
                         start=(g == 0), stop=(g == DG - 1))
                for g in range(DG):
                    b.mm(pb[5][0:32, :], wx1v[:, g, DTRANK:96], lbv[:, g, :],
                         start=(g == 0), stop=(g == DG - 1), inc=(g == DG - 1))
                xpn = b.sems["tensor"].n
                b.wait_val("scalar", b.sems["tensor"], xpn)
                dtev = b.act(dtb[:, :], pb[4][0:DTRANK, :], AF.Identity)
                bcev = b.act(t_sm[0:32, 0:512], pb[5][0:32, :], AF.Identity)
                b.wait_val("scalar", b.sems["qgpsimd"], bcd_guard[0])
                b.wait_val("scalar", b.sems["scalar"], bcev)
                bcst = b.dma("scalar", bc_d[:, 0:LC], t_sm[0:32, 0:512])
                b.wait_val("scalar", b.sems["qscalar"], bcst)
                b.wait_val("scalar", b.sems["vector"], shf_done)
                b.wait_val("tensor", b.sems["scalar"], dtev)
                for g in range(DG):
                    bank = pb[6] if g % 2 == 0 else pb[4]
                    if g >= 2:
                        b.wait_val("tensor", b.sems["vector"], dproj_done[g % 2])
                    b.mm(bank[:, :], wdt1v[:, g, :], dtb[:, :],
                         start=True, stop=True, inc=True)
                    mmn = b.sems["tensor"].n
                    b.wait_val("scalar", b.sems["tensor"], mmn)
                    if g >= 1:
                        b.wait_val("scalar", b.sems["vector"], dproj_done[(g + 1) % 2])
                    b.act(t_sm[:, 0:512], bank[:, :], AF.Identity,
                          bias=pv["b_dt1"][:, g:g + 1])
                    if k == 0 and g == 0 and "th" in dbg:
                        pdn = b.sems["scalar"].n
                        b.wait_val("scalar", b.sems["scalar"], pdn)
                        b.dma("scalar", th_d[:, 0:512], t_sm[:, 0:512])
                    b.act(t_d[:, 0:512], t_sm[:, 0:512], AF.Abs)
                    b.act(t_d[:, 0:512], t_d[:, 0:512], AF.Exp, scale=-1.0)
                    spn = b.act(t_d[:, 0:512], t_d[:, 0:512], AF.Ln, bias=1.0)
                    b.wait_val("vector", b.sems["scalar"], spn)
                    b.ts(t_d[:, 512:1024], t_sm[:, 0:512], 0.0, ALU.max)
                    dn = b.tt(xv[:, g, :], t_d[:, 512:1024], t_d[:, 0:512], ALU.add)
                    if k == 0 and g == 0 and "th" in dbg:
                        b.wait_val("scalar", b.sems["vector"], dn)
                        b.dma("scalar", th_d[:, 512:1024], t_d[:, 0:512])
                        b.dma("scalar", th_d[:, 1024:1536], xv[:, g, :])
                        b.wait_val("vector", b.sems["qscalar"], b.sems["qscalar"].n)
                    if g == 0:
                        dproj_done = [0, 0]
                    dproj_done[g % 2] = dn
                dl1n = b.tt(t_v[:], t_x[:], t_u[:], ALU.mult)
                prev_scans[0] = dl1n
                if k == 0:
                    b.wait_val("scalar", b.sems["vector"], dl1n)
                    b.dma("scalar", a1_d, A1_sb[:])
                    b.dma("scalar", xn1_d, t_u[:])
                    dstn = b.dma("scalar", dl1_d, t_x[:])
                    b.wait_val("vector", b.sems["qscalar"], dstn)
                for q4 in range(4):
                    ns = q4 * 4
                    b.wait_val("gpsimd", b.sems["vector"], bc_guard[0])
                    rbb = b.dma("gpsimd", rr(t_w[:, 0:2048], "p (j t) -> p j t", j=4),
                                bc_d[ns:ns + 4, 0:LC].unsqueeze(0).broadcast_to((P, 4, LC)))
                    rcc = b.dma("gpsimd", rr(t_w[:, 2048:4096], "p (j t) -> p j t", j=4),
                                bc_d[16 + ns:16 + ns + 4, 0:LC].unsqueeze(0).broadcast_to((P, 4, LC)))
                    bcd_guard[0] = rcc
                    b.wait_val("vector", b.sems["qgpsimd"], rcc)
                    h8 = rr(t_h[:], "p (j t) -> p j t", j=4)
                    seb = rr(t_d[:, 0:2048], "p (j t) -> p j t", j=4)
                    for g in range(DG):
                        b.wait_val("scalar", b.sems["vector"], prev_scans[0])
                        expn = 0
                        for j in range(4):
                            n = ns + j
                            expn = b.act(seb[:, j, :], xv[:, g, :], AF.Exp,
                                         scale=A1_sb[:, g * NSTATE + n:g * NSTATE + n + 1])
                        for j in range(4):
                            n = ns + j
                            sa = t_d[:, 2048 + (j % 2) * 512:2560 + (j % 2) * 512]
                            b.tt(sa, vv_[:, g, :],
                                 rr(t_w[:, 0:2048], "p (j t) -> p j t", j=4)[:, j, :],
                                 ALU.mult)
                            if j == 0:
                                b.wait_val("vector", b.sems["scalar"], expn)
                            b.scan(h8[:, j, :], seb[:, j, :], sa,
                                   hcar[:, g * NSTATE + n:g * NSTATE + n + 1])
                        b.cpy(hcar[:, g * NSTATE + ns:g * NSTATE + ns + 4].unsqueeze(2),
                              h8[:, :, 511:512])
                        b.tt(t_h[:], t_h[:], t_w[:, 2048:4096], ALU.mult)
                        if q4 == 0:
                            b.red(yv[:, g, :], rr(t_h[:], "p (n t) -> p t n", n=4))
                        else:
                            b.red(t_d[:, 3072:3584], rr(t_h[:], "p (n t) -> p t n", n=4))
                            b.tt(yv[:, g, :], yv[:, g, :], t_d[:, 3072:3584], ALU.add)
                        prev_scans[0] = b.sems["vector"].n
                    bc_guard[0] = b.sems["vector"].n
                for g in range(DG):
                    b.stt(yv[:, g, :], uv[:, g, :], pv["D_par"][:, g:g + 1],
                          yv[:, g, :], ALU.mult, ALU.add)
                s1_done = b.sems["vector"].n
                b.wait_val("tensor", b.sems["vector"], s1_done)
                ev_prev = [0, 0]
                for m in range(DG):
                    bank = pb[m % 2]
                    if ev_prev[m % 2]:
                        b.wait_val("tensor", b.sems["scalar"], ev_prev[m % 2])
                    for gk in range(DG):
                        b.mm(bank[:, :], wRv[:, gk, m, :], cbv[:, gk, :],
                             start=(gk == 0), stop=(gk == DG - 1), inc=(gk == DG - 1))
                    mmn = b.sems["tensor"].n
                    b.wait_val("scalar", b.sems["tensor"], mmn)
                    ev_prev[m % 2] = b.act(vv_[:, m, :], bank[:, :], AF.Sigmoid)
                rt_done = b.sems["scalar"].n
                ev_prev = [0, 0]
                for m in range(DG):
                    bank = pb[m % 2]
                    if ev_prev[m % 2]:
                        b.wait_val("tensor", b.sems["scalar"], ev_prev[m % 2])
                    for gk in range(DG):
                        b.mm(bank[:, :], wKv[:, gk, m, :], cb2v[:, gk, :],
                             start=(gk == 0), stop=(gk == DG - 1), inc=(gk == DG - 1))
                    mmn = b.sems["tensor"].n
                    b.wait_val("scalar", b.sems["tensor"], mmn)
                    b.act(t_sm[:, 0:512], bank[:, :], AF.Relu)
                    ev_prev[m % 2] = b.act(lbv[:, m, :], t_sm[:, 0:512], AF.Square)
                k2_done = b.sems["scalar"].n
                b.wait_val("tensor", b.sems["scalar"], k2_done)
                b.wait_val("sync", b.sems["vector"], bc_guard[0])
                b.wait_val("sync", b.sems["qgpsimd"], b.sems["qgpsimd"].n)
                ldx1 = b.dma("sync", wv, rr(x1_d, "(g p) t -> p g t", p=P)[:, :, sl])
                ev_prev = [0, 0]
                for m in range(DG):
                    bank = pb[m % 2]
                    if ev_prev[m % 2]:
                        b.wait_val("tensor", b.sems["vector"], ev_prev[m % 2])
                    for gk in range(DG):
                        b.mm(bank[:, :], wVv[:, gk, m, :], lbv[:, gk, :],
                             start=(gk == 0), stop=(gk == DG - 1), inc=(gk == DG - 1))
                    mmn = b.sems["tensor"].n
                    b.wait_val("vector", b.sems["tensor"], mmn)
                    ev_prev[m % 2] = b.tt(xv[:, m, :], bank[:, :], vv_[:, m, :],
                                          ALU.mult)
                g1_done = b.sems["vector"].n
                b.wait_val("scalar", b.sems["vector"], g1_done)
                sgn = b.act(t_x[:], t_x[:], AF.Silu)
                b.wait_val("vector", b.sems["scalar"], sgn)
                b.wait_val("vector", b.sems["sync"], ldx1)
                b.tt(t_x[:], t_x[:], t_y[:], ALU.mult)
                onn = b.tt(t_x[:], t_x[:], t_w[:], ALU.add)
                b.wait_val("scalar", b.sems["vector"], onn)
                b.dma("scalar", rr(out_fm, "(g p) t -> p g t", p=P)[:, :, sl], xv)
                b.barrier()
            b.flush()

        for k2, ap in dbg_outs.items():
            b.dma("scalar", ap, DBG_REG[k2])
        b.barrier()
        b.flush()
    return nc


def make_in_maps(inputs):
    i = inputs
    C = np.ascontiguousarray
    x_flat = np.asarray(i["L0_attn_src"], np.float32).reshape(TGLOB, D)
    x_fm_full = C(x_flat.T)
    def vec_pg(v):
        return C(np.asarray(v, np.float32).reshape(DG, P).T)

    shared = {
        "x_fm_full": x_fm_full,
        "W_inT": C(np.asarray(i["L0_attn_in_proj_w"], np.float32).T),
        "conv_w0": vec_pg(np.asarray(i["L0_attn_conv_w"], np.float32)[:, 0]),
        "conv_w1": vec_pg(np.asarray(i["L0_attn_conv_w"], np.float32)[:, 1]),
        "conv_b": vec_pg(i["L0_attn_conv_b"]),
        "W_xpT": C(np.asarray(i["L0_attn_x_proj_w"], np.float32).T),
        "W_dtT": C(np.asarray(i["L0_attn_dt_proj_w"], np.float32).T),
        "dt_b": vec_pg(i["L0_attn_dt_proj_b"]),
        "A_log0": C(np.asarray(i["L0_attn_A_log"], np.float32).reshape(DG, P, NSTATE).transpose(1, 0, 2)),
        "Dp0": vec_pg(i["L0_attn_Dp"]),
        "W_outT": C(np.asarray(i["L0_attn_out_proj_w"], np.float32).T),
        "mnorm_g": vec_pg(i["L0_attn_mnorm_g"]),
        "mnorm_b": vec_pg(i["L0_attn_mnorm_b"]),
        "n1_g": vec_pg(i["L0_attn_n1_g"]),
        "n1_b": vec_pg(i["L0_attn_n1_b"]),
        "n2_g": vec_pg(i["L0_attn_n2_g"]),
        "n2_b": vec_pg(i["L0_attn_n2_b"]),
        "n3_g": vec_pg(i["L0_attn_n3_g"]),
        "n3_b": vec_pg(i["L0_attn_n3_b"]),
        "W_f1T": C(np.asarray(i["L0_attn_ffn_w1"], np.float32).T),
        "W_f2T": C(np.asarray(i["L0_attn_ffn_w2"], np.float32).T),
        "W_feT": C(np.asarray(i["L0_attn_fe_w"], np.float32).T),
        "fe_b": vec_pg(i["L0_attn_fe_b"]),
        "WrT": C(np.asarray(i["L0_ffn_Wr"], np.float32).T),
        "W1": C(np.asarray(i["L0_ffn_W1"], np.float32)),
        "b1": C(np.asarray(i["L0_ffn_b1"], np.float32).reshape(NEXP, 32, P).transpose(2, 0, 1)),
        "W2": C(np.asarray(i["L0_ffn_W2"], np.float32)),
        "b2": C(np.asarray(i["L0_ffn_b2"], np.float32).reshape(NEXP, 8, P).transpose(2, 0, 1)),
        "fw1": C(np.asarray(i["L0_ffn_fw1"], np.float32)),
        "fb1": C(np.asarray(i["L0_ffn_fb1"], np.float32).reshape(32, P).T),
        "fw2": C(np.asarray(i["L0_ffn_fw2"], np.float32)),
        "fb2": vec_pg(i["L0_ffn_fb2"]),
        "A_log1": C(np.asarray(i["L1_attn_A_log"], np.float32).reshape(DG, P, NSTATE).transpose(1, 0, 2)),
        "W_x1T": C(np.asarray(i["L1_attn_W_x"], np.float32).T),
        "W_dt1T": C(np.asarray(i["L1_attn_W_dt"], np.float32).T),
        "b_dt1": vec_pg(i["L1_attn_b_dt"]),
        "D_par": vec_pg(i["L1_attn_D_param"]),
        "WR1T": C(np.asarray(i["L1_ffn_W_r"], np.float32).T),
        "WK1T": C(np.asarray(i["L1_ffn_W_k"], np.float32).T),
        "WV1T": C(np.asarray(i["L1_ffn_W_v"], np.float32).T),
        "mu_r": vec_pg(i["L1_ffn_mu_r"]),
        "mu_k": vec_pg(i["L1_ffn_mu_k"]),
    }
    in_maps = []
    for c in range(8):
        m = dict(shared)
        m["x_fm"] = C(x_fm_full[:, c * NTOK:(c + 1) * NTOK])
        m["idx_own"] = (c * (NTOK // 32) + np.arange(P, dtype=np.int32)).reshape(P, 1)
        in_maps.append(m)
    return in_maps


def kernel(**inputs):
    nc = build_program(ALL_PHASES)
    in_maps = make_in_maps(inputs)
    res = run_bass_kernel_spmd(nc, in_maps, core_ids=list(range(8)))
    out = np.concatenate(
        [np.asarray(res.results[c]["out_fm"]).T for c in range(8)], axis=0)
    return np.ascontiguousarray(out.reshape(16, SEQ, D).astype(np.float32))
